# version 08
# baseline (speedup 1.0000x reference)
import contextlib
import numpy as np
import ml_dtypes
import concourse.bass as bass
import concourse.mybir as mybir
from concourse.bass_utils import run_bass_kernel_spmd

F32 = mybir.dt.float32
BF16 = mybir.dt.bfloat16
I32 = mybir.dt.int32
AF = mybir.ActivationFunctionType
ALU = mybir.AluOpType
AX = mybir.AxisListType
NBF = ml_dtypes.bfloat16

B, T, D = 4, 4096, 1024
NC = 8
TPC = 2048
EXT = TPC + 128
NTT = TPC // 128


_DMA_OPS = {"dma_start", "dma_start_transpose"}


class _ChainedEngine:

    def __init__(self, eng, chain, kind):
        self._eng = eng
        self._ch = chain
        self._kind = kind

    def __getattr__(self, name):
        f = getattr(self._eng, name)
        if not callable(f):
            return f

        def wrapped(*a, **k):
            ch = self._ch
            kind = self._kind
            if kind in ("vector", "scalar"):
                sem, cnt = ch.sem[kind], ch.cnt[kind]
                if cnt > ch.seg_base[kind]:
                    self._eng.wait_ge(sem, cnt)
            r = f(*a, **k)
            if name in _DMA_OPS:
                dk = kind if kind == "hdma" else "dma"
                r.then_inc(ch.sem[dk], 16)
                ch.cnt[dk] += 16
            else:
                r.then_inc(ch.sem[kind], 1)
                ch.cnt[kind] += 1
            return r
        return wrapped


class Chain:

    def __init__(self, nc, es):
        self.sem = {
            "dma": es.enter_context(nc.semaphore("chd")),
            "hdma": es.enter_context(nc.semaphore("chh")),
            "tensor": es.enter_context(nc.semaphore("cht")),
            "vector": es.enter_context(nc.semaphore("chv")),
            "scalar": es.enter_context(nc.semaphore("chs")),
        }
        self.cnt = {"dma": 0, "hdma": 0, "tensor": 0, "vector": 0, "scalar": 0}
        self.seg_base = dict(self.cnt)

    def seg(self, block, engname, fn):
        ch = self
        kind = ("dma" if engname == "gpsimd" else
                "hdma" if engname == "sync" else engname)
        waits = {k: v for k, v in ch.cnt.items() if v}

        def section(eng):
            for k, v in waits.items():
                eng.wait_ge(ch.sem[k], v)
            ch.seg_base = dict(ch.cnt)
            p = _ChainedEngine(eng, ch, kind)
            if kind in ("dma", "hdma"):
                for _ in fn(p):
                    pass
            else:
                fn(p)
        getattr(block, engname)(section)

    def final_wait(self, block):
        ch = self

        def section(eng):
            for k, v in ch.cnt.items():
                if v:
                    eng.wait_ge(ch.sem[k], v)
        block.gpsimd(section)


def _psum_views(handle, n, w):
    class _V:
        def __init__(self, off):
            self.off = off

        def __getitem__(self, idx):
            if not isinstance(idx, tuple):
                return handle[:, self.off:self.off + w]
            rows, cols = idx
            cs = cols.start or 0
            ce = cols.stop if cols.stop is not None else w
            return handle[rows, self.off + cs:self.off + ce]
    return [_V(i * w) for i in range(n)]


def _tr_wave(ch, blk, items, pst, idb):
    W = len(pst)
    for w0 in range(0, len(items), W):
        wave = items[w0:w0 + W]

        def tseg(t, wave=wave):
            for j, (srcf, _) in enumerate(wave):
                r = t.transpose(pst[j][:], srcf(), idb[:])
            return r
        ch.seg(blk, "tensor", tseg)

        def vseg(v, wave=wave):
            for j, (_, dstf) in enumerate(wave):
                r = v.tensor_copy(dstf(), pst[j][:])
            return r
        ch.seg(blk, "vector", vseg)


def _ln_block(ch, blk, nc_objs, with_out_x, o_x, o_ln, x_in_tiles, add_tiles):
    xt, m, vv, sq, lnb = nc_objs

    def resid(v):
        for i in range(NTT):
            xs = xt[:, i * D:(i + 1) * D]
            for af in add_tiles:
                v.tensor_tensor(xs, xs, af(i), op=ALU.add)
            v.reduce_sum(m[:, i:i + 1], xs, axis=AX.X)
            v.tensor_scalar(m[:, i:i + 1], m[:, i:i + 1], 1.0 / D, None,
                            op0=ALU.mult)
        r = v.tensor_copy(m[:, 0:1], m[:, 0:1])
        return r
    ch.seg(blk, "vector", resid)
    if with_out_x:
        ch.seg(blk, "gpsimd", lambda g: [
            g.dma_start(out=o_x.rearrange("(n p) c -> p n c", p=128), in_=xt[:].rearrange("p (n c) -> p n c", c=D))])

    def center(v):
        for i in range(NTT):
            xs = xt[:, i * D:(i + 1) * D]
            v.tensor_scalar(xs, xs, m[:, i:i + 1], None, op0=ALU.subtract)
            v.tensor_tensor(sq[:], xs, xs, op=ALU.mult)
            v.reduce_sum(vv[:, i:i + 1], sq[:], axis=AX.X)
            r = v.tensor_scalar(vv[:, i:i + 1], vv[:, i:i + 1], 1.0 / D, None,
                                op0=ALU.mult)
        return r
    ch.seg(blk, "vector", center)

    def epsadd(v):
        return v.tensor_scalar(vv[:], vv[:], 1e-5, None, op0=ALU.add)
    ch.seg(blk, "vector", epsadd)

    def rstds(s):
        s.activation(vv[:], vv[:], AF.Ln)
        return s.activation(vv[:], vv[:], AF.Exp, scale=-0.5)
    ch.seg(blk, "scalar", rstds)
    for i in range(NTT):
        def lnw(v, i=i):
            return v.tensor_scalar(lnb[:], xt[:, i * D:(i + 1) * D],
                                   vv[:, i:i + 1], None, op0=ALU.mult)
        ch.seg(blk, "vector", lnw)
        ch.seg(blk, "gpsimd", lambda g, i=i: [
            g.dma_start(out=o_ln[i * 128:(i + 1) * 128, :], in_=lnb[:])])


def build_L1():
    nc = bass.Bass("TRN2", target_bir_lowering=False)
    x = nc.dram_tensor("x", [TPC, D], F32, kind="ExternalInput").ap()
    nwrep = nc.dram_tensor("nwrep", [128, D], F32, kind="ExternalInput").ap()
    o_ln = nc.dram_tensor("o_ln", [TPC, D], F32, kind="ExternalOutput").ap()
    o_xn = nc.dram_tensor("o_xn", [TPC, D], BF16, kind="ExternalOutput").ap()
    with contextlib.ExitStack() as es:
        xt = es.enter_context(nc.sbuf_tensor("xt", [128, NTT * D], F32))
        nw = es.enter_context(nc.sbuf_tensor("nw", [128, D], F32))
        m = es.enter_context(nc.sbuf_tensor("m", [128, NTT], F32))
        vv = es.enter_context(nc.sbuf_tensor("vv", [128, NTT], F32))
        rstd = es.enter_context(nc.sbuf_tensor("rstd", [128, NTT], F32))
        tmp = es.enter_context(nc.sbuf_tensor("tmp", [128, NTT], F32))
        lnt = es.enter_context(nc.sbuf_tensor("lnt", [128, D], F32))
        xnt = es.enter_context(nc.sbuf_tensor("xnt", [128, D], BF16))
        sq = es.enter_context(nc.sbuf_tensor("sq", [128, D], F32))
        ch = Chain(nc, es)
        with nc.Block() as blk:
            ch.seg(blk, "gpsimd", lambda g: [
                g.dma_start(out=xt[:].rearrange("p (n c) -> p n c", c=D), in_=x.rearrange("(n p) c -> p n c", p=128)),
                g.dma_start(out=nw[:], in_=nwrep)])

            def stats(v):
                for i in range(NTT):
                    sl = xt[:, i * D:(i + 1) * D]
                    v.reduce_sum(m[:, i:i + 1], sl, axis=AX.X)
                    v.tensor_scalar(m[:, i:i + 1], m[:, i:i + 1], 1.0 / D, None,
                                    op0=ALU.mult)
                    v.tensor_scalar(sl, sl, m[:, i:i + 1], None, op0=ALU.subtract)
                    v.tensor_tensor(sq[:], sl, sl, op=ALU.mult)
                    v.reduce_sum(vv[:, i:i + 1], sq[:], axis=AX.X)
                    r = v.tensor_scalar(vv[:, i:i + 1], vv[:, i:i + 1], 1.0 / D,
                                        None, op0=ALU.mult)
                return r
            ch.seg(blk, "vector", stats)

            def epsadd1(v):
                return v.tensor_scalar(rstd[:], vv[:], 1e-5, None, op0=ALU.add)
            ch.seg(blk, "vector", epsadd1)

            def rstds(s):
                s.activation(rstd[:], rstd[:], AF.Ln)
                return s.activation(rstd[:], rstd[:], AF.Exp, scale=-0.5)
            ch.seg(blk, "scalar", rstds)

            def q(v):
                v.tensor_tensor(tmp[:], rstd[:], rstd[:], op=ALU.mult)
                return v.tensor_tensor(tmp[:], tmp[:], vv[:], op=ALU.mult)
            ch.seg(blk, "vector", q)

            def epsadd2(v):
                return v.tensor_scalar(tmp[:], tmp[:], 1e-6, None, op0=ALU.add)
            ch.seg(blk, "vector", epsadd2)

            def rs2(s):
                s.activation(tmp[:], tmp[:], AF.Ln)
                return s.activation(tmp[:], tmp[:], AF.Exp, scale=-0.5)
            ch.seg(blk, "scalar", rs2)

            def fin(v):
                return v.tensor_tensor(tmp[:], rstd[:], tmp[:], op=ALU.mult)
            ch.seg(blk, "vector", fin)
            for i in range(NTT):
                def wr(v, i=i):
                    xc = xt[:, i * D:(i + 1) * D]
                    v.tensor_scalar(lnt[:], xc, rstd[:, i:i + 1], None, op0=ALU.mult)
                    v.tensor_scalar(xc, xc, tmp[:, i:i + 1], None, op0=ALU.mult)
                    return v.tensor_tensor(xnt[:], xc, nw[:], op=ALU.mult)
                ch.seg(blk, "vector", wr)
                ch.seg(blk, "gpsimd", lambda g, i=i: [
                    g.dma_start(out=o_ln[i * 128:(i + 1) * 128, :], in_=lnt[:]),
                    g.dma_start(out=o_xn[i * 128:(i + 1) * 128, :], in_=xnt[:])])
            ch.final_wait(blk)
    return nc


def build_L2():
    NE = EXT // 128
    nc = bass.Bass("TRN2", target_bir_lowering=False)
    xn = nc.dram_tensor("xn", [EXT, D], BF16, kind="ExternalInput").ap()
    w = nc.dram_tensor("w", [D, 2 * D], F32, kind="ExternalInput").ap()
    bi = nc.dram_tensor("bi", [128, 16], F32, kind="ExternalInput").ap()
    cw = nc.dram_tensor("cw", [128, 32], F32, kind="ExternalInput").ap()
    cb = nc.dram_tensor("cb", [128, 8], F32, kind="ExternalInput").ap()
    zf = nc.dram_tensor("zf", [128, 1], F32, kind="ExternalInput").ap()
    ident = nc.dram_tensor("ident", [128, 128], BF16, kind="ExternalInput").ap()
    o_u = nc.dram_tensor("o_u", [D, TPC], F32, kind="ExternalOutput").ap()
    o_g = nc.dram_tensor("o_g", [D, TPC], BF16, kind="ExternalOutput").ap()
    NT5 = [(0, 512), (512, 512), (1024, 512), (1536, 512), (2048, 128)]
    with contextlib.ExitStack() as es:
        xT = [es.enter_context(nc.sbuf_tensor(f"xT{k}", [128, EXT], BF16))
              for k in range(8)]
        wsb = [es.enter_context(nc.sbuf_tensor(f"wsb{k}", [128, 2 * D], BF16))
               for k in range(8)]
        idb = es.enter_context(nc.sbuf_tensor("idb", [128, 128], BF16))
        bib = es.enter_context(nc.sbuf_tensor("bib", [128, 16], F32))
        cwb = es.enter_context(nc.sbuf_tensor("cwb", [128, 32], F32))
        cbb = es.enter_context(nc.sbuf_tensor("cbb", [128, 8], F32))
        zfb = es.enter_context(nc.sbuf_tensor("zfb", [128, 1], F32))
        uraw = es.enter_context(nc.sbuf_tensor("uraw", [128, EXT], F32))
        ucv = es.enter_context(nc.sbuf_tensor("ucv", [128, TPC], F32))
        usil = es.enter_context(nc.sbuf_tensor("usil", [128, TPC], F32))
        gsil = es.enter_context(nc.sbuf_tensor("gsil", [128, TPC], BF16))
        pst = [es.enter_context(nc.psum_tensor(f"pst{k}", [128, 128], BF16))
               for k in range(4)]
        psm = [es.enter_context(nc.psum_tensor(f"psm{k}", [128, 512], F32))
               for k in range(3)]
        ch = Chain(nc, es)
        with nc.Block() as blk:
            for kc in range(8):
                ch.seg(blk, "sync", lambda sy, kc=kc: [
                    sy.dma_start_transpose(xT[kc][:],
                                           xn[:, kc * 128:(kc + 1) * 128])])
            ch.seg(blk, "gpsimd", lambda g: [
                g.dma_start(out=idb[:], in_=ident),
                g.dma_start(out=bib[:], in_=bi),
                g.dma_start(out=cwb[:], in_=cw),
                g.dma_start(out=cbb[:], in_=cb),
                g.dma_start(out=zfb[:], in_=zf)] + [
                g.dma_start(out=wsb[k][:], in_=w[k * 128:(k + 1) * 128, :])
                for k in range(8)])

            for oc in range(16):
                for w0 in range(0, 5, 3):
                    sub = list(enumerate(NT5))[w0:w0 + 3]

                    def mseg(t, oc=oc, sub=sub):
                        for ni, (n0, nn) in sub:
                            for kc in range(8):
                                r = t.matmul(psm[ni - sub[0][0]][0:128, 0:nn],
                                             wsb[kc][:, oc * 128:(oc + 1) * 128],
                                             xT[kc][:, n0:n0 + nn],
                                             start=(kc == 0), stop=(kc == 7))
                        return r
                    ch.seg(blk, "tensor", mseg)

                    def eseg(v, oc=oc, sub=sub):
                        for ni, (n0, nn) in sub:
                            r = v.tensor_scalar(uraw[:, n0:n0 + nn],
                                                psm[ni - sub[0][0]][0:128, 0:nn],
                                                bib[:, oc:oc + 1], None, op0=ALU.add)
                        return r
                    ch.seg(blk, "vector", eseg)
                if oc < 8:
                    def conv(v, oc=oc):
                        v.tensor_scalar(uraw[:, 0:128], uraw[:, 0:128], zfb[:, 0:1],
                                        None, op0=ALU.mult)
                        v.tensor_scalar(ucv[:], uraw[:, 125:125 + TPC],
                                        cwb[:, oc * 4:oc * 4 + 1],
                                        cbb[:, oc:oc + 1], op0=ALU.mult, op1=ALU.add)
                        for k in range(1, 4):
                            r = v.scalar_tensor_tensor(
                                ucv[:], uraw[:, 125 + k:125 + k + TPC],
                                cwb[:, oc * 4 + k:oc * 4 + k + 1], ucv[:],
                                op0=ALU.mult, op1=ALU.add)
                        return r
                    ch.seg(blk, "vector", conv)
                    ch.seg(blk, "scalar",
                           lambda s: s.activation(usil[:], ucv[:], AF.Silu))
                    ch.seg(blk, "gpsimd", lambda g, oc=oc: [
                        g.dma_start(out=o_u[oc * 128:(oc + 1) * 128, :], in_=usil[:])])
                else:
                    ch.seg(blk, "scalar",
                           lambda s: s.activation(gsil[:], uraw[:, 128:EXT], AF.Silu))
                    ch.seg(blk, "gpsimd", lambda g, oc=oc: [
                        g.dma_start(out=o_g[(oc - 8) * 128:(oc - 7) * 128, :],
                                    in_=gsil[:])])
            ch.final_wait(blk)
    return nc


def build_L3():
    nc = bass.Bass("TRN2", target_bir_lowering=False)
    u = nc.dram_tensor("u", [D, TPC], F32, kind="ExternalInput").ap()
    pw = nc.dram_tensor("pw", [D, 3 * D], F32, kind="ExternalInput").ap()
    pb = nc.dram_tensor("pb", [128, 24], F32, kind="ExternalInput").ap()
    o_a = nc.dram_tensor("o_a", [D, TPC], F32, kind="ExternalOutput").ap()
    o_w = nc.dram_tensor("o_w", [D, TPC], F32, kind="ExternalOutput").ap()
    o_c = nc.dram_tensor("o_c", [D, TPC], BF16, kind="ExternalOutput").ap()
    with contextlib.ExitStack() as es:
        uf = [es.enter_context(nc.sbuf_tensor(f"uf{k}", [128, TPC], F32))
              for k in range(8)]
        ub = [es.enter_context(nc.sbuf_tensor(f"ub{k}", [128, TPC], BF16))
              for k in range(8)]
        wsb = es.enter_context(nc.sbuf_tensor("wsb", [128, 8 * 128], BF16))
        pbb = es.enter_context(nc.sbuf_tensor("pbb", [128, 24], F32))
        acc = es.enter_context(nc.sbuf_tensor("acc", [128, TPC], F32))
        accb = es.enter_context(nc.sbuf_tensor("accb", [128, TPC], BF16))
        psm = [es.enter_context(nc.psum_tensor(f"psm{k}", [128, 512], F32))
               for k in range(4)]
        ch = Chain(nc, es)
        with nc.Block() as blk:
            ch.seg(blk, "gpsimd", lambda g: [
                g.dma_start(out=uf[k][:], in_=u[k * 128:(k + 1) * 128, :])
                for k in range(8)] + [
                g.dma_start(out=pbb[:], in_=pb)])

            def cast(v):
                for k in range(8):
                    r = v.tensor_copy(ub[k][:], uf[k][:])
                return r
            ch.seg(blk, "vector", cast)
            for oc in range(24):
                ch.seg(blk, "gpsimd", lambda g, oc=oc: [
                    g.dma_start(out=wsb[:],
                                in_=bass.AP(pw.tensor, oc * 128,
                                            [[3 * D, 128], [128 * 3 * D, 8],
                                             [1, 128]]))])

                def mseg(t, oc=oc):
                    for ni in range(4):
                        for kc in range(8):
                            r = t.matmul(psm[ni][:],
                                         wsb[:, kc * 128:(kc + 1) * 128],
                                         ub[kc][:, ni * 512:(ni + 1) * 512],
                                         start=(kc == 0), stop=(kc == 7))
                    return r
                ch.seg(blk, "tensor", mseg)
                if oc < 8:
                    def ev(s, oc=oc):
                        for ni in range(4):
                            r = s.activation(acc[:, ni * 512:(ni + 1) * 512],
                                             psm[ni][:], AF.Sigmoid,
                                             bias=pbb[:, oc:oc + 1])
                        return r
                    ch.seg(blk, "scalar", ev)
                    ch.seg(blk, "gpsimd", lambda g, oc=oc: [
                        g.dma_start(out=o_a[oc * 128:(oc + 1) * 128, :], in_=acc[:])])
                elif oc < 16:
                    def ev(s, oc=oc):
                        for ni in range(4):
                            r = s.activation(acc[:, ni * 512:(ni + 1) * 512],
                                             psm[ni][:], AF.Tanh,
                                             bias=pbb[:, oc:oc + 1])
                        return r
                    ch.seg(blk, "scalar", ev)
                    ch.seg(blk, "vector",
                           lambda v, oc=oc: v.tensor_tensor(acc[:], acc[:],
                                                            uf[oc - 8][:],
                                                            op=ALU.mult))
                    ch.seg(blk, "gpsimd", lambda g, oc=oc: [
                        g.dma_start(out=o_w[(oc - 8) * 128:(oc - 7) * 128, :],
                                    in_=acc[:])])
                else:
                    def ev(s, oc=oc):
                        for ni in range(4):
                            r = s.activation(accb[:, ni * 512:(ni + 1) * 512],
                                             psm[ni][:], AF.Sigmoid,
                                             bias=pbb[:, oc:oc + 1])
                        return r
                    ch.seg(blk, "scalar", ev)
                    ch.seg(blk, "gpsimd", lambda g, oc=oc: [
                        g.dma_start(out=o_c[(oc - 16) * 128:(oc - 15) * 128, :],
                                    in_=accb[:])])
            ch.final_wait(blk)
    return nc


def build_L4():
    CH = 512
    nc = bass.Bass("TRN2", target_bir_lowering=False)
    a = nc.dram_tensor("a", [CH, T], F32, kind="ExternalInput").ap()
    w = nc.dram_tensor("w", [CH, T], F32, kind="ExternalInput").ap()
    c = nc.dram_tensor("c", [CH, T], BF16, kind="ExternalInput").ap()
    u = nc.dram_tensor("u", [CH, T], F32, kind="ExternalInput").ap()
    g_ = nc.dram_tensor("g", [CH, T], BF16, kind="ExternalInput").ap()
    skp = nc.dram_tensor("skp", [128, 4], F32, kind="ExternalInput").ap()
    o_y = nc.dram_tensor("o_y", [CH, T], BF16, kind="ExternalOutput").ap()
    with contextlib.ExitStack() as es:
        ab = es.enter_context(nc.sbuf_tensor("ab", [128, T], F32))
        wb = es.enter_context(nc.sbuf_tensor("wb", [128, T], F32))
        cb = es.enter_context(nc.sbuf_tensor("cb", [128, T], F32))
        ub = es.enter_context(nc.sbuf_tensor("ub", [128, T], F32))
        gb = es.enter_context(nc.sbuf_tensor("gb", [128, T], F32))
        sk = es.enter_context(nc.sbuf_tensor("sk", [128, 4], F32))
        st = es.enter_context(nc.sbuf_tensor("st", [128, T], F32))
        yb = es.enter_context(nc.sbuf_tensor("yb", [128, T], BF16))
        ch = Chain(nc, es)
        with nc.Block() as blk:
            ch.seg(blk, "gpsimd", lambda g: [
                g.dma_start(out=sk[:], in_=skp)])
            for grp in range(4):
                sl = slice(grp * 128, (grp + 1) * 128)
                ch.seg(blk, "gpsimd", lambda g, sl=sl: [
                    g.dma_start(out=ab[:], in_=a[sl, :]),
                    g.dma_start(out=wb[:], in_=w[sl, :]),
                    g.dma_start(out=cb[:], in_=c[sl, :]),
                    g.dma_start(out=ub[:], in_=u[sl, :]),
                    g.dma_start(out=gb[:], in_=g_[sl, :])])

                def scan(v, grp=grp):
                    v.tensor_tensor_scan(st[:], ab[:], wb[:], 0.0,
                                         op0=ALU.mult, op1=ALU.add)
                    v.tensor_tensor(st[:], st[:], cb[:], op=ALU.mult)
                    v.tensor_scalar(ub[:], ub[:], sk[:, grp:grp + 1], None,
                                    op0=ALU.mult)
                    v.tensor_tensor(st[:], st[:], ub[:], op=ALU.add)
                    return v.tensor_tensor(yb[:], st[:], gb[:], op=ALU.mult)
                ch.seg(blk, "vector", scan)
                ch.seg(blk, "gpsimd", lambda g, sl=sl: [
                    g.dma_start(out=o_y[sl, :], in_=yb[:])])
            ch.final_wait(blk)
    return nc


def build_L5():
    nc = bass.Bass("TRN2", target_bir_lowering=False)
    y = nc.dram_tensor("y", [D, TPC], BF16, kind="ExternalInput").ap()
    ow = nc.dram_tensor("ow", [D, D], F32, kind="ExternalInput").ap()
    ob = nc.dram_tensor("ob", [128, 8], F32, kind="ExternalInput").ap()
    x = nc.dram_tensor("x", [TPC, D], F32, kind="ExternalInput").ap()
    ln = nc.dram_tensor("ln", [TPC, D], F32, kind="ExternalInput").ap()
    ident = nc.dram_tensor("ident", [128, 128], BF16, kind="ExternalInput").ap()
    o_x1 = nc.dram_tensor("o_x1", [TPC, D], F32, kind="ExternalOutput").ap()
    o_ln2 = nc.dram_tensor("o_ln2", [TPC, D], BF16, kind="ExternalOutput").ap()
    with contextlib.ExitStack() as es:
        yb = [es.enter_context(nc.sbuf_tensor(f"yb{k}", [128, TPC], BF16))
              for k in range(8)]
        wsb = es.enter_context(nc.sbuf_tensor("wsb", [128, 8 * 128], BF16))
        obb = es.enter_context(nc.sbuf_tensor("obb", [128, 8], F32))
        idb = es.enter_context(nc.sbuf_tensor("idb", [128, 128], BF16))
        dT = es.enter_context(nc.sbuf_tensor("dT", [128, TPC], BF16))
        dtok = es.enter_context(nc.sbuf_tensor("dtok", [128, NTT * D], F32))
        xt = es.enter_context(nc.sbuf_tensor("xt", [128, NTT * D], F32))
        lns = es.enter_context(nc.sbuf_tensor("lns", [128, 4 * D], F32))
        m = es.enter_context(nc.sbuf_tensor("m", [128, NTT], F32))
        vv = es.enter_context(nc.sbuf_tensor("vv", [128, NTT], F32))
        sq = es.enter_context(nc.sbuf_tensor("sq", [128, D], F32))
        lnb = es.enter_context(nc.sbuf_tensor("lnb", [128, D], BF16))
        psm = [es.enter_context(nc.psum_tensor(f"psm{k}", [128, 512], F32))
               for k in range(4)]
        pst = [es.enter_context(nc.psum_tensor(f"pst{k}", [128, 128], BF16))
               for k in range(4)]
        ch = Chain(nc, es)
        with nc.Block() as blk:
            ch.seg(blk, "gpsimd", lambda g: [
                g.dma_start(out=yb[k][:], in_=y[k * 128:(k + 1) * 128, :])
                for k in range(8)] + [
                g.dma_start(out=obb[:], in_=ob),
                g.dma_start(out=idb[:], in_=ident),
                g.dma_start(out=xt[:].rearrange("p (n c) -> p n c", c=D), in_=x.rearrange("(n p) c -> p n c", p=128))])
            for r0 in range(0, NTT, 4):
                ch.seg(blk, "gpsimd", lambda g, r0=r0: [
                    g.dma_start(out=lns[:].rearrange("p (n c) -> p n c", c=D),
                                in_=ln[r0 * 128:(r0 + 4) * 128, :]
                                .rearrange("(n p) c -> p n c", p=128))])

                def addln(v, r0=r0):
                    for j in range(4):
                        v.tensor_tensor(xt[:, (r0 + j) * D:(r0 + j + 1) * D],
                                        xt[:, (r0 + j) * D:(r0 + j + 1) * D],
                                        lns[:, j * D:(j + 1) * D], op=ALU.add)
                ch.seg(blk, "vector", addln)
            for oc in range(8):
                ch.seg(blk, "gpsimd", lambda g, oc=oc: [
                    g.dma_start(out=wsb[:],
                                in_=bass.AP(ow.tensor, oc * 128,
                                            [[D, 128], [128 * D, 8], [1, 128]]))])

                def mseg(t, oc=oc):
                    for ni in range(4):
                        for kc in range(8):
                            r = t.matmul(psm[ni][:],
                                         wsb[:, kc * 128:(kc + 1) * 128],
                                         yb[kc][:, ni * 512:(ni + 1) * 512],
                                         start=(kc == 0), stop=(kc == 7))
                    return r
                ch.seg(blk, "tensor", mseg)

                def ev(v, oc=oc):
                    for ni in range(4):
                        r = v.tensor_scalar(dT[:, ni * 512:(ni + 1) * 512],
                                            psm[ni][:], obb[:, oc:oc + 1], None,
                                            op0=ALU.add)
                    return r
                ch.seg(blk, "vector", ev)
                items = [((lambda n=n: dT[:, n * 128:(n + 1) * 128]),
                          (lambda n=n, oc=oc:
                           dtok[:, n * D + oc * 128:n * D + oc * 128 + 128]))
                         for n in range(NTT)]
                _tr_wave(ch, blk, items, pst, idb)
            _ln_block(ch, blk, (xt, m, vv, sq, lnb), True, o_x1, o_ln2,
                      None, [lambda i: dtok[:, i * D:(i + 1) * D]])
            ch.final_wait(blk)
    return nc


def build_L6():
    nc = bass.Bass("TRN2", target_bir_lowering=False)
    ln2 = nc.dram_tensor("ln2", [TPC, D], BF16, kind="ExternalInput").ap()
    mk = nc.dram_tensor("mk", [4, 64, 16, 16], F32, kind="ExternalInput").ap()
    up = nc.dram_tensor("up", [4, 64, 64, 16], F32, kind="ExternalInput").ap()
    wc = nc.dram_tensor("wc", [D, 16 * D], F32, kind="ExternalInput").ap()
    msk = nc.dram_tensor("msk", [128, 128], BF16, kind="ExternalInput").ap()
    ident = nc.dram_tensor("ident", [128, 128], BF16, kind="ExternalInput").ap()
    x1 = nc.dram_tensor("x1", [TPC, D], F32, kind="ExternalInput").ap()
    o_x2 = nc.dram_tensor("o_x2", [TPC, D], F32, kind="ExternalOutput").ap()
    o_ln3 = nc.dram_tensor("o_ln3", [TPC, D], BF16, kind="ExternalOutput").ap()
    chbm = nc.dram_tensor("chbm", [4 * 8 * 8 * 16 * 64], BF16)
    with contextlib.ExitStack() as es:
        mskb = es.enter_context(nc.sbuf_tensor("mskb", [128, 128], BF16))
        idb = es.enter_context(nc.sbuf_tensor("idb", [128, 128], BF16))
        d1 = [es.enter_context(nc.sbuf_tensor(f"d1_{k}", [128, TPC], BF16))
              for k in range(8)]
        psm = [es.enter_context(nc.psum_tensor(f"psm{k}", [128, 512], F32))
               for k in range(4)]
        pstb = es.enter_context(nc.psum_tensor("pstb", [128, 512], BF16))
        pst = _psum_views(pstb, 4, 128)
        es_mid = es.enter_context(contextlib.ExitStack())
        lT = [es_mid.enter_context(nc.sbuf_tensor(f"lT{k}", [128, TPC], BF16))
              for k in range(8)]
        bd = es_mid.enter_context(nc.sbuf_tensor("bd", [128, 128 * 128], BF16))
        ch = Chain(nc, es)
        with nc.Block() as blk:
            ch.seg(blk, "gpsimd", lambda g: [
                g.dma_start(out=mskb[:], in_=msk),
                g.dma_start(out=idb[:], in_=ident)])
            for kc in range(8):
                ch.seg(blk, "sync", lambda sy, kc=kc: [
                    sy.dma_start_transpose(lT[kc][:],
                                           ln2[:, kc * 128:(kc + 1) * 128])])
            with contextlib.ExitStack() as es2:
                mkb = es2.enter_context(nc.sbuf_tensor("mkb", [128, 32 * 16], BF16))
                upb = es2.enter_context(nc.sbuf_tensor("upb", [128, 128 * 16], BF16))
                mkT = es2.enter_context(nc.sbuf_tensor("mkT", [16, 32 * 128], BF16))
                upT = es2.enter_context(nc.sbuf_tensor("upT", [16, 128 * 128], BF16))
                cst = es2.enter_context(nc.sbuf_tensor("cst", [16, 32 * 512], BF16))
                csb = es2.enter_context(nc.sbuf_tensor("csb", [128, 64], BF16))
                psc = es2.enter_context(nc.psum_tensor("psc", [16, 512], F32))
                ch.seg(blk, "gpsimd", lambda g: [
                    g.dma_start(out=mkb[:, i * 16:(i + 1) * 16],
                                in_=bass.AP(mk.tensor, i * 2048,
                                            [[16, 128], [1, 16]]))
                    for i in range(32)] + [
                    g.dma_start(out=upb[:, i * 16:(i + 1) * 16],
                                in_=bass.AP(up.tensor, i * 2048,
                                            [[16, 128], [1, 16]]))
                    for i in range(128)])
                items = []
                for i in range(32):
                    items.append((
                        (lambda i=i: mkb[:, i * 16:(i + 1) * 16]),
                        (lambda i=i: mkT[0:16, i * 128:(i + 1) * 128]),
                        True))
                for i in range(128):
                    items.append((
                        (lambda i=i: upb[:, i * 16:(i + 1) * 16]),
                        (lambda i=i: upT[0:16, i * 128:(i + 1) * 128]),
                        True))
                W = len(pst)
                for w0 in range(0, len(items), W):
                    wave = items[w0:w0 + W]

                    def tseg(t, wave=wave):
                        for j, (srcf, _, _) in enumerate(wave):
                            t.transpose(pst[j][0:16, 0:128], srcf(), idb[:])
                    ch.seg(blk, "tensor", tseg)

                    def vseg(v, wave=wave):
                        for j, (_, dstf, _) in enumerate(wave):
                            v.tensor_copy(dstf(), pst[j][0:16, 0:128])
                    ch.seg(blk, "vector", vseg)
                for hg in range(32):
                    def cm(t, hg=hg):
                        for rr in range(8):
                            r8 = hg * 8 + rr
                            t.matmul(psc[0:16, rr * 64:(rr + 1) * 64],
                                     mkT[0:16, r8 * 128 // 8 + 0:0] if False else
                                     mkT[0:16, hg * 128 + rr * 16:
                                         hg * 128 + rr * 16 + 16],
                                     upT[0:16, (hg * 8 + rr) * 128 + 0:
                                         (hg * 8 + rr) * 128 + 0] if False else
                                     upT[0:16, (hg * 8 + rr) * 128 // 2 + 0:0]
                                     if False else
                                     upT[0:16, hg * 512 + rr * 64:
                                         hg * 512 + rr * 64 + 64],
                                     start=True, stop=True)
                    ch.seg(blk, "tensor", cm)
                    ch.seg(blk, "vector",
                           lambda v, hg=hg: v.tensor_copy(
                               cst[:, hg * 512:(hg + 1) * 512], psc[0:16, :]))
                ch.seg(blk, "gpsimd", lambda g: [
                    g.dma_start(
                        out=bass.AP(chbm, hg * 8192,
                                    [[64, 16], [1024, 8], [1, 64]]),
                        in_=cst[:, hg * 512:(hg + 1) * 512]
                        .rearrange("p (a f) -> p a f", a=8))
                    for hg in range(32)])
                for hg in range(32):
                    ch.seg(blk, "gpsimd", lambda g, hg=hg: [
                        g.dma_start(out=csb[:],
                                    in_=bass.AP(chbm, hg * 8192,
                                                [[64, 128], [1, 64]]))])

                    def bdm(v, hg=hg):
                        for cc in range(4):
                            sl = bd[:, (hg * 4 + cc) * 128:
                                    (hg * 4 + cc) * 128 + 128]
                            v.tensor_tensor(
                                sl.rearrange("p (a f) -> p a f", a=8),
                                bass.AP(csb, 16 * cc,
                                        [[64, 128], [0, 8], [1, 16]]),
                                bass.AP(mskb, 0, [[128, 128], [16, 8], [1, 16]]),
                                op=ALU.mult)
                    ch.seg(blk, "vector", bdm)
            with contextlib.ExitStack() as es3:
                hz = [es3.enter_context(nc.sbuf_tensor(f"hz{k}", [128, TPC], BF16))
                      for k in range(16)]
                wblk = [es3.enter_context(nc.sbuf_tensor(f"wblk{j}", [128, 2048],
                                                         BF16)) for j in range(2)]
                wT = es3.enter_context(nc.sbuf_tensor("wT", [128, 16 * 128], BF16))
                wcc = es3.enter_context(nc.sbuf_tensor("wcc", [128, 2 * 128], BF16))
                scr4 = es3.enter_context(nc.sbuf_tensor("scr4", [128, 4 * 512],
                                                        BF16))
                pc = [es3.enter_context(nc.psum_tensor(f"pc{j}", [128, 512], F32))
                      for j in range(2)]
                pt2 = es3.enter_context(nc.psum_tensor("pt2", [128, 128], BF16))
                pt = [pst[0], pt2]
                s3t = es3.enter_context(nc.semaphore("s3t"))
                s3v = es3.enter_context(nc.semaphore("s3v"))
                s3a = es3.enter_context(nc.semaphore("s3a"))
                s3d = es3.enter_context(nc.semaphore("s3d"))
                s3w = es3.enter_context(nc.semaphore("s3w"))
                entry = {k: v for k, v in ch.cnt.items() if v}

                def te_z(it2, q):
                    return it2 * 112 + q + 1

                def te_wt(it2, oc, w):
                    return it2 * 112 + 16 + oc * 12 + w + 1

                def te_cp(it2, oc, ni):
                    return it2 * 112 + 16 + oc * 12 + 8 + ni + 1

                def ve_wg(it2, oc, w):
                    return it2 * 160 + oc * 20 + 2 * w + 1

                def ve_we(it2, oc, w):
                    return it2 * 160 + oc * 20 + 2 * w + 2

                def ve_acc(it2, oc, ni):
                    return it2 * 160 + oc * 20 + 16 + ni + 1

                def a_g(it2, q):
                    return it2 * 16 + q + 1

                D0 = ch.cnt["dma"]

                def s3_gp(g):
                    for k, v in entry.items():
                        g.wait_ge(ch.sem[k], v)
                    for it2 in range(8):
                        h, gh = it2 // 2, it2 % 2
                        for oc in range(8):
                            if it2 * 8 + oc >= 2:
                                po = it2 * 8 + oc - 2
                                g.wait_ge(s3v, (po // 8) * 160
                                          + (po % 8) * 20 + 15)
                            g.dma_start(out=wblk[oc % 2][:],
                                        in_=bass.AP(wc.tensor,
                                                    oc * 128 * 16384
                                                    + h * 4096 + gh * 2048,
                                                    [[16384, 128], [1, 2048]])
                                        ).then_inc(s3d, 16)
                    g.wait_ge(s3d, 64 * 16)
                blk.gpsimd(s3_gp)

                def s3_te(t):
                    for k, v in entry.items():
                        t.wait_ge(ch.sem[k], v)
                    for it2 in range(8):
                        h, gh = it2 // 2, it2 % 2
                        for q in range(16):
                            gi, cc = q // 4, q % 4
                            g2 = gh * 4 + gi
                            if q > 0:
                                t.wait_ge(s3a, a_g(it2, q - 1))
                            elif it2 > 0:
                                t.wait_ge(s3a, a_g(it2 - 1, 15))
                            for ni in range(4):
                                r = t.matmul(
                                    psm[ni][:],
                                    bd[:, ((h * 8 + g2) * 4 + cc) * 128:
                                       ((h * 8 + g2) * 4 + cc) * 128 + 128],
                                    lT[g2][:, ni * 512:(ni + 1) * 512],
                                    start=True, stop=True)
                            r.then_inc(s3t, 1)
                        for oc in range(8):
                            for w in range(8):
                                t.wait_ge(s3v, ve_wg(it2, oc, w))
                                if w >= 1:
                                    t.wait_ge(s3v, ve_we(it2, oc, w - 1))
                                elif oc > 0:
                                    t.wait_ge(s3v, ve_we(it2, oc - 1, 7))
                                elif it2 > 0:
                                    t.wait_ge(s3v, ve_we(it2 - 1, 7, 7))
                                for j in range(2):
                                    r = t.transpose(pt[j][:],
                                                    wcc[:, j * 128:(j + 1) * 128],
                                                    idb[:])
                                r.then_inc(s3t, 1)
                            for ni in range(4):
                                if ni == 0:
                                    t.wait_ge(s3a, a_g(it2, 15))
                                    t.wait_ge(s3v, ve_we(it2, oc, 7))
                                b = ni % 2
                                if oc > 0:
                                    t.wait_ge(s3v, ve_acc(it2, oc - 1, b + 2))
                                elif it2 > 0:
                                    t.wait_ge(s3v, ve_acc(it2 - 1, 7, b + 2))
                                if ni >= 2:
                                    t.wait_ge(s3v, ve_acc(it2, oc, ni - 2))
                                for q in range(16):
                                    r = t.matmul(pc[b][:],
                                                 wT[:, q * 128:(q + 1) * 128],
                                                 hz[q][:, ni * 512:(ni + 1) * 512],
                                                 start=(q == 0), stop=(q == 15))
                                r.then_inc(s3t, 1)
                blk.tensor(s3_te)

                def s3_act(s):
                    for k, v in entry.items():
                        s.wait_ge(ch.sem[k], v)
                    for it2 in range(8):
                        for q in range(16):
                            s.wait_ge(s3t, te_z(it2, q))
                            if q == 0 and it2 > 0:
                                s.wait_ge(s3t, te_cp(it2 - 1, 7, 3))
                            for ni in range(4):
                                r = s.activation(
                                    hz[q][:, ni * 512:(ni + 1) * 512],
                                    psm[ni][:], AF.Gelu)
                            r.then_inc(s3a, 1)
                blk.scalar(s3_act)

                def s3_ve(v):
                    for k, v2 in entry.items():
                        v.wait_ge(ch.sem[k], v2)
                    nvv = 0
                    for it2 in range(8):
                        for oc in range(8):
                            for w in range(8):
                                v.wait_ge(s3d, 32 * ((it2 * 8 + oc) // 2 + 1))
                                if w > 0:
                                    v.wait_ge(s3t, te_wt(it2, oc, w - 1))
                                for j in range(2):
                                    q = w * 2 + j
                                    gi, cc = q // 4, q % 4
                                    r = v.tensor_copy(
                                        wcc[:, j * 128:(j + 1) * 128]
                                        .rearrange("p (a f) -> p a f", a=8),
                                        bass.AP(wblk[oc % 2],
                                                gi * 512 + 16 * cc,
                                                [[2048, 128], [64, 8], [1, 16]]))
                                r.then_inc(s3v, 1)
                                v.wait_ge(s3t, te_wt(it2, oc, w))
                                for j in range(2):
                                    q = w * 2 + j
                                    r = v.tensor_copy(
                                        wT[:, q * 128:(q + 1) * 128], pt[j][:])
                                r.then_inc(s3v, 1)
                            for ni in range(4):
                                v.wait_ge(s3t, te_cp(it2, oc, ni))
                                b = ni % 2
                                if it2 == 0:
                                    r = v.tensor_copy(
                                        d1[oc][:, ni * 512:(ni + 1) * 512],
                                        pc[b][:])
                                else:
                                    v.tensor_copy(
                                        scr4[:, ni * 512:(ni + 1) * 512],
                                        pc[b][:]).then_inc(s3w, 1)
                                    nvv += 1
                                    v.wait_ge(s3w, nvv)
                                    r = v.tensor_tensor(
                                        d1[oc][:, ni * 512:(ni + 1) * 512],
                                        d1[oc][:, ni * 512:(ni + 1) * 512],
                                        scr4[:, ni * 512:(ni + 1) * 512],
                                        op=ALU.add)
                                r.then_inc(s3v, 1)
                blk.vector(s3_ve)

                def bridge(v):
                    v.wait_ge(s3t, 8 * 112)
                    v.wait_ge(s3v, 8 * 160)
                    v.wait_ge(s3a, 8 * 16)
                    v.wait_ge(s3d, 64 * 16)
                    v.memset(scr4[:, 0:4], 0.0).then_inc(ch.sem["vector"], 1)
                blk.vector(bridge)
                ch.cnt["vector"] += 1
            es_mid.close()
            with contextlib.ExitStack() as es4:
                dtok = es4.enter_context(nc.sbuf_tensor("dtok",
                                                        [128, NTT * D], F32))
                xt = es4.enter_context(nc.sbuf_tensor("xt", [128, NTT * D], F32))
                m = es4.enter_context(nc.sbuf_tensor("m", [128, NTT], F32))
                vv = es4.enter_context(nc.sbuf_tensor("vv", [128, NTT], F32))
                sq = es4.enter_context(nc.sbuf_tensor("sq", [128, D], F32))
                lnb = es4.enter_context(nc.sbuf_tensor("lnb", [128, D], BF16))
                ch.seg(blk, "gpsimd", lambda g: [
                    g.dma_start(out=xt[:].rearrange("p (n c) -> p n c", c=D),
                                in_=x1.rearrange("(n p) c -> p n c", p=128))])
                for oc in range(8):
                    items = [((lambda n=n, oc=oc: d1[oc][:, n * 128:(n + 1) * 128]),
                              (lambda n=n, oc=oc:
                               dtok[:, n * D + oc * 128:n * D + oc * 128 + 128]))
                             for n in range(NTT)]
                    _tr_wave(ch, blk, items, pst, idb)
                _ln_block(ch, blk, (xt, m, vv, sq, lnb), True, o_x2, o_ln3,
                          None, [lambda i: dtok[:, i * D:(i + 1) * D]])
            ch.final_wait(blk)
    return nc


def build_L7():
    NT = T // 128
    nc = bass.Bass("TRN2", target_bir_lowering=False)
    ln3 = nc.dram_tensor("ln3", [T, D], BF16, kind="ExternalInput").ap()
    wq = nc.dram_tensor("wq", [512, D], F32, kind="ExternalInput").ap()
    wk = nc.dram_tensor("wk", [512, D], F32, kind="ExternalInput").ap()
    wv = nc.dram_tensor("wv", [512, D], F32, kind="ExternalInput").ap()
    rot = nc.dram_tensor("rot", [128, 128], BF16, kind="ExternalInput").ap()
    cosb = nc.dram_tensor("cosb", [128, T], BF16, kind="ExternalInput").ap()
    sinb = nc.dram_tensor("sinb", [128, T], BF16, kind="ExternalInput").ap()
    ident = nc.dram_tensor("ident", [128, 128], BF16, kind="ExternalInput").ap()
    o_qr = nc.dram_tensor("o_qr", [4, 128, T], BF16, kind="ExternalOutput").ap()
    o_kr = nc.dram_tensor("o_kr", [4, 128, T], BF16, kind="ExternalOutput").ap()
    o_v = nc.dram_tensor("o_v", [T, 512], BF16, kind="ExternalOutput").ap()
    with contextlib.ExitStack() as es:
        lT = [es.enter_context(nc.sbuf_tensor(f"lT{k}", [128, T], BF16))
              for k in range(8)]
        idb = es.enter_context(nc.sbuf_tensor("idb", [128, 128], BF16))
        rotb = es.enter_context(nc.sbuf_tensor("rotb", [128, 128], BF16))
        cosbs = es.enter_context(nc.sbuf_tensor("cosbs", [128, T], BF16))
        sinbs = es.enter_context(nc.sbuf_tensor("sinbs", [128, T], BF16))
        wqb = es.enter_context(nc.sbuf_tensor("wqb", [128, D], BF16))
        wqT = es.enter_context(nc.sbuf_tensor("wqT", [128, 8 * 128], BF16))
        qT = es.enter_context(nc.sbuf_tensor("qT", [128, T], BF16))
        rq = es.enter_context(nc.sbuf_tensor("rq", [128, T], BF16))
        t1 = es.enter_context(nc.sbuf_tensor("t1", [128, T], BF16))
        wvT = [es.enter_context(nc.sbuf_tensor(f"wvT{k}", [128, 512], BF16))
               for k in range(8)]
        vtk4 = es.enter_context(nc.sbuf_tensor("vtk4", [128, 4 * 512], BF16))
        psm = [es.enter_context(nc.psum_tensor(f"psm{k}", [128, 512], F32))
               for k in range(4)]
        pst = [es.enter_context(nc.psum_tensor(f"pst{k}", [128, 128], BF16))
               for k in range(2)]
        ch = Chain(nc, es)
        with nc.Block() as blk:
            for kc in range(8):
                ch.seg(blk, "sync", lambda sy, kc=kc: [
                    sy.dma_start_transpose(lT[kc][:],
                                           ln3[:, kc * 128:(kc + 1) * 128])])
            ch.seg(blk, "gpsimd", lambda g: [
                g.dma_start(out=idb[:], in_=ident),
                g.dma_start(out=rotb[:], in_=rot),
                g.dma_start(out=cosbs[:], in_=cosb),
                g.dma_start(out=sinbs[:], in_=sinb)])
            for wsrc, odst in ((wq, o_qr), (wk, o_kr)):
                for hp in range(4):
                    ch.seg(blk, "gpsimd", lambda g, wsrc=wsrc, hp=hp: [
                        g.dma_start(out=wqb[:],
                                    in_=wsrc[hp * 128:(hp + 1) * 128, :])])
                    items = [((lambda kc=kc: wqb[:, kc * 128:(kc + 1) * 128]),
                              (lambda kc=kc: wqT[:, kc * 128:(kc + 1) * 128]))
                             for kc in range(8)]
                    _tr_wave(ch, blk, items, pst, idb)
                    for n0 in range(0, 8, 4):
                        def mm(t, n0=n0):
                            for ni in range(4):
                                for kc in range(8):
                                    r = t.matmul(psm[ni][:],
                                                 wqT[:, kc * 128:(kc + 1) * 128],
                                                 lT[kc][:, (n0 + ni) * 512:
                                                         (n0 + ni + 1) * 512],
                                                 start=(kc == 0), stop=(kc == 7))
                            return r
                        ch.seg(blk, "tensor", mm)

                        def ev(v, n0=n0):
                            for ni in range(4):
                                r = v.tensor_copy(
                                    qT[:, (n0 + ni) * 512:(n0 + ni + 1) * 512],
                                    psm[ni][:])
                            return r
                        ch.seg(blk, "vector", ev)
                    for n0 in range(0, 8, 4):
                        def rm(t, n0=n0):
                            for ni in range(4):
                                r = t.matmul(psm[ni][:], rotb[:],
                                             qT[:, (n0 + ni) * 512:(n0 + ni + 1) * 512],
                                             start=True, stop=True)
                            return r
                        ch.seg(blk, "tensor", rm)

                        def re(v, n0=n0):
                            for ni in range(4):
                                r = v.tensor_copy(
                                    rq[:, (n0 + ni) * 512:(n0 + ni + 1) * 512],
                                    psm[ni][:])
                            return r
                        ch.seg(blk, "vector", re)

                    def rope(v):
                        v.tensor_tensor(t1[:], qT[:], cosbs[:], op=ALU.mult)
                        v.tensor_tensor(rq[:], rq[:], sinbs[:], op=ALU.mult)
                        return v.tensor_tensor(t1[:], t1[:], rq[:], op=ALU.add)
                    ch.seg(blk, "vector", rope)
                    ch.seg(blk, "gpsimd", lambda g, odst=odst, hp=hp: [
                        g.dma_start(out=odst[hp, :, :], in_=t1[:])])
            for ro in range(4):
                ch.seg(blk, "gpsimd", lambda g, ro=ro: [
                    g.dma_start(out=wqb[:], in_=wv[ro * 128:(ro + 1) * 128, :])])
                items = [((lambda kc=kc: wqb[:, kc * 128:(kc + 1) * 128]),
                          (lambda kc=kc, ro=ro: wvT[kc][:, ro * 128:(ro + 1) * 128]))
                         for kc in range(8)]
                _tr_wave(ch, blk, items, pst, idb)
            for tc0 in range(0, NT, 4):
                def vm(t, tc0=tc0):
                    for j in range(4):
                        for kc in range(8):
                            t.matmul(psm[j][:],
                                     lT[kc][:, (tc0 + j) * 128:(tc0 + j + 1) * 128],
                                     wvT[kc][:], start=(kc == 0), stop=(kc == 7))
                ch.seg(blk, "tensor", vm)

                def vev(v, tc0=tc0):
                    for j in range(4):
                        v.tensor_copy(vtk4[:, j * 512:(j + 1) * 512], psm[j][:])
                ch.seg(blk, "vector", vev)
                ch.seg(blk, "gpsimd", lambda g, tc0=tc0: [
                    g.dma_start(out=o_v[tc0 * 128:(tc0 + 4) * 128, :]
                                .rearrange("(a p) f -> p a f", p=128),
                                in_=vtk4[:].rearrange("p (a f) -> p a f", f=512))])
            ch.final_wait(blk)
    return nc


def build_L8():
    NT = T // 128
    nc = bass.Bass("TRN2", target_bir_lowering=False)
    qr = nc.dram_tensor("qr", [4, 128, T], BF16, kind="ExternalInput").ap()
    kr = nc.dram_tensor("kr", [4, 128, T], BF16, kind="ExternalInput").ap()
    vt = nc.dram_tensor("vt", [T, 512], BF16, kind="ExternalInput").ap()
    ident = nc.dram_tensor("ident", [128, 128], BF16, kind="ExternalInput").ap()
    o_d = nc.dram_tensor("o_d", [T, 512], BF16, kind="ExternalOutput").ap()
    HPD = 288
    QTE = NT + 8
    HXE = 8 * NT + 1
    with contextlib.ExitStack() as es:
        qs = es.enter_context(nc.sbuf_tensor("qs", [128, T], BF16))
        ks = es.enter_context(nc.sbuf_tensor("ks", [128, T], BF16))
        vp = [es.enter_context(nc.sbuf_tensor(f"vp{j}", [128, NT * 65], BF16))
              for j in range(2)]
        idb = es.enter_context(nc.sbuf_tensor("idb", [128, 128], BF16))
        pa = [[es.enter_context(nc.sbuf_tensor(f"pa{j}_{p}", [128, 512], BF16))
               for p in range(2)] for j in range(2)]
        ou = [es.enter_context(nc.sbuf_tensor(f"ou{j}", [128, 512], BF16))
              for j in range(2)]
        rz = es.enter_context(nc.sbuf_tensor("rz", [128, 1], F32))
        da = es.enter_context(nc.sbuf_tensor("da", [128, 4 * 128], BF16))
        psc = [es.enter_context(nc.psum_tensor(f"psc{j}", [128, 512], F32))
               for j in range(4)]
        psv = [es.enter_context(nc.psum_tensor(f"psv{j}", [128, 512], F32))
               for j in range(2)]
        pst = [es.enter_context(nc.psum_tensor(f"pstA", [128, 128], BF16)),
               es.enter_context(nc.psum_tensor(f"pstB", [128, 128], BF16))]
        s_ld = es.enter_context(nc.semaphore("s_ld"))
        s_st = es.enter_context(nc.semaphore("s_st"))
        s_qk = es.enter_context(nc.semaphore("s_qk"))
        s_ex = es.enter_context(nc.semaphore("s_ex"))
        s_pv = es.enter_context(nc.semaphore("s_pv"))
        s_ep = es.enter_context(nc.semaphore("s_ep"))
        s_vv = es.enter_context(nc.semaphore("s_vv"))
        with nc.Block() as blk:
            def gp(g):
                g.dma_start(out=idb[:], in_=ident).then_inc(s_ld, 16)
                nld = 16
                for hp in range(4):
                    if hp > 0:
                        g.wait_ge(s_pv, hp * 8 * QTE)
                    g.dma_start(out=qs[:], in_=qr[hp, :, :]).then_inc(s_ld, 16)
                    g.dma_start(out=ks[:], in_=kr[hp, :, :]).then_inc(s_ld, 16)
                    for j in range(2):
                        for c0 in range(4):
                            g.dma_start(
                                out=bass.AP(vp[j], c0 * 8 * 65,
                                            [[NT * 65, 128], [65, 8], [1, 64]]),
                                in_=bass.AP(vt.tensor,
                                            c0 * 8 * 128 * 512 + hp * 128 + j * 64,
                                            [[512, 128], [128 * 512, 8], [1, 64]])
                            ).then_inc(s_ld, 16)
                    nld += 160
                    for qt in range(8):
                        it = hp * 8 + qt
                        g.wait_ge(s_ep, it * 9 + 9)
                        g.dma_start(
                            out=bass.AP(o_d.tensor, qt * 512 * 512 + hp * 128,
                                        [[512, 128], [128 * 512, 4], [1, 128]]),
                            in_=da[:].rearrange("p (a f) -> p a f", a=4)
                        ).then_inc(s_st, 16)
                g.wait_ge(s_ld, nld)
                g.wait_ge(s_st, 32 * 16)
            blk.gpsimd(gp)

            def ve(v):
                nvv = 0
                for hp in range(4):
                    v.wait_ge(s_ld, 16 + (hp + 1) * 160)
                    v.memset(bass.AP(vp[0], 64, [[NT * 65, 128], [65, NT]]), 1.0)
                    v.memset(bass.AP(vp[1], 64,
                                     [[NT * 65, 128], [65, NT]]),
                             1.0).then_inc(s_ex, 1)
                    for qt in range(8):
                        it = hp * 8 + qt
                        v.wait_ge(s_pv, it * QTE + NT)
                        v.tensor_copy(ou[0][0:65, :], psv[0][0:65, :])
                        v.tensor_copy(ou[1][0:65, :],
                                      psv[1][0:65, :]).then_inc(s_ep, 1)
                        if it > 0:
                            v.wait_ge(s_st, it * 16)
                        for k in range(8):
                            qsi, j = k // 2, k % 2
                            v.wait_ge(s_pv, it * QTE + NT + k + 1)
                            if k > 0:
                                v.wait_ge(s_ep, it * 9 + 1 + k)
                            v.reciprocal(rz[:],
                                         pst[k % 2][:, 64:65]).then_inc(s_vv, 1)
                            nvv += 1
                            v.wait_ge(s_vv, nvv)
                            v.tensor_scalar(
                                da[:, qsi * 128 + j * 64:
                                   qsi * 128 + j * 64 + 64],
                                pst[k % 2][:, 0:64], rz[:],
                                None, op0=ALU.mult).then_inc(s_ep, 1)
            blk.vector(ve)

            def sc(s):
                for hp in range(4):
                    H = hp * HXE + 1
                    for qt in range(8):
                        it = hp * 8 + qt
                        for c in range(NT):
                            s.wait_ge(s_qk, it * NT + c + 1)
                            if c >= 2:
                                s.wait_ge(s_pv, it * QTE + c - 1)
                            elif it > 0:
                                s.wait_ge(s_pv, (it - 1) * QTE + NT - 1 + c)
                            s.activation(pa[0][c % 2][:], psc[2 * (c % 2)][:],
                                         AF.Exp, scale=0.125)
                            s.activation(pa[1][c % 2][:],
                                         psc[2 * (c % 2) + 1][:],
                                         AF.Exp, scale=0.125).then_inc(s_ex, 1)
            blk.scalar(sc)

            def te(t):
                for hp in range(4):
                    t.wait_ge(s_ld, 16 + (hp + 1) * 160)
                    t.wait_ge(s_ex, hp * HXE + 1)
                    H = hp * HXE + 1
                    for qt in range(8):
                        it = hp * 8 + qt
                        q0 = qt * 512

                        def qk_mm(c):
                            if c >= 2 or qt > 0:
                                t.wait_ge(s_ex, H + qt * NT + c - 1)
                            t.matmul(psc[2 * (c % 2)][:],
                                     ks[0:64, c * 128:(c + 1) * 128],
                                     qs[0:64, q0:q0 + 512],
                                     start=True, stop=True, tile_position=(0, 0))
                            t.matmul(psc[2 * (c % 2) + 1][:],
                                     ks[64:128, c * 128:(c + 1) * 128],
                                     qs[64:128, q0:q0 + 512],
                                     start=True, stop=True,
                                     tile_position=(64, 0)).then_inc(s_qk, 1)

                        def pv_mm(c):
                            t.wait_ge(s_ex, H + qt * NT + c + 1)
                            if c == 0 and it > 0:
                                t.wait_ge(s_ep, (it - 1) * 9 + 1)
                            t.matmul(psv[0][0:65, :],
                                     vp[0][:, c * 65:(c + 1) * 65],
                                     pa[0][c % 2][:],
                                     start=(c == 0), stop=(c == NT - 1))
                            t.matmul(psv[1][0:65, :],
                                     vp[1][:, c * 65:(c + 1) * 65],
                                     pa[1][c % 2][:],
                                     start=(c == 0),
                                     stop=(c == NT - 1)).then_inc(s_pv, 1)
                        qk_mm(0)
                        for c in range(1, NT):
                            qk_mm(c)
                            pv_mm(c - 1)
                        pv_mm(NT - 1)
                        t.wait_ge(s_ep, it * 9 + 1)
                        for k in range(8):
                            qsi, j = k // 2, k % 2
                            if k >= 2:
                                t.wait_ge(s_ep, it * 9 + k)
                            elif it > 0:
                                t.wait_ge(s_ep, (it - 1) * 9 + k + 8)
                            t.transpose(pst[k % 2][:, 0:65],
                                        ou[j][0:65, qsi * 128:(qsi + 1) * 128],
                                        idb[0:65, 0:65]).then_inc(s_pv, 1)
            blk.tensor(te)
    return nc


def build_L9():
    nc = bass.Bass("TRN2", target_bir_lowering=False)
    x2 = nc.dram_tensor("x2", [TPC, D], F32, kind="ExternalInput").ap()
    dat = nc.dram_tensor("dat", [TPC, D], BF16, kind="ExternalInput").ap()
    o_x3 = nc.dram_tensor("o_x3", [TPC, D], F32, kind="ExternalOutput").ap()
    o_ln4 = nc.dram_tensor("o_ln4", [TPC, D], BF16, kind="ExternalOutput").ap()
    with contextlib.ExitStack() as es:
        xt = es.enter_context(nc.sbuf_tensor("xt", [128, NTT * D], F32))
        db = es.enter_context(nc.sbuf_tensor("db", [128, NTT * D], F32))
        m = es.enter_context(nc.sbuf_tensor("m", [128, NTT], F32))
        vv = es.enter_context(nc.sbuf_tensor("vv", [128, NTT], F32))
        sq = es.enter_context(nc.sbuf_tensor("sq", [128, D], F32))
        lnb = es.enter_context(nc.sbuf_tensor("lnb", [128, D], BF16))
        ch = Chain(nc, es)
        with nc.Block() as blk:
            ch.seg(blk, "gpsimd", lambda g: [
                g.dma_start(out=xt[:].rearrange("p (n c) -> p n c", c=D), in_=x2.rearrange("(n p) c -> p n c", p=128)),
                g.dma_start(out=db[:].rearrange("p (n c) -> p n c", c=D), in_=dat.rearrange("(n p) c -> p n c", p=128))])
            _ln_block(ch, blk, (xt, m, vv, sq, lnb), True, o_x3, o_ln4,
                      None, [lambda i: db[:, i * D:(i + 1) * D]])
            ch.final_wait(blk)
    return nc


def build_L10():
    NA = 2048
    NAT = NA // 128
    nc = bass.Bass("TRN2", target_bir_lowering=False)
    ln4 = nc.dram_tensor("ln4", [NA, D], BF16, kind="ExternalInput").ap()
    w1 = nc.dram_tensor("w1", [D, 2048], F32, kind="ExternalInput").ap()
    b1 = nc.dram_tensor("b1", [128, 2048], F32, kind="ExternalInput").ap()
    w2 = nc.dram_tensor("w2", [D, D], F32, kind="ExternalInput").ap()
    b2 = nc.dram_tensor("b2", [128, 8], F32, kind="ExternalInput").ap()
    ri = nc.dram_tensor("ri", [NA, 2], I32, kind="ExternalInput").ap()
    rw = nc.dram_tensor("rw", [128, 16], F32, kind="ExternalInput").ap()
    ev_ = nc.dram_tensor("ev", [128, 1], F32, kind="ExternalInput").ap()
    ident = nc.dram_tensor("ident", [128, 128], BF16, kind="ExternalInput").ap()
    o_p = nc.dram_tensor("o_p", [NA, D], BF16, kind="ExternalOutput").ap()
    with contextlib.ExitStack() as es:
        lT = [es.enter_context(nc.sbuf_tensor(f"lT{k}", [128, NA], BF16))
              for k in range(8)]
        idb = es.enter_context(nc.sbuf_tensor("idb", [128, 128], BF16))
        w1b = [es.enter_context(nc.sbuf_tensor(f"w1b{k}", [128, 2048], BF16))
               for k in range(8)]
        b1b = es.enter_context(nc.sbuf_tensor("b1b", [128, 2048], F32))
        w2b = [es.enter_context(nc.sbuf_tensor(f"w2b{k}", [128, D], BF16))
               for k in range(8)]
        b2b = es.enter_context(nc.sbuf_tensor("b2b", [128, 8], F32))
        idxb = es.enter_context(nc.sbuf_tensor("idxb", [128, 32], I32))
        idxf = es.enter_context(nc.sbuf_tensor("idxf", [128, 32], F32))
        evb = es.enter_context(nc.sbuf_tensor("evb", [128, 1], F32))
        rwb = es.enter_context(nc.sbuf_tensor("rwb", [128, 16], F32))
        wcol = es.enter_context(nc.sbuf_tensor("wcol", [128, 16], F32))
        eq = es.enter_context(nc.sbuf_tensor("eq", [128, 32], F32))
        gc = es.enter_context(nc.sbuf_tensor("gc", [128, 1024], F32))
        sg = es.enter_context(nc.sbuf_tensor("sg", [128, 1024], F32))
        uc = es.enter_context(nc.sbuf_tensor("uc", [128, 1024], F32))
        inter = es.enter_context(nc.sbuf_tensor("inter", [128, D], BF16))
        iT = [es.enter_context(nc.sbuf_tensor(f"iT{k}", [128, NA], BF16))
              for k in range(8)]
        dTo = [es.enter_context(nc.sbuf_tensor(f"dTo{k}", [128, NA], BF16))
               for k in range(8)]
        dtok = es.enter_context(nc.sbuf_tensor("dtok", [128, D], BF16))
        psm = [es.enter_context(nc.psum_tensor(f"psm{k}", [128, 512], F32))
               for k in range(4)]
        pst = [es.enter_context(nc.psum_tensor(f"pst{k}", [128, 128], BF16))
               for k in range(2)]
        ch = Chain(nc, es)
        with nc.Block() as blk:
            for kc in range(8):
                ch.seg(blk, "sync", lambda sy, kc=kc: [
                    sy.dma_start_transpose(lT[kc][:],
                                           ln4[:, kc * 128:(kc + 1) * 128])])
            ch.seg(blk, "gpsimd", lambda g: [
                g.dma_start(out=idb[:], in_=ident),
                g.dma_start(out=b1b[:], in_=b1),
                g.dma_start(out=b2b[:], in_=b2),
                g.dma_start(out=evb[:], in_=ev_),
                g.dma_start(out=idxb[:].rearrange("p (c k) -> p c k", k=2),
                            in_=ri.rearrange("(c p) k -> p c k", p=128)),
                g.dma_start(out=rwb[:], in_=rw)] + [
                g.dma_start(out=w1b[k][:], in_=w1[k * 128:(k + 1) * 128, :])
                for k in range(8)] + [
                g.dma_start(out=w2b[k][:], in_=w2[k * 128:(k + 1) * 128, :])
                for k in range(8)])

            def wcol_f(v):
                v.tensor_copy(idxf[:], idxb[:])
                v.tensor_scalar(eq[:, 0:16],
                                bass.AP(idxf, 0, [[32, 128], [2, 16]]),
                                evb[:], None, op0=ALU.is_equal)
                v.tensor_scalar(eq[:, 16:32],
                                bass.AP(idxf, 1, [[32, 128], [2, 16]]),
                                evb[:], None, op0=ALU.is_equal)
                v.tensor_tensor(wcol[:], eq[:, 0:16], eq[:, 16:32], op=ALU.add)
                return v.tensor_tensor(wcol[:], wcol[:], rwb[:], op=ALU.mult)
            ch.seg(blk, "vector", wcol_f)
            for tc in range(NAT):
                def mm(t, tc=tc):
                    for o4 in range(4):
                        for kc in range(8):
                            t.matmul(psm[o4][:],
                                     lT[kc][:, tc * 128:(tc + 1) * 128],
                                     w1b[kc][:, o4 * 512:(o4 + 1) * 512],
                                     start=(kc == 0), stop=(kc == 7))
                ch.seg(blk, "tensor", mm)

                def swi(v):
                    for o4 in range(4):
                        v.tensor_tensor(psm[o4][:], psm[o4][:],
                                        b1b[:, o4 * 512:(o4 + 1) * 512], op=ALU.add)
                        v.tensor_scalar(gc[:, o4 * 256:(o4 + 1) * 256],
                                        bass.AP(psm[o4], 0, [[512, 128], [2, 256]]),
                                        7.0, None, op0=ALU.min)
                        v.tensor_scalar(uc[:, o4 * 256:(o4 + 1) * 256],
                                        bass.AP(psm[o4], 1, [[512, 128], [2, 256]]),
                                        7.0, -7.0, op0=ALU.min, op1=ALU.max)
                ch.seg(blk, "vector", swi)
                ch.seg(blk, "scalar",
                       lambda s: s.activation(sg[:], gc[:], AF.Sigmoid,
                                              scale=1.702))

                def glu(v):
                    v.tensor_tensor(gc[:], gc[:], sg[:], op=ALU.mult)
                    v.scalar_tensor_tensor(inter[:], uc[:], 1.0, gc[:],
                                           op0=ALU.add, op1=ALU.mult)
                ch.seg(blk, "vector", glu)
                items = [((lambda ic=ic: inter[:, ic * 128:(ic + 1) * 128]),
                          (lambda ic=ic, tc=tc: iT[ic][:, tc * 128:(tc + 1) * 128]))
                         for ic in range(8)]
                _tr_wave(ch, blk, items, pst, idb)
            for oc in range(8):
                def dm(t, oc=oc):
                    for ni in range(4):
                        for kc in range(8):
                            r = t.matmul(psm[ni][:],
                                         w2b[kc][:, oc * 128:(oc + 1) * 128],
                                         iT[kc][:, ni * 512:(ni + 1) * 512],
                                         start=(kc == 0), stop=(kc == 7))
                    return r
                ch.seg(blk, "tensor", dm)

                def ev2(v, oc=oc):
                    for ni in range(4):
                        r = v.tensor_scalar(dTo[oc][:, ni * 512:(ni + 1) * 512],
                                            psm[ni][:], b2b[:, oc:oc + 1], None,
                                            op0=ALU.add)
                    return r
                ch.seg(blk, "vector", ev2)
            for tc in range(NAT):
                items = [((lambda oc=oc, tc=tc: dTo[oc][:, tc * 128:(tc + 1) * 128]),
                          (lambda oc=oc: dtok[:, oc * 128:(oc + 1) * 128]))
                         for oc in range(8)]
                _tr_wave(ch, blk, items, pst, idb)

                def wmul(v, tc=tc):
                    return v.tensor_scalar(dtok[:], dtok[:], wcol[:, tc:tc + 1],
                                           None, op0=ALU.mult)
                ch.seg(blk, "vector", wmul)
                ch.seg(blk, "gpsimd", lambda g, tc=tc: [
                    g.dma_start(out=o_p[tc * 128:(tc + 1) * 128, :], in_=dtok[:])])
            ch.final_wait(blk)
    return nc


def build_L11():
    R = 2048 // NC
    nc = bass.Bass("TRN2", target_bir_lowering=False)
    x3 = nc.dram_tensor("x3", [R, D], F32, kind="ExternalInput").ap()
    ps = [nc.dram_tensor(f"p{e}", [R, D], BF16, kind="ExternalInput").ap()
          for e in range(8)]
    o = nc.dram_tensor("o", [R, D], F32, kind="ExternalOutput").ap()
    with contextlib.ExitStack() as es:
        xb = es.enter_context(nc.sbuf_tensor("xb", [128, (R // 128) * D], F32))
        pb = es.enter_context(nc.sbuf_tensor("pb", [128, (R // 128) * D], F32))
        ch = Chain(nc, es)
        with nc.Block() as blk:
            ch.seg(blk, "gpsimd", lambda g: [
                g.dma_start(out=xb[:].rearrange("p (n c) -> p n c", c=D), in_=x3.rearrange("(n p) c -> p n c", p=128))])
            for e in range(8):
                ch.seg(blk, "gpsimd", lambda g, e=e: [
                    g.dma_start(out=pb[:].rearrange("p (n c) -> p n c", c=D),
                                in_=ps[e].rearrange("(n p) c -> p n c", p=128))])
                ch.seg(blk, "vector",
                       lambda v: v.tensor_tensor(xb[:], xb[:], pb[:], op=ALU.add))
            ch.seg(blk, "gpsimd", lambda g: [
                g.dma_start(out=o.rearrange("(n p) c -> p n c", p=128), in_=xb[:].rearrange("p (n c) -> p n c", c=D))])
            ch.final_wait(blk)
    return nc


def _cat(outs, name, axis):
    return np.concatenate([o[name] for o in outs], axis=axis)


_BUILD_CACHE = {}


def _run(key, build_fn, in_maps):
    if key not in _BUILD_CACHE:
        _BUILD_CACHE[key] = build_fn()
    return run_bass_kernel_spmd(_BUILD_CACHE[key], in_maps,
                                core_ids=list(range(NC))).results


def _consts():
    ident = np.eye(128, dtype=np.float32).astype(NBF)
    mask = np.zeros((128, 128), np.float32)
    for b0 in range(8):
        mask[b0 * 16:(b0 + 1) * 16, b0 * 16:(b0 + 1) * 16] = 1.0
    mask = mask.astype(NBF)
    P = np.zeros((128, 128), np.float32)
    for i in range(64):
        P[2 * i, 2 * i + 1] = -1.0
        P[2 * i + 1, 2 * i] = 1.0
    rotT = P.T.copy().astype(NBF)
    pos = np.arange(T, dtype=np.float32)
    inv = np.power(10000.0, -2.0 * np.arange(32, dtype=np.float32) / 64)
    ang = pos[:, None] * inv[None, :]
    cos = np.repeat(np.cos(ang), 2, axis=1).T
    sin = np.repeat(np.sin(ang), 2, axis=1).T
    cosb = np.concatenate([cos, cos], 0).astype(NBF)
    sinb = np.concatenate([sin, sin], 0).astype(NBF)
    return ident, mask, rotT, cosb, sinb


def kernel(**inp):
    ident, mask, rotT, cosb, sinb = _consts()
    x = np.ascontiguousarray(inp["L0_attn_x"].reshape(B * T, D).astype(np.float32))

    nwrep = np.ascontiguousarray(
        np.broadcast_to(inp["L0_attn_norm_w"][None, :], (128, D)).astype(np.float32))
    o1 = _run("L1", build_L1, [
        {"x": x[c * TPC:(c + 1) * TPC], "nwrep": nwrep} for c in range(NC)])
    ln = _cat(o1, "o_ln", 0)
    xn = _cat(o1, "o_xn", 0)

    in_w = np.ascontiguousarray(inp["L0_attn_in_w"].astype(np.float32))
    in_b = np.ascontiguousarray(inp["L0_attn_in_b"].reshape(16, 128).T)
    conv_w = np.ascontiguousarray(
        inp["L0_attn_conv_w"].reshape(4, 8, 128).transpose(2, 1, 0).reshape(128, 32))
    conv_b = np.ascontiguousarray(inp["L0_attn_conv_b"].reshape(8, 128).T)
    zero_lb = np.zeros((128, D), NBF)
    maps = []
    for c in range(NC):
        s0 = c * TPC
        lb = zero_lb if (c % 2 == 0) else np.ascontiguousarray(xn[s0 - 128:s0])
        zfv = np.zeros((128, 1), np.float32) if c % 2 == 0 else \
            np.ones((128, 1), np.float32)
        maps.append({"xn": np.concatenate([lb, xn[s0:s0 + TPC]], 0),
                     "w": in_w, "bi": in_b, "cw": conv_w, "cb": conv_b,
                     "zf": zfv, "ident": ident})
    o2 = _run("L2", build_L2, maps)
    uT = _cat(o2, "o_u", 1)
    gT = _cat(o2, "o_g", 1)

    pw = np.ascontiguousarray(inp["L0_attn_param_w"].astype(np.float32))
    pb = np.ascontiguousarray(inp["L0_attn_param_b"].reshape(24, 128).T)
    o3 = _run("L3", build_L3, [
        {"u": np.ascontiguousarray(uT[:, c * TPC:(c + 1) * TPC]),
         "pw": pw, "pb": pb} for c in range(NC)])
    aT = _cat(o3, "o_a", 1)
    wT = _cat(o3, "o_w", 1)
    cT = _cat(o3, "o_c", 1)

    skip_r = np.ascontiguousarray(inp["L0_attn_skip"].reshape(8, 128))
    maps = []
    for c in range(NC):
        b, hh = c // 2, c % 2
        rs = slice(hh * 512, (hh + 1) * 512)
        cs = slice(b * T, (b + 1) * T)
        maps.append({"a": np.ascontiguousarray(aT[rs, cs]),
                     "w": np.ascontiguousarray(wT[rs, cs]),
                     "c": np.ascontiguousarray(cT[rs, cs]),
                     "u": np.ascontiguousarray(uT[rs, cs]),
                     "g": np.ascontiguousarray(gT[rs, cs]),
                     "skp": np.ascontiguousarray(skip_r[hh * 4:(hh + 1) * 4].T)})
    o4 = _run("L4", build_L4, maps)
    yT = np.empty((D, B * T), NBF)
    for c in range(NC):
        b, hh = c // 2, c % 2
        yT[hh * 512:(hh + 1) * 512, b * T:(b + 1) * T] = o4[c]["o_y"]

    ow = np.ascontiguousarray(inp["L0_attn_out_w"].astype(np.float32))
    ob = np.ascontiguousarray(inp["L0_attn_out_b"].reshape(8, 128).T)
    o5 = _run("L5", build_L5, [
        {"y": np.ascontiguousarray(yT[:, c * TPC:(c + 1) * TPC]),
         "ow": ow, "ob": ob, "x": x[c * TPC:(c + 1) * TPC],
         "ln": ln[c * TPC:(c + 1) * TPC], "ident": ident} for c in range(NC)])
    x1 = _cat(o5, "o_x1", 0)
    ln2 = _cat(o5, "o_ln2", 0)

    o6 = _run("L6", build_L6, [
        {"ln2": ln2[c * TPC:(c + 1) * TPC], "mk": inp["L0_ffn_rail_maker"],
         "up": inp["L0_ffn_rail_up_proj"], "wc": inp["L0_ffn_w_cproj"],
         "msk": mask, "ident": ident, "x1": x1[c * TPC:(c + 1) * TPC]}
        for c in range(NC)])
    x2 = _cat(o6, "o_x2", 0)
    ln3 = _cat(o6, "o_ln3", 0)

    maps = []
    for c in range(NC):
        b, hh = c // 2, c % 2
        maps.append({"ln3": ln3[b * T:(b + 1) * T],
                     "wq": np.ascontiguousarray(
                         inp["L1_attn_Wq"][hh * 512:(hh + 1) * 512].astype(np.float32)),
                     "wk": np.ascontiguousarray(
                         inp["L1_attn_Wk"][hh * 512:(hh + 1) * 512].astype(np.float32)),
                     "wv": np.ascontiguousarray(
                         inp["L1_attn_Wv"][hh * 512:(hh + 1) * 512].astype(np.float32)),
                     "rot": rotT, "cosb": cosb, "sinb": sinb, "ident": ident})
    o7 = _run("L7", build_L7, maps)

    o8 = _run("L8", build_L8, [
        {"qr": o7[c]["o_qr"], "kr": o7[c]["o_kr"], "vt": o7[c]["o_v"],
         "ident": ident} for c in range(NC)])
    datt = np.empty((B * T, D), NBF)
    for c in range(NC):
        b, hh = c // 2, c % 2
        datt[b * T:(b + 1) * T, hh * 512:(hh + 1) * 512] = o8[c]["o_d"]

    o9 = _run("L9", build_L9, [
        {"x2": x2[c * TPC:(c + 1) * TPC],
         "dat": np.ascontiguousarray(datt[c * TPC:(c + 1) * TPC])}
        for c in range(NC)])
    x3 = _cat(o9, "o_x3", 0)
    ln4a = o9[0]["o_ln4"]

    NA = 2048
    b1r = np.ascontiguousarray(
        np.broadcast_to(inp["L1_ffn_gate_up_proj_bias"][:, None, :],
                        (8, 128, 2048)).astype(np.float32))
    maps = []
    for e in range(NC):
        maps.append({"ln4": ln4a,
                     "w1": np.ascontiguousarray(
                         inp["L1_ffn_gate_up_proj"][e].astype(np.float32)),
                     "b1": b1r[e],
                     "w2": np.ascontiguousarray(
                         inp["L1_ffn_down_proj"][e].astype(np.float32)),
                     "b2": np.ascontiguousarray(
                         inp["L1_ffn_down_proj_bias"][e].reshape(8, 128).T),
                     "ri": inp["L1_ffn_router_indices"],
                     "rw": np.ascontiguousarray(
                         inp["L1_ffn_routing_weights"][:, e].reshape(16, 128).T),
                     "ev": np.full((128, 1), float(e), np.float32),
                     "ident": ident})
    o10 = _run("L10", build_L10, maps)

    R = NA // NC
    o11 = _run("L11", build_L11, [
        dict({"x3": x3[c * R:(c + 1) * R]},
             **{f"p{e}": np.ascontiguousarray(o10[e]["o_p"][c * R:(c + 1) * R])
                for e in range(8)}) for c in range(NC)])
    fin_active = _cat(o11, "o", 0)

    out = x3.copy()
    out[0:NA] = fin_active
    return out.reshape(B, T, D)


# version 09
# speedup vs baseline: 1.0002x; 1.0002x over previous
import contextlib
import numpy as np
import ml_dtypes
import concourse.bass as bass
import concourse.mybir as mybir
from concourse.bass_utils import run_bass_kernel_spmd

F32 = mybir.dt.float32
BF16 = mybir.dt.bfloat16
I32 = mybir.dt.int32
AF = mybir.ActivationFunctionType
ALU = mybir.AluOpType
AX = mybir.AxisListType
NBF = ml_dtypes.bfloat16

B, T, D = 4, 4096, 1024
NC = 8
TPC = 2048
EXT = TPC + 128
NTT = TPC // 128


_DMA_OPS = {"dma_start", "dma_start_transpose"}


class _ChainedEngine:

    def __init__(self, eng, chain, kind):
        self._eng = eng
        self._ch = chain
        self._kind = kind

    def __getattr__(self, name):
        f = getattr(self._eng, name)
        if not callable(f):
            return f

        def wrapped(*a, **k):
            ch = self._ch
            kind = self._kind
            if kind in ("vector", "scalar"):
                sem, cnt = ch.sem[kind], ch.cnt[kind]
                if cnt > ch.seg_base[kind]:
                    self._eng.wait_ge(sem, cnt)
            r = f(*a, **k)
            if name in _DMA_OPS:
                dk = kind if kind == "hdma" else "dma"
                r.then_inc(ch.sem[dk], 16)
                ch.cnt[dk] += 16
            else:
                r.then_inc(ch.sem[kind], 1)
                ch.cnt[kind] += 1
            return r
        return wrapped


class Chain:

    def __init__(self, nc, es):
        self.sem = {
            "dma": es.enter_context(nc.semaphore("chd")),
            "hdma": es.enter_context(nc.semaphore("chh")),
            "tensor": es.enter_context(nc.semaphore("cht")),
            "vector": es.enter_context(nc.semaphore("chv")),
            "scalar": es.enter_context(nc.semaphore("chs")),
        }
        self.cnt = {"dma": 0, "hdma": 0, "tensor": 0, "vector": 0, "scalar": 0}
        self.seg_base = dict(self.cnt)

    def seg(self, block, engname, fn):
        ch = self
        kind = ("dma" if engname == "gpsimd" else
                "hdma" if engname == "sync" else engname)
        waits = {k: v for k, v in ch.cnt.items() if v}

        def section(eng):
            for k, v in waits.items():
                eng.wait_ge(ch.sem[k], v)
            ch.seg_base = dict(ch.cnt)
            p = _ChainedEngine(eng, ch, kind)
            if kind in ("dma", "hdma"):
                for _ in fn(p):
                    pass
            else:
                fn(p)
        getattr(block, engname)(section)

    def final_wait(self, block):
        ch = self

        def section(eng):
            for k, v in ch.cnt.items():
                if v:
                    eng.wait_ge(ch.sem[k], v)
        block.gpsimd(section)


def _psum_views(handle, n, w):
    class _V:
        def __init__(self, off):
            self.off = off

        def __getitem__(self, idx):
            if not isinstance(idx, tuple):
                return handle[:, self.off:self.off + w]
            rows, cols = idx
            cs = cols.start or 0
            ce = cols.stop if cols.stop is not None else w
            return handle[rows, self.off + cs:self.off + ce]
    return [_V(i * w) for i in range(n)]


def _tr_wave(ch, blk, items, pst, idb):
    W = len(pst)
    for w0 in range(0, len(items), W):
        wave = items[w0:w0 + W]

        def tseg(t, wave=wave):
            for j, (srcf, _) in enumerate(wave):
                r = t.transpose(pst[j][:], srcf(), idb[:])
            return r
        ch.seg(blk, "tensor", tseg)

        def vseg(v, wave=wave):
            for j, (_, dstf) in enumerate(wave):
                r = v.tensor_copy(dstf(), pst[j][:])
            return r
        ch.seg(blk, "vector", vseg)


def _ln_block(ch, blk, nc_objs, with_out_x, o_x, o_ln, x_in_tiles, add_tiles):
    xt, m, vv, sq, lnb = nc_objs

    def resid(v):
        for i in range(NTT):
            xs = xt[:, i * D:(i + 1) * D]
            for af in add_tiles:
                v.tensor_tensor(xs, xs, af(i), op=ALU.add)
            v.reduce_sum(m[:, i:i + 1], xs, axis=AX.X)
            v.tensor_scalar(m[:, i:i + 1], m[:, i:i + 1], 1.0 / D, None,
                            op0=ALU.mult)
        r = v.tensor_copy(m[:, 0:1], m[:, 0:1])
        return r
    ch.seg(blk, "vector", resid)
    if with_out_x:
        ch.seg(blk, "gpsimd", lambda g: [
            g.dma_start(out=o_x.rearrange("(n p) c -> p n c", p=128), in_=xt[:].rearrange("p (n c) -> p n c", c=D))])

    def center(v):
        for i in range(NTT):
            xs = xt[:, i * D:(i + 1) * D]
            v.tensor_scalar(xs, xs, m[:, i:i + 1], None, op0=ALU.subtract)
            v.tensor_tensor(sq[:], xs, xs, op=ALU.mult)
            v.reduce_sum(vv[:, i:i + 1], sq[:], axis=AX.X)
            r = v.tensor_scalar(vv[:, i:i + 1], vv[:, i:i + 1], 1.0 / D, None,
                                op0=ALU.mult)
        return r
    ch.seg(blk, "vector", center)

    def epsadd(v):
        return v.tensor_scalar(vv[:], vv[:], 1e-5, None, op0=ALU.add)
    ch.seg(blk, "vector", epsadd)

    def rstds(s):
        s.activation(vv[:], vv[:], AF.Ln)
        return s.activation(vv[:], vv[:], AF.Exp, scale=-0.5)
    ch.seg(blk, "scalar", rstds)
    for i in range(NTT):
        def lnw(v, i=i):
            return v.tensor_scalar(lnb[:], xt[:, i * D:(i + 1) * D],
                                   vv[:, i:i + 1], None, op0=ALU.mult)
        ch.seg(blk, "vector", lnw)
        ch.seg(blk, "gpsimd", lambda g, i=i: [
            g.dma_start(out=o_ln[i * 128:(i + 1) * 128, :], in_=lnb[:])])


def build_L1():
    nc = bass.Bass("TRN2", target_bir_lowering=False)
    x = nc.dram_tensor("x", [TPC, D], F32, kind="ExternalInput").ap()
    nwrep = nc.dram_tensor("nwrep", [128, D], F32, kind="ExternalInput").ap()
    o_ln = nc.dram_tensor("o_ln", [TPC, D], F32, kind="ExternalOutput").ap()
    o_xn = nc.dram_tensor("o_xn", [TPC, D], BF16, kind="ExternalOutput").ap()
    with contextlib.ExitStack() as es:
        xt = es.enter_context(nc.sbuf_tensor("xt", [128, NTT * D], F32))
        nw = es.enter_context(nc.sbuf_tensor("nw", [128, D], F32))
        m = es.enter_context(nc.sbuf_tensor("m", [128, NTT], F32))
        vv = es.enter_context(nc.sbuf_tensor("vv", [128, NTT], F32))
        rstd = es.enter_context(nc.sbuf_tensor("rstd", [128, NTT], F32))
        tmp = es.enter_context(nc.sbuf_tensor("tmp", [128, NTT], F32))
        lnt = es.enter_context(nc.sbuf_tensor("lnt", [128, D], F32))
        xnt = es.enter_context(nc.sbuf_tensor("xnt", [128, D], BF16))
        sq = es.enter_context(nc.sbuf_tensor("sq", [128, D], F32))
        ch = Chain(nc, es)
        with nc.Block() as blk:
            ch.seg(blk, "gpsimd", lambda g: [
                g.dma_start(out=xt[:].rearrange("p (n c) -> p n c", c=D), in_=x.rearrange("(n p) c -> p n c", p=128)),
                g.dma_start(out=nw[:], in_=nwrep)])

            def stats(v):
                for i in range(NTT):
                    sl = xt[:, i * D:(i + 1) * D]
                    v.reduce_sum(m[:, i:i + 1], sl, axis=AX.X)
                    v.tensor_scalar(m[:, i:i + 1], m[:, i:i + 1], 1.0 / D, None,
                                    op0=ALU.mult)
                    v.tensor_scalar(sl, sl, m[:, i:i + 1], None, op0=ALU.subtract)
                    v.tensor_tensor(sq[:], sl, sl, op=ALU.mult)
                    v.reduce_sum(vv[:, i:i + 1], sq[:], axis=AX.X)
                    r = v.tensor_scalar(vv[:, i:i + 1], vv[:, i:i + 1], 1.0 / D,
                                        None, op0=ALU.mult)
                return r
            ch.seg(blk, "vector", stats)

            def epsadd1(v):
                return v.tensor_scalar(rstd[:], vv[:], 1e-5, None, op0=ALU.add)
            ch.seg(blk, "vector", epsadd1)

            def rstds(s):
                s.activation(rstd[:], rstd[:], AF.Ln)
                return s.activation(rstd[:], rstd[:], AF.Exp, scale=-0.5)
            ch.seg(blk, "scalar", rstds)

            def q(v):
                v.tensor_tensor(tmp[:], rstd[:], rstd[:], op=ALU.mult)
                return v.tensor_tensor(tmp[:], tmp[:], vv[:], op=ALU.mult)
            ch.seg(blk, "vector", q)

            def epsadd2(v):
                return v.tensor_scalar(tmp[:], tmp[:], 1e-6, None, op0=ALU.add)
            ch.seg(blk, "vector", epsadd2)

            def rs2(s):
                s.activation(tmp[:], tmp[:], AF.Ln)
                return s.activation(tmp[:], tmp[:], AF.Exp, scale=-0.5)
            ch.seg(blk, "scalar", rs2)

            def fin(v):
                return v.tensor_tensor(tmp[:], rstd[:], tmp[:], op=ALU.mult)
            ch.seg(blk, "vector", fin)
            for i in range(NTT):
                def wr(v, i=i):
                    xc = xt[:, i * D:(i + 1) * D]
                    v.tensor_scalar(lnt[:], xc, rstd[:, i:i + 1], None, op0=ALU.mult)
                    v.tensor_scalar(xc, xc, tmp[:, i:i + 1], None, op0=ALU.mult)
                    return v.tensor_tensor(xnt[:], xc, nw[:], op=ALU.mult)
                ch.seg(blk, "vector", wr)
                ch.seg(blk, "gpsimd", lambda g, i=i: [
                    g.dma_start(out=o_ln[i * 128:(i + 1) * 128, :], in_=lnt[:]),
                    g.dma_start(out=o_xn[i * 128:(i + 1) * 128, :], in_=xnt[:])])
            ch.final_wait(blk)
    return nc


def build_L2():
    NE = EXT // 128
    nc = bass.Bass("TRN2", target_bir_lowering=False)
    xn = nc.dram_tensor("xn", [EXT, D], BF16, kind="ExternalInput").ap()
    w = nc.dram_tensor("w", [D, 2 * D], F32, kind="ExternalInput").ap()
    bi = nc.dram_tensor("bi", [128, 16], F32, kind="ExternalInput").ap()
    cw = nc.dram_tensor("cw", [128, 32], F32, kind="ExternalInput").ap()
    cb = nc.dram_tensor("cb", [128, 8], F32, kind="ExternalInput").ap()
    zf = nc.dram_tensor("zf", [128, 1], F32, kind="ExternalInput").ap()
    ident = nc.dram_tensor("ident", [128, 128], BF16, kind="ExternalInput").ap()
    o_u = nc.dram_tensor("o_u", [D, TPC], F32, kind="ExternalOutput").ap()
    o_g = nc.dram_tensor("o_g", [D, TPC], BF16, kind="ExternalOutput").ap()
    NT5 = [(0, 512), (512, 512), (1024, 512), (1536, 512), (2048, 128)]
    with contextlib.ExitStack() as es:
        xT = [es.enter_context(nc.sbuf_tensor(f"xT{k}", [128, EXT], BF16))
              for k in range(8)]
        wsb = [es.enter_context(nc.sbuf_tensor(f"wsb{k}", [128, 2 * D], BF16))
               for k in range(8)]
        idb = es.enter_context(nc.sbuf_tensor("idb", [128, 128], BF16))
        bib = es.enter_context(nc.sbuf_tensor("bib", [128, 16], F32))
        cwb = es.enter_context(nc.sbuf_tensor("cwb", [128, 32], F32))
        cbb = es.enter_context(nc.sbuf_tensor("cbb", [128, 8], F32))
        zfb = es.enter_context(nc.sbuf_tensor("zfb", [128, 1], F32))
        uraw = es.enter_context(nc.sbuf_tensor("uraw", [128, EXT], F32))
        ucv = es.enter_context(nc.sbuf_tensor("ucv", [128, TPC], F32))
        usil = es.enter_context(nc.sbuf_tensor("usil", [128, TPC], F32))
        gsil = es.enter_context(nc.sbuf_tensor("gsil", [128, TPC], BF16))
        pst = [es.enter_context(nc.psum_tensor(f"pst{k}", [128, 128], BF16))
               for k in range(4)]
        psm = [es.enter_context(nc.psum_tensor(f"psm{k}", [128, 512], F32))
               for k in range(3)]
        ch = Chain(nc, es)
        with nc.Block() as blk:
            for kc in range(8):
                ch.seg(blk, "sync", lambda sy, kc=kc: [
                    sy.dma_start_transpose(xT[kc][:],
                                           xn[:, kc * 128:(kc + 1) * 128])])
            ch.seg(blk, "gpsimd", lambda g: [
                g.dma_start(out=idb[:], in_=ident),
                g.dma_start(out=bib[:], in_=bi),
                g.dma_start(out=cwb[:], in_=cw),
                g.dma_start(out=cbb[:], in_=cb),
                g.dma_start(out=zfb[:], in_=zf)] + [
                g.dma_start(out=wsb[k][:], in_=w[k * 128:(k + 1) * 128, :])
                for k in range(8)])

            for oc in range(16):
                for w0 in range(0, 5, 3):
                    sub = list(enumerate(NT5))[w0:w0 + 3]

                    def mseg(t, oc=oc, sub=sub):
                        for ni, (n0, nn) in sub:
                            for kc in range(8):
                                r = t.matmul(psm[ni - sub[0][0]][0:128, 0:nn],
                                             wsb[kc][:, oc * 128:(oc + 1) * 128],
                                             xT[kc][:, n0:n0 + nn],
                                             start=(kc == 0), stop=(kc == 7))
                        return r
                    ch.seg(blk, "tensor", mseg)

                    def eseg(v, oc=oc, sub=sub):
                        for ni, (n0, nn) in sub:
                            r = v.tensor_scalar(uraw[:, n0:n0 + nn],
                                                psm[ni - sub[0][0]][0:128, 0:nn],
                                                bib[:, oc:oc + 1], None, op0=ALU.add)
                        return r
                    ch.seg(blk, "vector", eseg)
                if oc < 8:
                    def conv(v, oc=oc):
                        v.tensor_scalar(uraw[:, 0:128], uraw[:, 0:128], zfb[:, 0:1],
                                        None, op0=ALU.mult)
                        v.tensor_scalar(ucv[:], uraw[:, 125:125 + TPC],
                                        cwb[:, oc * 4:oc * 4 + 1],
                                        cbb[:, oc:oc + 1], op0=ALU.mult, op1=ALU.add)
                        for k in range(1, 4):
                            r = v.scalar_tensor_tensor(
                                ucv[:], uraw[:, 125 + k:125 + k + TPC],
                                cwb[:, oc * 4 + k:oc * 4 + k + 1], ucv[:],
                                op0=ALU.mult, op1=ALU.add)
                        return r
                    ch.seg(blk, "vector", conv)
                    ch.seg(blk, "scalar",
                           lambda s: s.activation(usil[:], ucv[:], AF.Silu))
                    ch.seg(blk, "gpsimd", lambda g, oc=oc: [
                        g.dma_start(out=o_u[oc * 128:(oc + 1) * 128, :], in_=usil[:])])
                else:
                    ch.seg(blk, "scalar",
                           lambda s: s.activation(gsil[:], uraw[:, 128:EXT], AF.Silu))
                    ch.seg(blk, "gpsimd", lambda g, oc=oc: [
                        g.dma_start(out=o_g[(oc - 8) * 128:(oc - 7) * 128, :],
                                    in_=gsil[:])])
            ch.final_wait(blk)
    return nc


def build_L3():
    nc = bass.Bass("TRN2", target_bir_lowering=False)
    u = nc.dram_tensor("u", [D, TPC], F32, kind="ExternalInput").ap()
    pw = nc.dram_tensor("pw", [D, 3 * D], F32, kind="ExternalInput").ap()
    pb = nc.dram_tensor("pb", [128, 24], F32, kind="ExternalInput").ap()
    o_a = nc.dram_tensor("o_a", [D, TPC], F32, kind="ExternalOutput").ap()
    o_w = nc.dram_tensor("o_w", [D, TPC], F32, kind="ExternalOutput").ap()
    o_c = nc.dram_tensor("o_c", [D, TPC], BF16, kind="ExternalOutput").ap()
    with contextlib.ExitStack() as es:
        uf = [es.enter_context(nc.sbuf_tensor(f"uf{k}", [128, TPC], F32))
              for k in range(8)]
        ub = [es.enter_context(nc.sbuf_tensor(f"ub{k}", [128, TPC], BF16))
              for k in range(8)]
        wsb = es.enter_context(nc.sbuf_tensor("wsb", [128, 8 * 128], BF16))
        pbb = es.enter_context(nc.sbuf_tensor("pbb", [128, 24], F32))
        acc = es.enter_context(nc.sbuf_tensor("acc", [128, TPC], F32))
        accb = es.enter_context(nc.sbuf_tensor("accb", [128, TPC], BF16))
        psm = [es.enter_context(nc.psum_tensor(f"psm{k}", [128, 512], F32))
               for k in range(4)]
        ch = Chain(nc, es)
        with nc.Block() as blk:
            ch.seg(blk, "gpsimd", lambda g: [
                g.dma_start(out=uf[k][:], in_=u[k * 128:(k + 1) * 128, :])
                for k in range(8)] + [
                g.dma_start(out=pbb[:], in_=pb)])

            def cast(v):
                for k in range(8):
                    r = v.tensor_copy(ub[k][:], uf[k][:])
                return r
            ch.seg(blk, "vector", cast)
            for oc in range(24):
                ch.seg(blk, "gpsimd", lambda g, oc=oc: [
                    g.dma_start(out=wsb[:],
                                in_=bass.AP(pw.tensor, oc * 128,
                                            [[3 * D, 128], [128 * 3 * D, 8],
                                             [1, 128]]))])

                def mseg(t, oc=oc):
                    for ni in range(4):
                        for kc in range(8):
                            r = t.matmul(psm[ni][:],
                                         wsb[:, kc * 128:(kc + 1) * 128],
                                         ub[kc][:, ni * 512:(ni + 1) * 512],
                                         start=(kc == 0), stop=(kc == 7))
                    return r
                ch.seg(blk, "tensor", mseg)
                if oc < 8:
                    def ev(s, oc=oc):
                        for ni in range(4):
                            r = s.activation(acc[:, ni * 512:(ni + 1) * 512],
                                             psm[ni][:], AF.Sigmoid,
                                             bias=pbb[:, oc:oc + 1])
                        return r
                    ch.seg(blk, "scalar", ev)
                    ch.seg(blk, "gpsimd", lambda g, oc=oc: [
                        g.dma_start(out=o_a[oc * 128:(oc + 1) * 128, :], in_=acc[:])])
                elif oc < 16:
                    def ev(s, oc=oc):
                        for ni in range(4):
                            r = s.activation(acc[:, ni * 512:(ni + 1) * 512],
                                             psm[ni][:], AF.Tanh,
                                             bias=pbb[:, oc:oc + 1])
                        return r
                    ch.seg(blk, "scalar", ev)
                    ch.seg(blk, "vector",
                           lambda v, oc=oc: v.tensor_tensor(acc[:], acc[:],
                                                            uf[oc - 8][:],
                                                            op=ALU.mult))
                    ch.seg(blk, "gpsimd", lambda g, oc=oc: [
                        g.dma_start(out=o_w[(oc - 8) * 128:(oc - 7) * 128, :],
                                    in_=acc[:])])
                else:
                    def ev(s, oc=oc):
                        for ni in range(4):
                            r = s.activation(accb[:, ni * 512:(ni + 1) * 512],
                                             psm[ni][:], AF.Sigmoid,
                                             bias=pbb[:, oc:oc + 1])
                        return r
                    ch.seg(blk, "scalar", ev)
                    ch.seg(blk, "gpsimd", lambda g, oc=oc: [
                        g.dma_start(out=o_c[(oc - 16) * 128:(oc - 15) * 128, :],
                                    in_=accb[:])])
            ch.final_wait(blk)
    return nc


def build_L4():
    CH = 512
    nc = bass.Bass("TRN2", target_bir_lowering=False)
    a = nc.dram_tensor("a", [CH, T], F32, kind="ExternalInput").ap()
    w = nc.dram_tensor("w", [CH, T], F32, kind="ExternalInput").ap()
    c = nc.dram_tensor("c", [CH, T], BF16, kind="ExternalInput").ap()
    u = nc.dram_tensor("u", [CH, T], F32, kind="ExternalInput").ap()
    g_ = nc.dram_tensor("g", [CH, T], BF16, kind="ExternalInput").ap()
    skp = nc.dram_tensor("skp", [128, 4], F32, kind="ExternalInput").ap()
    o_y = nc.dram_tensor("o_y", [CH, T], BF16, kind="ExternalOutput").ap()
    with contextlib.ExitStack() as es:
        ab = es.enter_context(nc.sbuf_tensor("ab", [128, T], F32))
        wb = es.enter_context(nc.sbuf_tensor("wb", [128, T], F32))
        cb = es.enter_context(nc.sbuf_tensor("cb", [128, T], F32))
        ub = es.enter_context(nc.sbuf_tensor("ub", [128, T], F32))
        gb = es.enter_context(nc.sbuf_tensor("gb", [128, T], F32))
        sk = es.enter_context(nc.sbuf_tensor("sk", [128, 4], F32))
        st = es.enter_context(nc.sbuf_tensor("st", [128, T], F32))
        yb = es.enter_context(nc.sbuf_tensor("yb", [128, T], BF16))
        ch = Chain(nc, es)
        with nc.Block() as blk:
            ch.seg(blk, "gpsimd", lambda g: [
                g.dma_start(out=sk[:], in_=skp)])
            for grp in range(4):
                sl = slice(grp * 128, (grp + 1) * 128)
                ch.seg(blk, "gpsimd", lambda g, sl=sl: [
                    g.dma_start(out=ab[:], in_=a[sl, :]),
                    g.dma_start(out=wb[:], in_=w[sl, :]),
                    g.dma_start(out=cb[:], in_=c[sl, :]),
                    g.dma_start(out=ub[:], in_=u[sl, :]),
                    g.dma_start(out=gb[:], in_=g_[sl, :])])

                def scan(v, grp=grp):
                    v.tensor_tensor_scan(st[:], ab[:], wb[:], 0.0,
                                         op0=ALU.mult, op1=ALU.add)
                    v.tensor_tensor(st[:], st[:], cb[:], op=ALU.mult)
                    v.tensor_scalar(ub[:], ub[:], sk[:, grp:grp + 1], None,
                                    op0=ALU.mult)
                    v.tensor_tensor(st[:], st[:], ub[:], op=ALU.add)
                    return v.tensor_tensor(yb[:], st[:], gb[:], op=ALU.mult)
                ch.seg(blk, "vector", scan)
                ch.seg(blk, "gpsimd", lambda g, sl=sl: [
                    g.dma_start(out=o_y[sl, :], in_=yb[:])])
            ch.final_wait(blk)
    return nc


def build_L5():
    nc = bass.Bass("TRN2", target_bir_lowering=False)
    y = nc.dram_tensor("y", [D, TPC], BF16, kind="ExternalInput").ap()
    ow = nc.dram_tensor("ow", [D, D], F32, kind="ExternalInput").ap()
    ob = nc.dram_tensor("ob", [128, 8], F32, kind="ExternalInput").ap()
    x = nc.dram_tensor("x", [TPC, D], F32, kind="ExternalInput").ap()
    ln = nc.dram_tensor("ln", [TPC, D], F32, kind="ExternalInput").ap()
    ident = nc.dram_tensor("ident", [128, 128], BF16, kind="ExternalInput").ap()
    o_x1 = nc.dram_tensor("o_x1", [TPC, D], F32, kind="ExternalOutput").ap()
    o_ln2 = nc.dram_tensor("o_ln2", [TPC, D], BF16, kind="ExternalOutput").ap()
    with contextlib.ExitStack() as es:
        yb = [es.enter_context(nc.sbuf_tensor(f"yb{k}", [128, TPC], BF16))
              for k in range(8)]
        wsb = es.enter_context(nc.sbuf_tensor("wsb", [128, 8 * 128], BF16))
        obb = es.enter_context(nc.sbuf_tensor("obb", [128, 8], F32))
        idb = es.enter_context(nc.sbuf_tensor("idb", [128, 128], BF16))
        dT = es.enter_context(nc.sbuf_tensor("dT", [128, TPC], BF16))
        dtok = es.enter_context(nc.sbuf_tensor("dtok", [128, NTT * D], F32))
        xt = es.enter_context(nc.sbuf_tensor("xt", [128, NTT * D], F32))
        lns = es.enter_context(nc.sbuf_tensor("lns", [128, 4 * D], F32))
        m = es.enter_context(nc.sbuf_tensor("m", [128, NTT], F32))
        vv = es.enter_context(nc.sbuf_tensor("vv", [128, NTT], F32))
        sq = es.enter_context(nc.sbuf_tensor("sq", [128, D], F32))
        lnb = es.enter_context(nc.sbuf_tensor("lnb", [128, D], BF16))
        psm = [es.enter_context(nc.psum_tensor(f"psm{k}", [128, 512], F32))
               for k in range(4)]
        pst = [es.enter_context(nc.psum_tensor(f"pst{k}", [128, 128], BF16))
               for k in range(4)]
        ch = Chain(nc, es)
        with nc.Block() as blk:
            ch.seg(blk, "gpsimd", lambda g: [
                g.dma_start(out=yb[k][:], in_=y[k * 128:(k + 1) * 128, :])
                for k in range(8)] + [
                g.dma_start(out=obb[:], in_=ob),
                g.dma_start(out=idb[:], in_=ident),
                g.dma_start(out=xt[:].rearrange("p (n c) -> p n c", c=D), in_=x.rearrange("(n p) c -> p n c", p=128))])
            for r0 in range(0, NTT, 4):
                ch.seg(blk, "gpsimd", lambda g, r0=r0: [
                    g.dma_start(out=lns[:].rearrange("p (n c) -> p n c", c=D),
                                in_=ln[r0 * 128:(r0 + 4) * 128, :]
                                .rearrange("(n p) c -> p n c", p=128))])

                def addln(v, r0=r0):
                    for j in range(4):
                        v.tensor_tensor(xt[:, (r0 + j) * D:(r0 + j + 1) * D],
                                        xt[:, (r0 + j) * D:(r0 + j + 1) * D],
                                        lns[:, j * D:(j + 1) * D], op=ALU.add)
                ch.seg(blk, "vector", addln)
            for oc in range(8):
                ch.seg(blk, "gpsimd", lambda g, oc=oc: [
                    g.dma_start(out=wsb[:],
                                in_=bass.AP(ow.tensor, oc * 128,
                                            [[D, 128], [128 * D, 8], [1, 128]]))])

                def mseg(t, oc=oc):
                    for ni in range(4):
                        for kc in range(8):
                            r = t.matmul(psm[ni][:],
                                         wsb[:, kc * 128:(kc + 1) * 128],
                                         yb[kc][:, ni * 512:(ni + 1) * 512],
                                         start=(kc == 0), stop=(kc == 7))
                    return r
                ch.seg(blk, "tensor", mseg)

                def ev(v, oc=oc):
                    for ni in range(4):
                        r = v.tensor_scalar(dT[:, ni * 512:(ni + 1) * 512],
                                            psm[ni][:], obb[:, oc:oc + 1], None,
                                            op0=ALU.add)
                    return r
                ch.seg(blk, "vector", ev)
                items = [((lambda n=n: dT[:, n * 128:(n + 1) * 128]),
                          (lambda n=n, oc=oc:
                           dtok[:, n * D + oc * 128:n * D + oc * 128 + 128]))
                         for n in range(NTT)]
                _tr_wave(ch, blk, items, pst, idb)
            _ln_block(ch, blk, (xt, m, vv, sq, lnb), True, o_x1, o_ln2,
                      None, [lambda i: dtok[:, i * D:(i + 1) * D]])
            ch.final_wait(blk)
    return nc


def build_L6():
    nc = bass.Bass("TRN2", target_bir_lowering=False)
    ln2 = nc.dram_tensor("ln2", [TPC, D], BF16, kind="ExternalInput").ap()
    mk = nc.dram_tensor("mk", [4, 64, 16, 16], F32, kind="ExternalInput").ap()
    up = nc.dram_tensor("up", [4, 64, 64, 16], F32, kind="ExternalInput").ap()
    wc = nc.dram_tensor("wc", [D, 16 * D], F32, kind="ExternalInput").ap()
    msk = nc.dram_tensor("msk", [128, 128], BF16, kind="ExternalInput").ap()
    ident = nc.dram_tensor("ident", [128, 128], BF16, kind="ExternalInput").ap()
    x1 = nc.dram_tensor("x1", [TPC, D], F32, kind="ExternalInput").ap()
    o_x2 = nc.dram_tensor("o_x2", [TPC, D], F32, kind="ExternalOutput").ap()
    o_ln3 = nc.dram_tensor("o_ln3", [TPC, D], BF16, kind="ExternalOutput").ap()
    chbm = nc.dram_tensor("chbm", [4 * 8 * 8 * 16 * 64], BF16)
    with contextlib.ExitStack() as es:
        mskb = es.enter_context(nc.sbuf_tensor("mskb", [128, 128], BF16))
        idb = es.enter_context(nc.sbuf_tensor("idb", [128, 128], BF16))
        d1 = [es.enter_context(nc.sbuf_tensor(f"d1_{k}", [128, TPC], BF16))
              for k in range(8)]
        psm = [es.enter_context(nc.psum_tensor(f"psm{k}", [128, 512], F32))
               for k in range(4)]
        pstb = es.enter_context(nc.psum_tensor("pstb", [128, 512], BF16))
        pst = _psum_views(pstb, 4, 128)
        es_mid = es.enter_context(contextlib.ExitStack())
        lT = [es_mid.enter_context(nc.sbuf_tensor(f"lT{k}", [128, TPC], BF16))
              for k in range(8)]
        bd = es_mid.enter_context(nc.sbuf_tensor("bd", [128, 128 * 128], BF16))
        ch = Chain(nc, es)
        with nc.Block() as blk:
            ch.seg(blk, "gpsimd", lambda g: [
                g.dma_start(out=mskb[:], in_=msk),
                g.dma_start(out=idb[:], in_=ident)])
            for kc in range(8):
                ch.seg(blk, "sync", lambda sy, kc=kc: [
                    sy.dma_start_transpose(lT[kc][:],
                                           ln2[:, kc * 128:(kc + 1) * 128])])
            with contextlib.ExitStack() as es2:
                mkb = es2.enter_context(nc.sbuf_tensor("mkb", [128, 32 * 16], BF16))
                upb = es2.enter_context(nc.sbuf_tensor("upb", [128, 128 * 16], BF16))
                mkT = es2.enter_context(nc.sbuf_tensor("mkT", [16, 32 * 128], BF16))
                upT = es2.enter_context(nc.sbuf_tensor("upT", [16, 128 * 128], BF16))
                cst = es2.enter_context(nc.sbuf_tensor("cst", [16, 32 * 512], BF16))
                csb = es2.enter_context(nc.sbuf_tensor("csb", [128, 64], BF16))
                psc = es2.enter_context(nc.psum_tensor("psc", [16, 512], F32))
                ch.seg(blk, "gpsimd", lambda g: [
                    g.dma_start(out=mkb[:, i * 16:(i + 1) * 16],
                                in_=bass.AP(mk.tensor, i * 2048,
                                            [[16, 128], [1, 16]]))
                    for i in range(32)] + [
                    g.dma_start(out=upb[:, i * 16:(i + 1) * 16],
                                in_=bass.AP(up.tensor, i * 2048,
                                            [[16, 128], [1, 16]]))
                    for i in range(128)])
                items = []
                for i in range(32):
                    items.append((
                        (lambda i=i: mkb[:, i * 16:(i + 1) * 16]),
                        (lambda i=i: mkT[0:16, i * 128:(i + 1) * 128]),
                        True))
                for i in range(128):
                    items.append((
                        (lambda i=i: upb[:, i * 16:(i + 1) * 16]),
                        (lambda i=i: upT[0:16, i * 128:(i + 1) * 128]),
                        True))
                W = len(pst)
                for w0 in range(0, len(items), W):
                    wave = items[w0:w0 + W]

                    def tseg(t, wave=wave):
                        for j, (srcf, _, _) in enumerate(wave):
                            t.transpose(pst[j][0:16, 0:128], srcf(), idb[:])
                    ch.seg(blk, "tensor", tseg)

                    def vseg(v, wave=wave):
                        for j, (_, dstf, _) in enumerate(wave):
                            v.tensor_copy(dstf(), pst[j][0:16, 0:128])
                    ch.seg(blk, "vector", vseg)
                for hg in range(32):
                    def cm(t, hg=hg):
                        for rr in range(8):
                            r8 = hg * 8 + rr
                            t.matmul(psc[0:16, rr * 64:(rr + 1) * 64],
                                     mkT[0:16, r8 * 128 // 8 + 0:0] if False else
                                     mkT[0:16, hg * 128 + rr * 16:
                                         hg * 128 + rr * 16 + 16],
                                     upT[0:16, (hg * 8 + rr) * 128 + 0:
                                         (hg * 8 + rr) * 128 + 0] if False else
                                     upT[0:16, (hg * 8 + rr) * 128 // 2 + 0:0]
                                     if False else
                                     upT[0:16, hg * 512 + rr * 64:
                                         hg * 512 + rr * 64 + 64],
                                     start=True, stop=True)
                    ch.seg(blk, "tensor", cm)
                    ch.seg(blk, "vector",
                           lambda v, hg=hg: v.tensor_copy(
                               cst[:, hg * 512:(hg + 1) * 512], psc[0:16, :]))
                ch.seg(blk, "gpsimd", lambda g: [
                    g.dma_start(
                        out=bass.AP(chbm, hg * 8192,
                                    [[64, 16], [1024, 8], [1, 64]]),
                        in_=cst[:, hg * 512:(hg + 1) * 512]
                        .rearrange("p (a f) -> p a f", a=8))
                    for hg in range(32)])
                for hg in range(32):
                    ch.seg(blk, "gpsimd", lambda g, hg=hg: [
                        g.dma_start(out=csb[:],
                                    in_=bass.AP(chbm, hg * 8192,
                                                [[64, 128], [1, 64]]))])

                    def bdm(v, hg=hg):
                        for cc in range(4):
                            sl = bd[:, (hg * 4 + cc) * 128:
                                    (hg * 4 + cc) * 128 + 128]
                            v.tensor_tensor(
                                sl.rearrange("p (a f) -> p a f", a=8),
                                bass.AP(csb, 16 * cc,
                                        [[64, 128], [0, 8], [1, 16]]),
                                bass.AP(mskb, 0, [[128, 128], [16, 8], [1, 16]]),
                                op=ALU.mult)
                    ch.seg(blk, "vector", bdm)
            with contextlib.ExitStack() as es3:
                hz = [es3.enter_context(nc.sbuf_tensor(f"hz{k}", [128, TPC], BF16))
                      for k in range(16)]
                wblk = [es3.enter_context(nc.sbuf_tensor(f"wblk{j}", [128, 2048],
                                                         BF16)) for j in range(2)]
                wT = es3.enter_context(nc.sbuf_tensor("wT", [128, 16 * 128], BF16))
                wcc = es3.enter_context(nc.sbuf_tensor("wcc", [128, 2 * 128], BF16))
                scr4 = es3.enter_context(nc.sbuf_tensor("scr4", [128, 4 * 512],
                                                        BF16))
                pc = [es3.enter_context(nc.psum_tensor(f"pc{j}", [128, 512], F32))
                      for j in range(2)]
                pt2 = es3.enter_context(nc.psum_tensor("pt2", [128, 128], BF16))
                pt = [pst[0], pt2]
                s3t = es3.enter_context(nc.semaphore("s3t"))
                s3v = es3.enter_context(nc.semaphore("s3v"))
                s3a = es3.enter_context(nc.semaphore("s3a"))
                s3d = es3.enter_context(nc.semaphore("s3d"))
                s3w = es3.enter_context(nc.semaphore("s3w"))
                entry = {k: v for k, v in ch.cnt.items() if v}

                def te_z(it2, q):
                    return it2 * 112 + q + 1

                def te_wt(it2, oc, w):
                    return it2 * 112 + 16 + oc * 12 + w + 1

                def te_cp(it2, oc, ni):
                    return it2 * 112 + 16 + oc * 12 + 8 + ni + 1

                def ve_wg(it2, oc, w):
                    return it2 * 160 + oc * 20 + 2 * w + 1

                def ve_we(it2, oc, w):
                    return it2 * 160 + oc * 20 + 2 * w + 2

                def ve_acc(it2, oc, ni):
                    return it2 * 160 + oc * 20 + 16 + ni + 1

                def a_g(it2, q):
                    return it2 * 16 + q + 1

                D0 = ch.cnt["dma"]

                def s3_gp(g):
                    for k, v in entry.items():
                        g.wait_ge(ch.sem[k], v)
                    for it2 in range(8):
                        h, gh = it2 // 2, it2 % 2
                        for oc in range(8):
                            if it2 * 8 + oc >= 2:
                                po = it2 * 8 + oc - 2
                                g.wait_ge(s3v, (po // 8) * 160
                                          + (po % 8) * 20 + 15)
                            g.dma_start(out=wblk[oc % 2][:],
                                        in_=bass.AP(wc.tensor,
                                                    oc * 128 * 16384
                                                    + h * 4096 + gh * 2048,
                                                    [[16384, 128], [1, 2048]])
                                        ).then_inc(s3d, 16)
                    g.wait_ge(s3d, 64 * 16)
                blk.gpsimd(s3_gp)

                def s3_te(t):
                    for k, v in entry.items():
                        t.wait_ge(ch.sem[k], v)
                    for it2 in range(8):
                        h, gh = it2 // 2, it2 % 2
                        for q in range(16):
                            gi, cc = q // 4, q % 4
                            g2 = gh * 4 + gi
                            if q > 0:
                                t.wait_ge(s3a, a_g(it2, q - 1))
                            elif it2 > 0:
                                t.wait_ge(s3a, a_g(it2 - 1, 15))
                            for ni in range(4):
                                r = t.matmul(
                                    psm[ni][:],
                                    bd[:, ((h * 8 + g2) * 4 + cc) * 128:
                                       ((h * 8 + g2) * 4 + cc) * 128 + 128],
                                    lT[g2][:, ni * 512:(ni + 1) * 512],
                                    start=True, stop=True)
                            r.then_inc(s3t, 1)
                        for oc in range(8):
                            for w in range(8):
                                t.wait_ge(s3v, ve_wg(it2, oc, w))
                                if w >= 1:
                                    t.wait_ge(s3v, ve_we(it2, oc, w - 1))
                                elif oc > 0:
                                    t.wait_ge(s3v, ve_we(it2, oc - 1, 7))
                                elif it2 > 0:
                                    t.wait_ge(s3v, ve_we(it2 - 1, 7, 7))
                                for j in range(2):
                                    r = t.transpose(pt[j][:],
                                                    wcc[:, j * 128:(j + 1) * 128],
                                                    idb[:])
                                r.then_inc(s3t, 1)
                            for ni in range(4):
                                if ni == 0:
                                    t.wait_ge(s3a, a_g(it2, 15))
                                    t.wait_ge(s3v, ve_we(it2, oc, 7))
                                b = ni % 2
                                if oc > 0:
                                    t.wait_ge(s3v, ve_acc(it2, oc - 1, b + 2))
                                elif it2 > 0:
                                    t.wait_ge(s3v, ve_acc(it2 - 1, 7, b + 2))
                                if ni >= 2:
                                    t.wait_ge(s3v, ve_acc(it2, oc, ni - 2))
                                for q in range(16):
                                    r = t.matmul(pc[b][:],
                                                 wT[:, q * 128:(q + 1) * 128],
                                                 hz[q][:, ni * 512:(ni + 1) * 512],
                                                 start=(q == 0), stop=(q == 15))
                                r.then_inc(s3t, 1)
                blk.tensor(s3_te)

                def s3_act(s):
                    for k, v in entry.items():
                        s.wait_ge(ch.sem[k], v)
                    for it2 in range(8):
                        for q in range(16):
                            s.wait_ge(s3t, te_z(it2, q))
                            if q == 0 and it2 > 0:
                                s.wait_ge(s3t, te_cp(it2 - 1, 7, 3))
                            for ni in range(4):
                                r = s.activation(
                                    hz[q][:, ni * 512:(ni + 1) * 512],
                                    psm[ni][:], AF.Gelu)
                            r.then_inc(s3a, 1)
                blk.scalar(s3_act)

                def s3_ve(v):
                    for k, v2 in entry.items():
                        v.wait_ge(ch.sem[k], v2)
                    nvv = 0
                    for it2 in range(8):
                        for oc in range(8):
                            for w in range(8):
                                v.wait_ge(s3d, 32 * ((it2 * 8 + oc) // 2 + 1))
                                if w > 0:
                                    v.wait_ge(s3t, te_wt(it2, oc, w - 1))
                                for j in range(2):
                                    q = w * 2 + j
                                    gi, cc = q // 4, q % 4
                                    r = v.tensor_copy(
                                        wcc[:, j * 128:(j + 1) * 128]
                                        .rearrange("p (a f) -> p a f", a=8),
                                        bass.AP(wblk[oc % 2],
                                                gi * 512 + 16 * cc,
                                                [[2048, 128], [64, 8], [1, 16]]))
                                r.then_inc(s3v, 1)
                                v.wait_ge(s3t, te_wt(it2, oc, w))
                                for j in range(2):
                                    q = w * 2 + j
                                    r = v.tensor_copy(
                                        wT[:, q * 128:(q + 1) * 128], pt[j][:])
                                r.then_inc(s3v, 1)
                            for ni in range(4):
                                v.wait_ge(s3t, te_cp(it2, oc, ni))
                                b = ni % 2
                                if it2 == 0:
                                    r = v.tensor_copy(
                                        d1[oc][:, ni * 512:(ni + 1) * 512],
                                        pc[b][:])
                                else:
                                    v.tensor_copy(
                                        scr4[:, ni * 512:(ni + 1) * 512],
                                        pc[b][:]).then_inc(s3w, 1)
                                    nvv += 1
                                    v.wait_ge(s3w, nvv)
                                    r = v.tensor_tensor(
                                        d1[oc][:, ni * 512:(ni + 1) * 512],
                                        d1[oc][:, ni * 512:(ni + 1) * 512],
                                        scr4[:, ni * 512:(ni + 1) * 512],
                                        op=ALU.add)
                                r.then_inc(s3v, 1)
                blk.vector(s3_ve)

                def bridge(v):
                    v.wait_ge(s3t, 8 * 112)
                    v.wait_ge(s3v, 8 * 160)
                    v.wait_ge(s3a, 8 * 16)
                    v.wait_ge(s3d, 64 * 16)
                    v.memset(scr4[:, 0:4], 0.0).then_inc(ch.sem["vector"], 1)
                blk.vector(bridge)
                ch.cnt["vector"] += 1
            es_mid.close()
            with contextlib.ExitStack() as es4:
                dtok = es4.enter_context(nc.sbuf_tensor("dtok",
                                                        [128, NTT * D], F32))
                xt = es4.enter_context(nc.sbuf_tensor("xt", [128, NTT * D], F32))
                m = es4.enter_context(nc.sbuf_tensor("m", [128, NTT], F32))
                vv = es4.enter_context(nc.sbuf_tensor("vv", [128, NTT], F32))
                sq = es4.enter_context(nc.sbuf_tensor("sq", [128, D], F32))
                lnb = es4.enter_context(nc.sbuf_tensor("lnb", [128, D], BF16))
                ch.seg(blk, "gpsimd", lambda g: [
                    g.dma_start(out=xt[:].rearrange("p (n c) -> p n c", c=D),
                                in_=x1.rearrange("(n p) c -> p n c", p=128))])
                for oc in range(8):
                    items = [((lambda n=n, oc=oc: d1[oc][:, n * 128:(n + 1) * 128]),
                              (lambda n=n, oc=oc:
                               dtok[:, n * D + oc * 128:n * D + oc * 128 + 128]))
                             for n in range(NTT)]
                    _tr_wave(ch, blk, items, pst, idb)
                _ln_block(ch, blk, (xt, m, vv, sq, lnb), True, o_x2, o_ln3,
                          None, [lambda i: dtok[:, i * D:(i + 1) * D]])
            ch.final_wait(blk)
    return nc


def build_L7():
    NT = T // 128
    nc = bass.Bass("TRN2", target_bir_lowering=False)
    ln3 = nc.dram_tensor("ln3", [T, D], BF16, kind="ExternalInput").ap()
    wq = nc.dram_tensor("wq", [512, D], F32, kind="ExternalInput").ap()
    wk = nc.dram_tensor("wk", [512, D], F32, kind="ExternalInput").ap()
    wv = nc.dram_tensor("wv", [512, D], F32, kind="ExternalInput").ap()
    rot = nc.dram_tensor("rot", [128, 128], BF16, kind="ExternalInput").ap()
    cosb = nc.dram_tensor("cosb", [128, T], F32, kind="ExternalInput").ap()
    sinb = nc.dram_tensor("sinb", [128, T], F32, kind="ExternalInput").ap()
    ident = nc.dram_tensor("ident", [128, 128], BF16, kind="ExternalInput").ap()
    o_qr = nc.dram_tensor("o_qr", [4, 128, T], BF16, kind="ExternalOutput").ap()
    o_kr = nc.dram_tensor("o_kr", [4, 128, T], BF16, kind="ExternalOutput").ap()
    o_v = nc.dram_tensor("o_v", [T, 512], BF16, kind="ExternalOutput").ap()
    with contextlib.ExitStack() as es:
        lT = [es.enter_context(nc.sbuf_tensor(f"lT{k}", [128, T], BF16))
              for k in range(8)]
        idb = es.enter_context(nc.sbuf_tensor("idb", [128, 128], BF16))
        rotb = es.enter_context(nc.sbuf_tensor("rotb", [128, 128], BF16))
        cosbs = es.enter_context(nc.sbuf_tensor("cosbs", [128, T], F32))
        sinbs = es.enter_context(nc.sbuf_tensor("sinbs", [128, T], F32))
        wqb = es.enter_context(nc.sbuf_tensor("wqb", [128, D], BF16))
        wqT = es.enter_context(nc.sbuf_tensor("wqT", [128, 8 * 128], BF16))
        qT = es.enter_context(nc.sbuf_tensor("qT", [128, T], BF16))
        rq = es.enter_context(nc.sbuf_tensor("rq", [128, T], BF16))
        t1 = es.enter_context(nc.sbuf_tensor("t1", [128, T], BF16))
        wvT = [es.enter_context(nc.sbuf_tensor(f"wvT{k}", [128, 512], BF16))
               for k in range(8)]
        vtk4 = es.enter_context(nc.sbuf_tensor("vtk4", [128, 4 * 512], BF16))
        psm = [es.enter_context(nc.psum_tensor(f"psm{k}", [128, 512], F32))
               for k in range(4)]
        pst = [es.enter_context(nc.psum_tensor(f"pst{k}", [128, 128], BF16))
               for k in range(2)]
        ch = Chain(nc, es)
        with nc.Block() as blk:
            for kc in range(8):
                ch.seg(blk, "sync", lambda sy, kc=kc: [
                    sy.dma_start_transpose(lT[kc][:],
                                           ln3[:, kc * 128:(kc + 1) * 128])])
            ch.seg(blk, "gpsimd", lambda g: [
                g.dma_start(out=idb[:], in_=ident),
                g.dma_start(out=rotb[:], in_=rot),
                g.dma_start(out=cosbs[:], in_=cosb),
                g.dma_start(out=sinbs[:], in_=sinb)])
            for wsrc, odst in ((wq, o_qr), (wk, o_kr)):
                for hp in range(4):
                    ch.seg(blk, "gpsimd", lambda g, wsrc=wsrc, hp=hp: [
                        g.dma_start(out=wqb[:],
                                    in_=wsrc[hp * 128:(hp + 1) * 128, :])])
                    items = [((lambda kc=kc: wqb[:, kc * 128:(kc + 1) * 128]),
                              (lambda kc=kc: wqT[:, kc * 128:(kc + 1) * 128]))
                             for kc in range(8)]
                    _tr_wave(ch, blk, items, pst, idb)
                    for n0 in range(0, 8, 4):
                        def mm(t, n0=n0):
                            for ni in range(4):
                                for kc in range(8):
                                    r = t.matmul(psm[ni][:],
                                                 wqT[:, kc * 128:(kc + 1) * 128],
                                                 lT[kc][:, (n0 + ni) * 512:
                                                         (n0 + ni + 1) * 512],
                                                 start=(kc == 0), stop=(kc == 7))
                            return r
                        ch.seg(blk, "tensor", mm)

                        def ev(v, n0=n0):
                            for ni in range(4):
                                r = v.tensor_copy(
                                    qT[:, (n0 + ni) * 512:(n0 + ni + 1) * 512],
                                    psm[ni][:])
                            return r
                        ch.seg(blk, "vector", ev)
                    for n0 in range(0, 8, 4):
                        def rm(t, n0=n0):
                            for ni in range(4):
                                r = t.matmul(psm[ni][:], rotb[:],
                                             qT[:, (n0 + ni) * 512:(n0 + ni + 1) * 512],
                                             start=True, stop=True)
                            return r
                        ch.seg(blk, "tensor", rm)

                        def re(v, n0=n0):
                            for ni in range(4):
                                r = v.tensor_copy(
                                    rq[:, (n0 + ni) * 512:(n0 + ni + 1) * 512],
                                    psm[ni][:])
                            return r
                        ch.seg(blk, "vector", re)

                    def rope(v):
                        v.tensor_tensor(t1[:], qT[:], cosbs[:], op=ALU.mult)
                        v.tensor_tensor(rq[:], rq[:], sinbs[:], op=ALU.mult)
                        return v.tensor_tensor(t1[:], t1[:], rq[:], op=ALU.add)
                    ch.seg(blk, "vector", rope)
                    ch.seg(blk, "gpsimd", lambda g, odst=odst, hp=hp: [
                        g.dma_start(out=odst[hp, :, :], in_=t1[:])])
            for ro in range(4):
                ch.seg(blk, "gpsimd", lambda g, ro=ro: [
                    g.dma_start(out=wqb[:], in_=wv[ro * 128:(ro + 1) * 128, :])])
                items = [((lambda kc=kc: wqb[:, kc * 128:(kc + 1) * 128]),
                          (lambda kc=kc, ro=ro: wvT[kc][:, ro * 128:(ro + 1) * 128]))
                         for kc in range(8)]
                _tr_wave(ch, blk, items, pst, idb)
            for tc0 in range(0, NT, 4):
                def vm(t, tc0=tc0):
                    for j in range(4):
                        for kc in range(8):
                            t.matmul(psm[j][:],
                                     lT[kc][:, (tc0 + j) * 128:(tc0 + j + 1) * 128],
                                     wvT[kc][:], start=(kc == 0), stop=(kc == 7))
                ch.seg(blk, "tensor", vm)

                def vev(v, tc0=tc0):
                    for j in range(4):
                        v.tensor_copy(vtk4[:, j * 512:(j + 1) * 512], psm[j][:])
                ch.seg(blk, "vector", vev)
                ch.seg(blk, "gpsimd", lambda g, tc0=tc0: [
                    g.dma_start(out=o_v[tc0 * 128:(tc0 + 4) * 128, :]
                                .rearrange("(a p) f -> p a f", p=128),
                                in_=vtk4[:].rearrange("p (a f) -> p a f", f=512))])
            ch.final_wait(blk)
    return nc


def build_L8():
    NT = T // 128
    nc = bass.Bass("TRN2", target_bir_lowering=False)
    qr = nc.dram_tensor("qr", [4, 128, T], BF16, kind="ExternalInput").ap()
    kr = nc.dram_tensor("kr", [4, 128, T], BF16, kind="ExternalInput").ap()
    vt = nc.dram_tensor("vt", [T, 512], BF16, kind="ExternalInput").ap()
    ident = nc.dram_tensor("ident", [128, 128], BF16, kind="ExternalInput").ap()
    o_d = nc.dram_tensor("o_d", [T, 512], BF16, kind="ExternalOutput").ap()
    HPD = 288
    QTE = NT + 8
    HXE = 8 * NT + 1
    with contextlib.ExitStack() as es:
        qs = es.enter_context(nc.sbuf_tensor("qs", [128, T], BF16))
        ks = es.enter_context(nc.sbuf_tensor("ks", [128, T], BF16))
        vp = [es.enter_context(nc.sbuf_tensor(f"vp{j}", [128, NT * 65], BF16))
              for j in range(2)]
        idb = es.enter_context(nc.sbuf_tensor("idb", [128, 128], BF16))
        pa = [[es.enter_context(nc.sbuf_tensor(f"pa{j}_{p}", [128, 512], BF16))
               for p in range(2)] for j in range(2)]
        ou = [es.enter_context(nc.sbuf_tensor(f"ou{j}", [128, 512], BF16))
              for j in range(2)]
        rz = es.enter_context(nc.sbuf_tensor("rz", [128, 1], F32))
        da = es.enter_context(nc.sbuf_tensor("da", [128, 4 * 128], BF16))
        psc = [es.enter_context(nc.psum_tensor(f"psc{j}", [128, 512], F32))
               for j in range(4)]
        psv = [es.enter_context(nc.psum_tensor(f"psv{j}", [128, 512], F32))
               for j in range(2)]
        pst = [es.enter_context(nc.psum_tensor(f"pstA", [128, 128], BF16)),
               es.enter_context(nc.psum_tensor(f"pstB", [128, 128], BF16))]
        s_ld = es.enter_context(nc.semaphore("s_ld"))
        s_st = es.enter_context(nc.semaphore("s_st"))
        s_qk = es.enter_context(nc.semaphore("s_qk"))
        s_ex = es.enter_context(nc.semaphore("s_ex"))
        s_pv = es.enter_context(nc.semaphore("s_pv"))
        s_ep = es.enter_context(nc.semaphore("s_ep"))
        s_vv = es.enter_context(nc.semaphore("s_vv"))
        with nc.Block() as blk:
            def gp(g):
                g.dma_start(out=idb[:], in_=ident).then_inc(s_ld, 16)
                nld = 16
                for hp in range(4):
                    if hp > 0:
                        g.wait_ge(s_pv, hp * 8 * QTE)
                    g.dma_start(out=qs[:], in_=qr[hp, :, :]).then_inc(s_ld, 16)
                    g.dma_start(out=ks[:], in_=kr[hp, :, :]).then_inc(s_ld, 16)
                    for j in range(2):
                        for c0 in range(4):
                            g.dma_start(
                                out=bass.AP(vp[j], c0 * 8 * 65,
                                            [[NT * 65, 128], [65, 8], [1, 64]]),
                                in_=bass.AP(vt.tensor,
                                            c0 * 8 * 128 * 512 + hp * 128 + j * 64,
                                            [[512, 128], [128 * 512, 8], [1, 64]])
                            ).then_inc(s_ld, 16)
                    nld += 160
                    for qt in range(8):
                        it = hp * 8 + qt
                        g.wait_ge(s_ep, it * 9 + 9)
                        g.dma_start(
                            out=bass.AP(o_d.tensor, qt * 512 * 512 + hp * 128,
                                        [[512, 128], [128 * 512, 4], [1, 128]]),
                            in_=da[:].rearrange("p (a f) -> p a f", a=4)
                        ).then_inc(s_st, 16)
                g.wait_ge(s_ld, nld)
                g.wait_ge(s_st, 32 * 16)
            blk.gpsimd(gp)

            def ve(v):
                nvv = 0
                for hp in range(4):
                    v.wait_ge(s_ld, 16 + (hp + 1) * 160)
                    v.memset(bass.AP(vp[0], 64, [[NT * 65, 128], [65, NT]]), 1.0)
                    v.memset(bass.AP(vp[1], 64,
                                     [[NT * 65, 128], [65, NT]]),
                             1.0).then_inc(s_ex, 1)
                    for qt in range(8):
                        it = hp * 8 + qt
                        v.wait_ge(s_pv, it * QTE + NT)
                        v.tensor_copy(ou[0][0:65, :], psv[0][0:65, :])
                        v.tensor_copy(ou[1][0:65, :],
                                      psv[1][0:65, :]).then_inc(s_ep, 1)
                        if it > 0:
                            v.wait_ge(s_st, it * 16)
                        for k in range(8):
                            qsi, j = k // 2, k % 2
                            v.wait_ge(s_pv, it * QTE + NT + k + 1)
                            if k > 0:
                                v.wait_ge(s_ep, it * 9 + 1 + k)
                            v.reciprocal(rz[:],
                                         pst[k % 2][:, 64:65]).then_inc(s_vv, 1)
                            nvv += 1
                            v.wait_ge(s_vv, nvv)
                            v.tensor_scalar(
                                da[:, qsi * 128 + j * 64:
                                   qsi * 128 + j * 64 + 64],
                                pst[k % 2][:, 0:64], rz[:],
                                None, op0=ALU.mult).then_inc(s_ep, 1)
            blk.vector(ve)

            def sc(s):
                for hp in range(4):
                    H = hp * HXE + 1
                    for qt in range(8):
                        it = hp * 8 + qt
                        for c in range(NT):
                            s.wait_ge(s_qk, it * NT + c + 1)
                            if c >= 2:
                                s.wait_ge(s_pv, it * QTE + c - 1)
                            elif it > 0:
                                s.wait_ge(s_pv, (it - 1) * QTE + NT - 1 + c)
                            s.activation(pa[0][c % 2][:], psc[2 * (c % 2)][:],
                                         AF.Exp, scale=0.125)
                            s.activation(pa[1][c % 2][:],
                                         psc[2 * (c % 2) + 1][:],
                                         AF.Exp, scale=0.125).then_inc(s_ex, 1)
            blk.scalar(sc)

            def te(t):
                for hp in range(4):
                    t.wait_ge(s_ld, 16 + (hp + 1) * 160)
                    t.wait_ge(s_ex, hp * HXE + 1)
                    H = hp * HXE + 1
                    for qt in range(8):
                        it = hp * 8 + qt
                        q0 = qt * 512

                        def qk_mm(c):
                            if c >= 2 or qt > 0:
                                t.wait_ge(s_ex, H + qt * NT + c - 1)
                            t.matmul(psc[2 * (c % 2)][:],
                                     ks[0:64, c * 128:(c + 1) * 128],
                                     qs[0:64, q0:q0 + 512],
                                     start=True, stop=True, tile_position=(0, 0))
                            t.matmul(psc[2 * (c % 2) + 1][:],
                                     ks[64:128, c * 128:(c + 1) * 128],
                                     qs[64:128, q0:q0 + 512],
                                     start=True, stop=True,
                                     tile_position=(64, 0)).then_inc(s_qk, 1)

                        def pv_mm(c):
                            t.wait_ge(s_ex, H + qt * NT + c + 1)
                            if c == 0 and it > 0:
                                t.wait_ge(s_ep, (it - 1) * 9 + 1)
                            t.matmul(psv[0][0:65, :],
                                     vp[0][:, c * 65:(c + 1) * 65],
                                     pa[0][c % 2][:],
                                     start=(c == 0), stop=(c == NT - 1))
                            t.matmul(psv[1][0:65, :],
                                     vp[1][:, c * 65:(c + 1) * 65],
                                     pa[1][c % 2][:],
                                     start=(c == 0),
                                     stop=(c == NT - 1)).then_inc(s_pv, 1)
                        qk_mm(0)
                        for c in range(1, NT):
                            qk_mm(c)
                            pv_mm(c - 1)
                        pv_mm(NT - 1)
                        t.wait_ge(s_ep, it * 9 + 1)
                        for k in range(8):
                            qsi, j = k // 2, k % 2
                            if k >= 2:
                                t.wait_ge(s_ep, it * 9 + k)
                            elif it > 0:
                                t.wait_ge(s_ep, (it - 1) * 9 + k + 8)
                            t.transpose(pst[k % 2][:, 0:65],
                                        ou[j][0:65, qsi * 128:(qsi + 1) * 128],
                                        idb[0:65, 0:65]).then_inc(s_pv, 1)
            blk.tensor(te)
    return nc


def build_L9():
    nc = bass.Bass("TRN2", target_bir_lowering=False)
    x2 = nc.dram_tensor("x2", [TPC, D], F32, kind="ExternalInput").ap()
    dat = nc.dram_tensor("dat", [TPC, D], BF16, kind="ExternalInput").ap()
    o_x3 = nc.dram_tensor("o_x3", [TPC, D], F32, kind="ExternalOutput").ap()
    o_ln4 = nc.dram_tensor("o_ln4", [TPC, D], BF16, kind="ExternalOutput").ap()
    with contextlib.ExitStack() as es:
        xt = es.enter_context(nc.sbuf_tensor("xt", [128, NTT * D], F32))
        db = es.enter_context(nc.sbuf_tensor("db", [128, NTT * D], F32))
        m = es.enter_context(nc.sbuf_tensor("m", [128, NTT], F32))
        vv = es.enter_context(nc.sbuf_tensor("vv", [128, NTT], F32))
        sq = es.enter_context(nc.sbuf_tensor("sq", [128, D], F32))
        lnb = es.enter_context(nc.sbuf_tensor("lnb", [128, D], BF16))
        ch = Chain(nc, es)
        with nc.Block() as blk:
            ch.seg(blk, "gpsimd", lambda g: [
                g.dma_start(out=xt[:].rearrange("p (n c) -> p n c", c=D), in_=x2.rearrange("(n p) c -> p n c", p=128)),
                g.dma_start(out=db[:].rearrange("p (n c) -> p n c", c=D), in_=dat.rearrange("(n p) c -> p n c", p=128))])
            _ln_block(ch, blk, (xt, m, vv, sq, lnb), True, o_x3, o_ln4,
                      None, [lambda i: db[:, i * D:(i + 1) * D]])
            ch.final_wait(blk)
    return nc


def build_L10():
    NA = 2048
    NAT = NA // 128
    nc = bass.Bass("TRN2", target_bir_lowering=False)
    ln4 = nc.dram_tensor("ln4", [NA, D], BF16, kind="ExternalInput").ap()
    w1 = nc.dram_tensor("w1", [D, 2048], F32, kind="ExternalInput").ap()
    b1 = nc.dram_tensor("b1", [128, 2048], F32, kind="ExternalInput").ap()
    w2 = nc.dram_tensor("w2", [D, D], F32, kind="ExternalInput").ap()
    b2 = nc.dram_tensor("b2", [128, 8], F32, kind="ExternalInput").ap()
    ri = nc.dram_tensor("ri", [NA, 2], I32, kind="ExternalInput").ap()
    rw = nc.dram_tensor("rw", [128, 16], F32, kind="ExternalInput").ap()
    ev_ = nc.dram_tensor("ev", [128, 1], F32, kind="ExternalInput").ap()
    ident = nc.dram_tensor("ident", [128, 128], BF16, kind="ExternalInput").ap()
    o_p = nc.dram_tensor("o_p", [NA, D], BF16, kind="ExternalOutput").ap()
    with contextlib.ExitStack() as es:
        lT = [es.enter_context(nc.sbuf_tensor(f"lT{k}", [128, NA], BF16))
              for k in range(8)]
        idb = es.enter_context(nc.sbuf_tensor("idb", [128, 128], BF16))
        w1b = [es.enter_context(nc.sbuf_tensor(f"w1b{k}", [128, 2048], BF16))
               for k in range(8)]
        b1b = es.enter_context(nc.sbuf_tensor("b1b", [128, 2048], F32))
        w2b = [es.enter_context(nc.sbuf_tensor(f"w2b{k}", [128, D], BF16))
               for k in range(8)]
        b2b = es.enter_context(nc.sbuf_tensor("b2b", [128, 8], F32))
        idxb = es.enter_context(nc.sbuf_tensor("idxb", [128, 32], I32))
        idxf = es.enter_context(nc.sbuf_tensor("idxf", [128, 32], F32))
        evb = es.enter_context(nc.sbuf_tensor("evb", [128, 1], F32))
        rwb = es.enter_context(nc.sbuf_tensor("rwb", [128, 16], F32))
        wcol = es.enter_context(nc.sbuf_tensor("wcol", [128, 16], F32))
        eq = es.enter_context(nc.sbuf_tensor("eq", [128, 32], F32))
        gc = es.enter_context(nc.sbuf_tensor("gc", [128, 1024], F32))
        sg = es.enter_context(nc.sbuf_tensor("sg", [128, 1024], F32))
        uc = es.enter_context(nc.sbuf_tensor("uc", [128, 1024], F32))
        inter = es.enter_context(nc.sbuf_tensor("inter", [128, D], BF16))
        iT = [es.enter_context(nc.sbuf_tensor(f"iT{k}", [128, NA], BF16))
              for k in range(8)]
        dTo = [es.enter_context(nc.sbuf_tensor(f"dTo{k}", [128, NA], BF16))
               for k in range(8)]
        dtok = es.enter_context(nc.sbuf_tensor("dtok", [128, D], BF16))
        psm = [es.enter_context(nc.psum_tensor(f"psm{k}", [128, 512], F32))
               for k in range(4)]
        pst = [es.enter_context(nc.psum_tensor(f"pst{k}", [128, 128], BF16))
               for k in range(2)]
        ch = Chain(nc, es)
        with nc.Block() as blk:
            for kc in range(8):
                ch.seg(blk, "sync", lambda sy, kc=kc: [
                    sy.dma_start_transpose(lT[kc][:],
                                           ln4[:, kc * 128:(kc + 1) * 128])])
            ch.seg(blk, "gpsimd", lambda g: [
                g.dma_start(out=idb[:], in_=ident),
                g.dma_start(out=b1b[:], in_=b1),
                g.dma_start(out=b2b[:], in_=b2),
                g.dma_start(out=evb[:], in_=ev_),
                g.dma_start(out=idxb[:].rearrange("p (c k) -> p c k", k=2),
                            in_=ri.rearrange("(c p) k -> p c k", p=128)),
                g.dma_start(out=rwb[:], in_=rw)] + [
                g.dma_start(out=w1b[k][:], in_=w1[k * 128:(k + 1) * 128, :])
                for k in range(8)] + [
                g.dma_start(out=w2b[k][:], in_=w2[k * 128:(k + 1) * 128, :])
                for k in range(8)])

            def wcol_f(v):
                v.tensor_copy(idxf[:], idxb[:])
                v.tensor_scalar(eq[:, 0:16],
                                bass.AP(idxf, 0, [[32, 128], [2, 16]]),
                                evb[:], None, op0=ALU.is_equal)
                v.tensor_scalar(eq[:, 16:32],
                                bass.AP(idxf, 1, [[32, 128], [2, 16]]),
                                evb[:], None, op0=ALU.is_equal)
                v.tensor_tensor(wcol[:], eq[:, 0:16], eq[:, 16:32], op=ALU.add)
                return v.tensor_tensor(wcol[:], wcol[:], rwb[:], op=ALU.mult)
            ch.seg(blk, "vector", wcol_f)
            for tc in range(NAT):
                def mm(t, tc=tc):
                    for o4 in range(4):
                        for kc in range(8):
                            t.matmul(psm[o4][:],
                                     lT[kc][:, tc * 128:(tc + 1) * 128],
                                     w1b[kc][:, o4 * 512:(o4 + 1) * 512],
                                     start=(kc == 0), stop=(kc == 7))
                ch.seg(blk, "tensor", mm)

                def swi(v):
                    for o4 in range(4):
                        v.tensor_tensor(psm[o4][:], psm[o4][:],
                                        b1b[:, o4 * 512:(o4 + 1) * 512], op=ALU.add)
                        v.tensor_scalar(gc[:, o4 * 256:(o4 + 1) * 256],
                                        bass.AP(psm[o4], 0, [[512, 128], [2, 256]]),
                                        7.0, None, op0=ALU.min)
                        v.tensor_scalar(uc[:, o4 * 256:(o4 + 1) * 256],
                                        bass.AP(psm[o4], 1, [[512, 128], [2, 256]]),
                                        7.0, -7.0, op0=ALU.min, op1=ALU.max)
                ch.seg(blk, "vector", swi)
                ch.seg(blk, "scalar",
                       lambda s: s.activation(sg[:], gc[:], AF.Sigmoid,
                                              scale=1.702))

                def glu(v):
                    v.tensor_tensor(gc[:], gc[:], sg[:], op=ALU.mult)
                    v.scalar_tensor_tensor(inter[:], uc[:], 1.0, gc[:],
                                           op0=ALU.add, op1=ALU.mult)
                ch.seg(blk, "vector", glu)
                items = [((lambda ic=ic: inter[:, ic * 128:(ic + 1) * 128]),
                          (lambda ic=ic, tc=tc: iT[ic][:, tc * 128:(tc + 1) * 128]))
                         for ic in range(8)]
                _tr_wave(ch, blk, items, pst, idb)
            for oc in range(8):
                def dm(t, oc=oc):
                    for ni in range(4):
                        for kc in range(8):
                            r = t.matmul(psm[ni][:],
                                         w2b[kc][:, oc * 128:(oc + 1) * 128],
                                         iT[kc][:, ni * 512:(ni + 1) * 512],
                                         start=(kc == 0), stop=(kc == 7))
                    return r
                ch.seg(blk, "tensor", dm)

                def ev2(v, oc=oc):
                    for ni in range(4):
                        r = v.tensor_scalar(dTo[oc][:, ni * 512:(ni + 1) * 512],
                                            psm[ni][:], b2b[:, oc:oc + 1], None,
                                            op0=ALU.add)
                    return r
                ch.seg(blk, "vector", ev2)
            for tc in range(NAT):
                items = [((lambda oc=oc, tc=tc: dTo[oc][:, tc * 128:(tc + 1) * 128]),
                          (lambda oc=oc: dtok[:, oc * 128:(oc + 1) * 128]))
                         for oc in range(8)]
                _tr_wave(ch, blk, items, pst, idb)

                def wmul(v, tc=tc):
                    return v.tensor_scalar(dtok[:], dtok[:], wcol[:, tc:tc + 1],
                                           None, op0=ALU.mult)
                ch.seg(blk, "vector", wmul)
                ch.seg(blk, "gpsimd", lambda g, tc=tc: [
                    g.dma_start(out=o_p[tc * 128:(tc + 1) * 128, :], in_=dtok[:])])
            ch.final_wait(blk)
    return nc


def build_L11():
    R = 2048 // NC
    nc = bass.Bass("TRN2", target_bir_lowering=False)
    x3 = nc.dram_tensor("x3", [R, D], F32, kind="ExternalInput").ap()
    ps = [nc.dram_tensor(f"p{e}", [R, D], BF16, kind="ExternalInput").ap()
          for e in range(8)]
    o = nc.dram_tensor("o", [R, D], F32, kind="ExternalOutput").ap()
    with contextlib.ExitStack() as es:
        xb = es.enter_context(nc.sbuf_tensor("xb", [128, (R // 128) * D], F32))
        pb = es.enter_context(nc.sbuf_tensor("pb", [128, (R // 128) * D], F32))
        ch = Chain(nc, es)
        with nc.Block() as blk:
            ch.seg(blk, "gpsimd", lambda g: [
                g.dma_start(out=xb[:].rearrange("p (n c) -> p n c", c=D), in_=x3.rearrange("(n p) c -> p n c", p=128))])
            for e in range(8):
                ch.seg(blk, "gpsimd", lambda g, e=e: [
                    g.dma_start(out=pb[:].rearrange("p (n c) -> p n c", c=D),
                                in_=ps[e].rearrange("(n p) c -> p n c", p=128))])
                ch.seg(blk, "vector",
                       lambda v: v.tensor_tensor(xb[:], xb[:], pb[:], op=ALU.add))
            ch.seg(blk, "gpsimd", lambda g: [
                g.dma_start(out=o.rearrange("(n p) c -> p n c", p=128), in_=xb[:].rearrange("p (n c) -> p n c", c=D))])
            ch.final_wait(blk)
    return nc


def _cat(outs, name, axis):
    return np.concatenate([o[name] for o in outs], axis=axis)


_BUILD_CACHE = {}


def _run(key, build_fn, in_maps):
    if key not in _BUILD_CACHE:
        _BUILD_CACHE[key] = build_fn()
    return run_bass_kernel_spmd(_BUILD_CACHE[key], in_maps,
                                core_ids=list(range(NC))).results


def _consts():
    ident = np.eye(128, dtype=np.float32).astype(NBF)
    mask = np.zeros((128, 128), np.float32)
    for b0 in range(8):
        mask[b0 * 16:(b0 + 1) * 16, b0 * 16:(b0 + 1) * 16] = 1.0
    mask = mask.astype(NBF)
    P = np.zeros((128, 128), np.float32)
    for i in range(64):
        P[2 * i, 2 * i + 1] = -1.0
        P[2 * i + 1, 2 * i] = 1.0
    rotT = P.T.copy().astype(NBF)
    pos = np.arange(T, dtype=np.float64)
    inv = np.power(10000.0, -2.0 * np.arange(32, dtype=np.float64) / 64)
    ang = pos[:, None] * inv[None, :]
    cos = np.repeat(np.cos(ang), 2, axis=1).T
    sin = np.repeat(np.sin(ang), 2, axis=1).T
    cosb = np.concatenate([cos, cos], 0).astype(np.float32)
    sinb = np.concatenate([sin, sin], 0).astype(np.float32)
    return ident, mask, rotT, cosb, sinb


def kernel(**inp):
    ident, mask, rotT, cosb, sinb = _consts()
    x = np.ascontiguousarray(inp["L0_attn_x"].reshape(B * T, D).astype(np.float32))

    nwrep = np.ascontiguousarray(
        np.broadcast_to(inp["L0_attn_norm_w"][None, :], (128, D)).astype(np.float32))
    o1 = _run("L1", build_L1, [
        {"x": x[c * TPC:(c + 1) * TPC], "nwrep": nwrep} for c in range(NC)])
    ln = _cat(o1, "o_ln", 0)
    xn = _cat(o1, "o_xn", 0)

    in_w = np.ascontiguousarray(inp["L0_attn_in_w"].astype(np.float32))
    in_b = np.ascontiguousarray(inp["L0_attn_in_b"].reshape(16, 128).T)
    conv_w = np.ascontiguousarray(
        inp["L0_attn_conv_w"].reshape(4, 8, 128).transpose(2, 1, 0).reshape(128, 32))
    conv_b = np.ascontiguousarray(inp["L0_attn_conv_b"].reshape(8, 128).T)
    zero_lb = np.zeros((128, D), NBF)
    maps = []
    for c in range(NC):
        s0 = c * TPC
        lb = zero_lb if (c % 2 == 0) else np.ascontiguousarray(xn[s0 - 128:s0])
        zfv = np.zeros((128, 1), np.float32) if c % 2 == 0 else \
            np.ones((128, 1), np.float32)
        maps.append({"xn": np.concatenate([lb, xn[s0:s0 + TPC]], 0),
                     "w": in_w, "bi": in_b, "cw": conv_w, "cb": conv_b,
                     "zf": zfv, "ident": ident})
    o2 = _run("L2", build_L2, maps)
    uT = _cat(o2, "o_u", 1)
    gT = _cat(o2, "o_g", 1)

    pw = np.ascontiguousarray(inp["L0_attn_param_w"].astype(np.float32))
    pb = np.ascontiguousarray(inp["L0_attn_param_b"].reshape(24, 128).T)
    o3 = _run("L3", build_L3, [
        {"u": np.ascontiguousarray(uT[:, c * TPC:(c + 1) * TPC]),
         "pw": pw, "pb": pb} for c in range(NC)])
    aT = _cat(o3, "o_a", 1)
    wT = _cat(o3, "o_w", 1)
    cT = _cat(o3, "o_c", 1)

    skip_r = np.ascontiguousarray(inp["L0_attn_skip"].reshape(8, 128))
    maps = []
    for c in range(NC):
        b, hh = c // 2, c % 2
        rs = slice(hh * 512, (hh + 1) * 512)
        cs = slice(b * T, (b + 1) * T)
        maps.append({"a": np.ascontiguousarray(aT[rs, cs]),
                     "w": np.ascontiguousarray(wT[rs, cs]),
                     "c": np.ascontiguousarray(cT[rs, cs]),
                     "u": np.ascontiguousarray(uT[rs, cs]),
                     "g": np.ascontiguousarray(gT[rs, cs]),
                     "skp": np.ascontiguousarray(skip_r[hh * 4:(hh + 1) * 4].T)})
    o4 = _run("L4", build_L4, maps)
    yT = np.empty((D, B * T), NBF)
    for c in range(NC):
        b, hh = c // 2, c % 2
        yT[hh * 512:(hh + 1) * 512, b * T:(b + 1) * T] = o4[c]["o_y"]

    ow = np.ascontiguousarray(inp["L0_attn_out_w"].astype(np.float32))
    ob = np.ascontiguousarray(inp["L0_attn_out_b"].reshape(8, 128).T)
    o5 = _run("L5", build_L5, [
        {"y": np.ascontiguousarray(yT[:, c * TPC:(c + 1) * TPC]),
         "ow": ow, "ob": ob, "x": x[c * TPC:(c + 1) * TPC],
         "ln": ln[c * TPC:(c + 1) * TPC], "ident": ident} for c in range(NC)])
    x1 = _cat(o5, "o_x1", 0)
    ln2 = _cat(o5, "o_ln2", 0)

    o6 = _run("L6", build_L6, [
        {"ln2": ln2[c * TPC:(c + 1) * TPC], "mk": inp["L0_ffn_rail_maker"],
         "up": inp["L0_ffn_rail_up_proj"], "wc": inp["L0_ffn_w_cproj"],
         "msk": mask, "ident": ident, "x1": x1[c * TPC:(c + 1) * TPC]}
        for c in range(NC)])
    x2 = _cat(o6, "o_x2", 0)
    ln3 = _cat(o6, "o_ln3", 0)

    maps = []
    for c in range(NC):
        b, hh = c // 2, c % 2
        maps.append({"ln3": ln3[b * T:(b + 1) * T],
                     "wq": np.ascontiguousarray(
                         inp["L1_attn_Wq"][hh * 512:(hh + 1) * 512].astype(np.float32)),
                     "wk": np.ascontiguousarray(
                         inp["L1_attn_Wk"][hh * 512:(hh + 1) * 512].astype(np.float32)),
                     "wv": np.ascontiguousarray(
                         inp["L1_attn_Wv"][hh * 512:(hh + 1) * 512].astype(np.float32)),
                     "rot": rotT, "cosb": cosb, "sinb": sinb, "ident": ident})
    o7 = _run("L7", build_L7, maps)

    o8 = _run("L8", build_L8, [
        {"qr": o7[c]["o_qr"], "kr": o7[c]["o_kr"], "vt": o7[c]["o_v"],
         "ident": ident} for c in range(NC)])
    datt = np.empty((B * T, D), NBF)
    for c in range(NC):
        b, hh = c // 2, c % 2
        datt[b * T:(b + 1) * T, hh * 512:(hh + 1) * 512] = o8[c]["o_d"]

    o9 = _run("L9", build_L9, [
        {"x2": x2[c * TPC:(c + 1) * TPC],
         "dat": np.ascontiguousarray(datt[c * TPC:(c + 1) * TPC])}
        for c in range(NC)])
    x3 = _cat(o9, "o_x3", 0)
    ln4a = o9[0]["o_ln4"]

    NA = 2048
    b1r = np.ascontiguousarray(
        np.broadcast_to(inp["L1_ffn_gate_up_proj_bias"][:, None, :],
                        (8, 128, 2048)).astype(np.float32))
    maps = []
    for e in range(NC):
        maps.append({"ln4": ln4a,
                     "w1": np.ascontiguousarray(
                         inp["L1_ffn_gate_up_proj"][e].astype(np.float32)),
                     "b1": b1r[e],
                     "w2": np.ascontiguousarray(
                         inp["L1_ffn_down_proj"][e].astype(np.float32)),
                     "b2": np.ascontiguousarray(
                         inp["L1_ffn_down_proj_bias"][e].reshape(8, 128).T),
                     "ri": inp["L1_ffn_router_indices"],
                     "rw": np.ascontiguousarray(
                         inp["L1_ffn_routing_weights"][:, e].reshape(16, 128).T),
                     "ev": np.full((128, 1), float(e), np.float32),
                     "ident": ident})
    o10 = _run("L10", build_L10, maps)

    R = NA // NC
    o11 = _run("L11", build_L11, [
        dict({"x3": x3[c * R:(c + 1) * R]},
             **{f"p{e}": np.ascontiguousarray(o10[e]["o_p"][c * R:(c + 1) * R])
                for e in range(8)}) for c in range(NC)])
    fin_active = _cat(o11, "o", 0)

    out = x3.copy()
    out[0:NA] = fin_active
    return out.reshape(B, T, D)


# version 10
# speedup vs baseline: 1.0158x; 1.0156x over previous
import contextlib
import numpy as np
import ml_dtypes
import concourse.bass as bass
import concourse.mybir as mybir
from concourse.bass_utils import run_bass_kernel_spmd

F32 = mybir.dt.float32
BF16 = mybir.dt.bfloat16
I32 = mybir.dt.int32
AF = mybir.ActivationFunctionType
ALU = mybir.AluOpType
AX = mybir.AxisListType
NBF = ml_dtypes.bfloat16

B, T, D = 4, 4096, 1024
NC = 8
TPC = 2048
EXT = TPC + 128
NTT = TPC // 128


_DMA_OPS = {"dma_start", "dma_start_transpose"}


class _ChainedEngine:

    def __init__(self, eng, chain, kind):
        self._eng = eng
        self._ch = chain
        self._kind = kind

    def __getattr__(self, name):
        f = getattr(self._eng, name)
        if not callable(f):
            return f

        def wrapped(*a, **k):
            ch = self._ch
            kind = self._kind
            if kind in ("vector", "scalar"):
                sem, cnt = ch.sem[kind], ch.cnt[kind]
                if cnt > ch.seg_base[kind]:
                    self._eng.wait_ge(sem, cnt)
            r = f(*a, **k)
            if name in _DMA_OPS:
                dk = kind if kind == "hdma" else "dma"
                r.then_inc(ch.sem[dk], 16)
                ch.cnt[dk] += 16
            else:
                r.then_inc(ch.sem[kind], 1)
                ch.cnt[kind] += 1
            return r
        return wrapped


class Chain:

    def __init__(self, nc, es):
        self.sem = {
            "dma": es.enter_context(nc.semaphore("chd")),
            "hdma": es.enter_context(nc.semaphore("chh")),
            "tensor": es.enter_context(nc.semaphore("cht")),
            "vector": es.enter_context(nc.semaphore("chv")),
            "scalar": es.enter_context(nc.semaphore("chs")),
        }
        self.cnt = {"dma": 0, "hdma": 0, "tensor": 0, "vector": 0, "scalar": 0}
        self.seg_base = dict(self.cnt)

    def seg(self, block, engname, fn):
        ch = self
        kind = ("dma" if engname == "gpsimd" else
                "hdma" if engname == "sync" else engname)
        waits = {k: v for k, v in ch.cnt.items() if v}

        def section(eng):
            for k, v in waits.items():
                eng.wait_ge(ch.sem[k], v)
            ch.seg_base = dict(ch.cnt)
            p = _ChainedEngine(eng, ch, kind)
            if kind in ("dma", "hdma"):
                for _ in fn(p):
                    pass
            else:
                fn(p)
        getattr(block, engname)(section)

    def final_wait(self, block):
        ch = self

        def section(eng):
            for k, v in ch.cnt.items():
                if v:
                    eng.wait_ge(ch.sem[k], v)
        block.gpsimd(section)


def _psum_views(handle, n, w):
    class _V:
        def __init__(self, off):
            self.off = off

        def __getitem__(self, idx):
            if not isinstance(idx, tuple):
                return handle[:, self.off:self.off + w]
            rows, cols = idx
            cs = cols.start or 0
            ce = cols.stop if cols.stop is not None else w
            return handle[rows, self.off + cs:self.off + ce]
    return [_V(i * w) for i in range(n)]


def _tr_wave(ch, blk, items, pst, idb):
    W = len(pst)
    for w0 in range(0, len(items), W):
        wave = items[w0:w0 + W]

        def tseg(t, wave=wave):
            for j, (srcf, _) in enumerate(wave):
                r = t.transpose(pst[j][:], srcf(), idb[:])
            return r
        ch.seg(blk, "tensor", tseg)

        def vseg(v, wave=wave):
            for j, (_, dstf) in enumerate(wave):
                r = v.tensor_copy(dstf(), pst[j][:])
            return r
        ch.seg(blk, "vector", vseg)


def _ln_block(ch, blk, nc_objs, with_out_x, o_x, o_ln, x_in_tiles, add_tiles):
    xt, m, vv, sq, lnb = nc_objs

    def resid(v):
        for i in range(NTT):
            xs = xt[:, i * D:(i + 1) * D]
            for af in add_tiles:
                v.tensor_tensor(xs, xs, af(i), op=ALU.add)
            v.reduce_sum(m[:, i:i + 1], xs, axis=AX.X)
            v.tensor_scalar(m[:, i:i + 1], m[:, i:i + 1], 1.0 / D, None,
                            op0=ALU.mult)
        r = v.tensor_copy(m[:, 0:1], m[:, 0:1])
        return r
    ch.seg(blk, "vector", resid)
    if with_out_x:
        ch.seg(blk, "gpsimd", lambda g: [
            g.dma_start(out=o_x.rearrange("(n p) c -> p n c", p=128), in_=xt[:].rearrange("p (n c) -> p n c", c=D))])

    def center(v):
        for i in range(NTT):
            xs = xt[:, i * D:(i + 1) * D]
            v.tensor_scalar(xs, xs, m[:, i:i + 1], None, op0=ALU.subtract)
            v.tensor_tensor(sq[:], xs, xs, op=ALU.mult)
            v.reduce_sum(vv[:, i:i + 1], sq[:], axis=AX.X)
            r = v.tensor_scalar(vv[:, i:i + 1], vv[:, i:i + 1], 1.0 / D, None,
                                op0=ALU.mult)
        return r
    ch.seg(blk, "vector", center)

    def epsadd(v):
        return v.tensor_scalar(vv[:], vv[:], 1e-5, None, op0=ALU.add)
    ch.seg(blk, "vector", epsadd)

    def rstds(s):
        s.activation(vv[:], vv[:], AF.Ln)
        return s.activation(vv[:], vv[:], AF.Exp, scale=-0.5)
    ch.seg(blk, "scalar", rstds)
    for i0 in range(0, NTT, 4):
        def lnw(v, i0=i0):
            for j in range(4):
                i = i0 + j
                v.tensor_scalar(lnb[:, j * D:(j + 1) * D],
                                xt[:, i * D:(i + 1) * D],
                                vv[:, i:i + 1], None, op0=ALU.mult)
        ch.seg(blk, "vector", lnw)
        ch.seg(blk, "gpsimd", lambda g, i0=i0: [
            g.dma_start(out=o_ln[i0 * 128:(i0 + 4) * 128, :]
                        .rearrange("(n p) c -> p n c", p=128),
                        in_=lnb[:].rearrange("p (n c) -> p n c", c=D))])


def build_L1():
    nc = bass.Bass("TRN2", target_bir_lowering=False)
    x = nc.dram_tensor("x", [TPC, D], F32, kind="ExternalInput").ap()
    nwrep = nc.dram_tensor("nwrep", [128, D], F32, kind="ExternalInput").ap()
    o_ln = nc.dram_tensor("o_ln", [TPC, D], F32, kind="ExternalOutput").ap()
    o_xn = nc.dram_tensor("o_xn", [TPC, D], BF16, kind="ExternalOutput").ap()
    with contextlib.ExitStack() as es:
        xt = es.enter_context(nc.sbuf_tensor("xt", [128, NTT * D], F32))
        nw = es.enter_context(nc.sbuf_tensor("nw", [128, D], F32))
        m = es.enter_context(nc.sbuf_tensor("m", [128, NTT], F32))
        vv = es.enter_context(nc.sbuf_tensor("vv", [128, NTT], F32))
        rstd = es.enter_context(nc.sbuf_tensor("rstd", [128, NTT], F32))
        tmp = es.enter_context(nc.sbuf_tensor("tmp", [128, NTT], F32))
        lnt = es.enter_context(nc.sbuf_tensor("lnt", [128, D], F32))
        xnt = es.enter_context(nc.sbuf_tensor("xnt", [128, D], BF16))
        sq = es.enter_context(nc.sbuf_tensor("sq", [128, D], F32))
        ch = Chain(nc, es)
        with nc.Block() as blk:
            ch.seg(blk, "gpsimd", lambda g: [
                g.dma_start(out=xt[:].rearrange("p (n c) -> p n c", c=D), in_=x.rearrange("(n p) c -> p n c", p=128)),
                g.dma_start(out=nw[:], in_=nwrep)])

            def stats(v):
                for i in range(NTT):
                    sl = xt[:, i * D:(i + 1) * D]
                    v.reduce_sum(m[:, i:i + 1], sl, axis=AX.X)
                    v.tensor_scalar(m[:, i:i + 1], m[:, i:i + 1], 1.0 / D, None,
                                    op0=ALU.mult)
                    v.tensor_scalar(sl, sl, m[:, i:i + 1], None, op0=ALU.subtract)
                    v.tensor_tensor(sq[:], sl, sl, op=ALU.mult)
                    v.reduce_sum(vv[:, i:i + 1], sq[:], axis=AX.X)
                    r = v.tensor_scalar(vv[:, i:i + 1], vv[:, i:i + 1], 1.0 / D,
                                        None, op0=ALU.mult)
                return r
            ch.seg(blk, "vector", stats)

            def epsadd1(v):
                return v.tensor_scalar(rstd[:], vv[:], 1e-5, None, op0=ALU.add)
            ch.seg(blk, "vector", epsadd1)

            def rstds(s):
                s.activation(rstd[:], rstd[:], AF.Ln)
                return s.activation(rstd[:], rstd[:], AF.Exp, scale=-0.5)
            ch.seg(blk, "scalar", rstds)

            def q(v):
                v.tensor_tensor(tmp[:], rstd[:], rstd[:], op=ALU.mult)
                return v.tensor_tensor(tmp[:], tmp[:], vv[:], op=ALU.mult)
            ch.seg(blk, "vector", q)

            def epsadd2(v):
                return v.tensor_scalar(tmp[:], tmp[:], 1e-6, None, op0=ALU.add)
            ch.seg(blk, "vector", epsadd2)

            def rs2(s):
                s.activation(tmp[:], tmp[:], AF.Ln)
                return s.activation(tmp[:], tmp[:], AF.Exp, scale=-0.5)
            ch.seg(blk, "scalar", rs2)

            def fin(v):
                return v.tensor_tensor(tmp[:], rstd[:], tmp[:], op=ALU.mult)
            ch.seg(blk, "vector", fin)
            for i in range(NTT):
                def wr(v, i=i):
                    xc = xt[:, i * D:(i + 1) * D]
                    v.tensor_scalar(lnt[:], xc, rstd[:, i:i + 1], None, op0=ALU.mult)
                    v.tensor_scalar(xc, xc, tmp[:, i:i + 1], None, op0=ALU.mult)
                    return v.tensor_tensor(xnt[:], xc, nw[:], op=ALU.mult)
                ch.seg(blk, "vector", wr)
                ch.seg(blk, "gpsimd", lambda g, i=i: [
                    g.dma_start(out=o_ln[i * 128:(i + 1) * 128, :], in_=lnt[:]),
                    g.dma_start(out=o_xn[i * 128:(i + 1) * 128, :], in_=xnt[:])])
            ch.final_wait(blk)
    return nc


def build_L2():
    NE = EXT // 128
    nc = bass.Bass("TRN2", target_bir_lowering=False)
    xn = nc.dram_tensor("xn", [EXT, D], BF16, kind="ExternalInput").ap()
    w = nc.dram_tensor("w", [D, 2 * D], F32, kind="ExternalInput").ap()
    bi = nc.dram_tensor("bi", [128, 16], F32, kind="ExternalInput").ap()
    cw = nc.dram_tensor("cw", [128, 32], F32, kind="ExternalInput").ap()
    cb = nc.dram_tensor("cb", [128, 8], F32, kind="ExternalInput").ap()
    zf = nc.dram_tensor("zf", [128, 1], F32, kind="ExternalInput").ap()
    ident = nc.dram_tensor("ident", [128, 128], BF16, kind="ExternalInput").ap()
    o_u = nc.dram_tensor("o_u", [D, TPC], F32, kind="ExternalOutput").ap()
    o_g = nc.dram_tensor("o_g", [D, TPC], BF16, kind="ExternalOutput").ap()
    NT5 = [(0, 512), (512, 512), (1024, 512), (1536, 512), (2048, 128)]
    with contextlib.ExitStack() as es:
        xT = [es.enter_context(nc.sbuf_tensor(f"xT{k}", [128, EXT], BF16))
              for k in range(8)]
        wsb = [es.enter_context(nc.sbuf_tensor(f"wsb{k}", [128, 2 * D], BF16))
               for k in range(8)]
        idb = es.enter_context(nc.sbuf_tensor("idb", [128, 128], BF16))
        bib = es.enter_context(nc.sbuf_tensor("bib", [128, 16], F32))
        cwb = es.enter_context(nc.sbuf_tensor("cwb", [128, 32], F32))
        cbb = es.enter_context(nc.sbuf_tensor("cbb", [128, 8], F32))
        zfb = es.enter_context(nc.sbuf_tensor("zfb", [128, 1], F32))
        uraw = es.enter_context(nc.sbuf_tensor("uraw", [128, EXT], F32))
        ucv = es.enter_context(nc.sbuf_tensor("ucv", [128, TPC], F32))
        usil = es.enter_context(nc.sbuf_tensor("usil", [128, TPC], F32))
        gsil = es.enter_context(nc.sbuf_tensor("gsil", [128, TPC], BF16))
        pst = [es.enter_context(nc.psum_tensor(f"pst{k}", [128, 128], BF16))
               for k in range(4)]
        psm = [es.enter_context(nc.psum_tensor(f"psm{k}", [128, 512], F32))
               for k in range(3)]
        ch = Chain(nc, es)
        with nc.Block() as blk:
            for kc in range(8):
                ch.seg(blk, "sync", lambda sy, kc=kc: [
                    sy.dma_start_transpose(xT[kc][:],
                                           xn[:, kc * 128:(kc + 1) * 128])])
            ch.seg(blk, "gpsimd", lambda g: [
                g.dma_start(out=idb[:], in_=ident),
                g.dma_start(out=bib[:], in_=bi),
                g.dma_start(out=cwb[:], in_=cw),
                g.dma_start(out=cbb[:], in_=cb),
                g.dma_start(out=zfb[:], in_=zf)] + [
                g.dma_start(out=wsb[k][:], in_=w[k * 128:(k + 1) * 128, :])
                for k in range(8)])

            for oc in range(16):
                for w0 in range(0, 5, 3):
                    sub = list(enumerate(NT5))[w0:w0 + 3]

                    def mseg(t, oc=oc, sub=sub):
                        for ni, (n0, nn) in sub:
                            for kc in range(8):
                                r = t.matmul(psm[ni - sub[0][0]][0:128, 0:nn],
                                             wsb[kc][:, oc * 128:(oc + 1) * 128],
                                             xT[kc][:, n0:n0 + nn],
                                             start=(kc == 0), stop=(kc == 7))
                        return r
                    ch.seg(blk, "tensor", mseg)

                    def eseg(v, oc=oc, sub=sub):
                        for ni, (n0, nn) in sub:
                            r = v.tensor_scalar(uraw[:, n0:n0 + nn],
                                                psm[ni - sub[0][0]][0:128, 0:nn],
                                                bib[:, oc:oc + 1], None, op0=ALU.add)
                        return r
                    ch.seg(blk, "vector", eseg)
                if oc < 8:
                    def conv(v, oc=oc):
                        v.tensor_scalar(uraw[:, 0:128], uraw[:, 0:128], zfb[:, 0:1],
                                        None, op0=ALU.mult)
                        v.tensor_scalar(ucv[:], uraw[:, 125:125 + TPC],
                                        cwb[:, oc * 4:oc * 4 + 1],
                                        cbb[:, oc:oc + 1], op0=ALU.mult, op1=ALU.add)
                        for k in range(1, 4):
                            r = v.scalar_tensor_tensor(
                                ucv[:], uraw[:, 125 + k:125 + k + TPC],
                                cwb[:, oc * 4 + k:oc * 4 + k + 1], ucv[:],
                                op0=ALU.mult, op1=ALU.add)
                        return r
                    ch.seg(blk, "vector", conv)
                    ch.seg(blk, "scalar",
                           lambda s: s.activation(usil[:], ucv[:], AF.Silu))
                    ch.seg(blk, "gpsimd", lambda g, oc=oc: [
                        g.dma_start(out=o_u[oc * 128:(oc + 1) * 128, :], in_=usil[:])])
                else:
                    ch.seg(blk, "scalar",
                           lambda s: s.activation(gsil[:], uraw[:, 128:EXT], AF.Silu))
                    ch.seg(blk, "gpsimd", lambda g, oc=oc: [
                        g.dma_start(out=o_g[(oc - 8) * 128:(oc - 7) * 128, :],
                                    in_=gsil[:])])
            ch.final_wait(blk)
    return nc


def build_L3():
    nc = bass.Bass("TRN2", target_bir_lowering=False)
    u = nc.dram_tensor("u", [D, TPC], F32, kind="ExternalInput").ap()
    pw = nc.dram_tensor("pw", [D, 3 * D], F32, kind="ExternalInput").ap()
    pb = nc.dram_tensor("pb", [128, 24], F32, kind="ExternalInput").ap()
    o_a = nc.dram_tensor("o_a", [D, TPC], F32, kind="ExternalOutput").ap()
    o_w = nc.dram_tensor("o_w", [D, TPC], F32, kind="ExternalOutput").ap()
    o_c = nc.dram_tensor("o_c", [D, TPC], BF16, kind="ExternalOutput").ap()
    with contextlib.ExitStack() as es:
        uf = [es.enter_context(nc.sbuf_tensor(f"uf{k}", [128, TPC], F32))
              for k in range(8)]
        ub = [es.enter_context(nc.sbuf_tensor(f"ub{k}", [128, TPC], BF16))
              for k in range(8)]
        wsb = es.enter_context(nc.sbuf_tensor("wsb", [128, 8 * 128], BF16))
        pbb = es.enter_context(nc.sbuf_tensor("pbb", [128, 24], F32))
        acc = es.enter_context(nc.sbuf_tensor("acc", [128, TPC], F32))
        accb = es.enter_context(nc.sbuf_tensor("accb", [128, TPC], BF16))
        psm = [es.enter_context(nc.psum_tensor(f"psm{k}", [128, 512], F32))
               for k in range(4)]
        ch = Chain(nc, es)
        with nc.Block() as blk:
            ch.seg(blk, "gpsimd", lambda g: [
                g.dma_start(out=uf[k][:], in_=u[k * 128:(k + 1) * 128, :])
                for k in range(8)] + [
                g.dma_start(out=pbb[:], in_=pb)])

            def cast(v):
                for k in range(8):
                    r = v.tensor_copy(ub[k][:], uf[k][:])
                return r
            ch.seg(blk, "vector", cast)
            for oc in range(24):
                ch.seg(blk, "gpsimd", lambda g, oc=oc: [
                    g.dma_start(out=wsb[:],
                                in_=bass.AP(pw.tensor, oc * 128,
                                            [[3 * D, 128], [128 * 3 * D, 8],
                                             [1, 128]]))])

                def mseg(t, oc=oc):
                    for ni in range(4):
                        for kc in range(8):
                            r = t.matmul(psm[ni][:],
                                         wsb[:, kc * 128:(kc + 1) * 128],
                                         ub[kc][:, ni * 512:(ni + 1) * 512],
                                         start=(kc == 0), stop=(kc == 7))
                    return r
                ch.seg(blk, "tensor", mseg)
                if oc < 8:
                    def ev(s, oc=oc):
                        for ni in range(4):
                            r = s.activation(acc[:, ni * 512:(ni + 1) * 512],
                                             psm[ni][:], AF.Sigmoid,
                                             bias=pbb[:, oc:oc + 1])
                        return r
                    ch.seg(blk, "scalar", ev)
                    ch.seg(blk, "gpsimd", lambda g, oc=oc: [
                        g.dma_start(out=o_a[oc * 128:(oc + 1) * 128, :], in_=acc[:])])
                elif oc < 16:
                    def ev(s, oc=oc):
                        for ni in range(4):
                            r = s.activation(acc[:, ni * 512:(ni + 1) * 512],
                                             psm[ni][:], AF.Tanh,
                                             bias=pbb[:, oc:oc + 1])
                        return r
                    ch.seg(blk, "scalar", ev)
                    ch.seg(blk, "vector",
                           lambda v, oc=oc: v.tensor_tensor(acc[:], acc[:],
                                                            uf[oc - 8][:],
                                                            op=ALU.mult))
                    ch.seg(blk, "gpsimd", lambda g, oc=oc: [
                        g.dma_start(out=o_w[(oc - 8) * 128:(oc - 7) * 128, :],
                                    in_=acc[:])])
                else:
                    def ev(s, oc=oc):
                        for ni in range(4):
                            r = s.activation(accb[:, ni * 512:(ni + 1) * 512],
                                             psm[ni][:], AF.Sigmoid,
                                             bias=pbb[:, oc:oc + 1])
                        return r
                    ch.seg(blk, "scalar", ev)
                    ch.seg(blk, "gpsimd", lambda g, oc=oc: [
                        g.dma_start(out=o_c[(oc - 16) * 128:(oc - 15) * 128, :],
                                    in_=accb[:])])
            ch.final_wait(blk)
    return nc


def build_L4():
    CH = 512
    nc = bass.Bass("TRN2", target_bir_lowering=False)
    a = nc.dram_tensor("a", [CH, T], F32, kind="ExternalInput").ap()
    w = nc.dram_tensor("w", [CH, T], F32, kind="ExternalInput").ap()
    c = nc.dram_tensor("c", [CH, T], BF16, kind="ExternalInput").ap()
    u = nc.dram_tensor("u", [CH, T], F32, kind="ExternalInput").ap()
    g_ = nc.dram_tensor("g", [CH, T], BF16, kind="ExternalInput").ap()
    skp = nc.dram_tensor("skp", [128, 4], F32, kind="ExternalInput").ap()
    o_y = nc.dram_tensor("o_y", [CH, T], BF16, kind="ExternalOutput").ap()
    with contextlib.ExitStack() as es:
        ab = es.enter_context(nc.sbuf_tensor("ab", [128, T], F32))
        wb = es.enter_context(nc.sbuf_tensor("wb", [128, T], F32))
        cb = es.enter_context(nc.sbuf_tensor("cb", [128, T], F32))
        ub = es.enter_context(nc.sbuf_tensor("ub", [128, T], F32))
        gb = es.enter_context(nc.sbuf_tensor("gb", [128, T], F32))
        sk = es.enter_context(nc.sbuf_tensor("sk", [128, 4], F32))
        st = es.enter_context(nc.sbuf_tensor("st", [128, T], F32))
        yb = es.enter_context(nc.sbuf_tensor("yb", [128, T], BF16))
        ch = Chain(nc, es)
        with nc.Block() as blk:
            ch.seg(blk, "gpsimd", lambda g: [
                g.dma_start(out=sk[:], in_=skp)])
            for grp in range(4):
                sl = slice(grp * 128, (grp + 1) * 128)
                ch.seg(blk, "gpsimd", lambda g, sl=sl: [
                    g.dma_start(out=ab[:], in_=a[sl, :]),
                    g.dma_start(out=wb[:], in_=w[sl, :]),
                    g.dma_start(out=cb[:], in_=c[sl, :]),
                    g.dma_start(out=ub[:], in_=u[sl, :]),
                    g.dma_start(out=gb[:], in_=g_[sl, :])])

                def scan(v, grp=grp):
                    v.tensor_tensor_scan(st[:], ab[:], wb[:], 0.0,
                                         op0=ALU.mult, op1=ALU.add)
                    v.tensor_tensor(st[:], st[:], cb[:], op=ALU.mult)
                    v.tensor_scalar(ub[:], ub[:], sk[:, grp:grp + 1], None,
                                    op0=ALU.mult)
                    v.tensor_tensor(st[:], st[:], ub[:], op=ALU.add)
                    return v.tensor_tensor(yb[:], st[:], gb[:], op=ALU.mult)
                ch.seg(blk, "vector", scan)
                ch.seg(blk, "gpsimd", lambda g, sl=sl: [
                    g.dma_start(out=o_y[sl, :], in_=yb[:])])
            ch.final_wait(blk)
    return nc


def build_L5():
    nc = bass.Bass("TRN2", target_bir_lowering=False)
    y = nc.dram_tensor("y", [D, TPC], BF16, kind="ExternalInput").ap()
    ow = nc.dram_tensor("ow", [D, D], F32, kind="ExternalInput").ap()
    ob = nc.dram_tensor("ob", [128, 8], F32, kind="ExternalInput").ap()
    x = nc.dram_tensor("x", [TPC, D], F32, kind="ExternalInput").ap()
    ln = nc.dram_tensor("ln", [TPC, D], F32, kind="ExternalInput").ap()
    ident = nc.dram_tensor("ident", [128, 128], BF16, kind="ExternalInput").ap()
    o_x1 = nc.dram_tensor("o_x1", [TPC, D], F32, kind="ExternalOutput").ap()
    o_ln2 = nc.dram_tensor("o_ln2", [TPC, D], BF16, kind="ExternalOutput").ap()
    with contextlib.ExitStack() as es:
        yb = [es.enter_context(nc.sbuf_tensor(f"yb{k}", [128, TPC], BF16))
              for k in range(8)]
        wsb = es.enter_context(nc.sbuf_tensor("wsb", [128, 8 * 128], BF16))
        obb = es.enter_context(nc.sbuf_tensor("obb", [128, 8], F32))
        idb = es.enter_context(nc.sbuf_tensor("idb", [128, 128], BF16))
        dT = es.enter_context(nc.sbuf_tensor("dT", [128, TPC], BF16))
        dtok = es.enter_context(nc.sbuf_tensor("dtok", [128, NTT * D], F32))
        xt = es.enter_context(nc.sbuf_tensor("xt", [128, NTT * D], F32))
        lns = es.enter_context(nc.sbuf_tensor("lns", [128, 4 * D], F32))
        m = es.enter_context(nc.sbuf_tensor("m", [128, NTT], F32))
        vv = es.enter_context(nc.sbuf_tensor("vv", [128, NTT], F32))
        sq = es.enter_context(nc.sbuf_tensor("sq", [128, D], F32))
        lnb = es.enter_context(nc.sbuf_tensor("lnb", [128, 4 * D], BF16))
        psm = [es.enter_context(nc.psum_tensor(f"psm{k}", [128, 512], F32))
               for k in range(4)]
        pst = [es.enter_context(nc.psum_tensor(f"pst{k}", [128, 128], BF16))
               for k in range(4)]
        ch = Chain(nc, es)
        with nc.Block() as blk:
            ch.seg(blk, "gpsimd", lambda g: [
                g.dma_start(out=yb[k][:], in_=y[k * 128:(k + 1) * 128, :])
                for k in range(8)] + [
                g.dma_start(out=obb[:], in_=ob),
                g.dma_start(out=idb[:], in_=ident),
                g.dma_start(out=xt[:].rearrange("p (n c) -> p n c", c=D), in_=x.rearrange("(n p) c -> p n c", p=128))])
            for r0 in range(0, NTT, 4):
                ch.seg(blk, "gpsimd", lambda g, r0=r0: [
                    g.dma_start(out=lns[:].rearrange("p (n c) -> p n c", c=D),
                                in_=ln[r0 * 128:(r0 + 4) * 128, :]
                                .rearrange("(n p) c -> p n c", p=128))])

                def addln(v, r0=r0):
                    for j in range(4):
                        v.tensor_tensor(xt[:, (r0 + j) * D:(r0 + j + 1) * D],
                                        xt[:, (r0 + j) * D:(r0 + j + 1) * D],
                                        lns[:, j * D:(j + 1) * D], op=ALU.add)
                ch.seg(blk, "vector", addln)
            for oc in range(8):
                ch.seg(blk, "gpsimd", lambda g, oc=oc: [
                    g.dma_start(out=wsb[:],
                                in_=bass.AP(ow.tensor, oc * 128,
                                            [[D, 128], [128 * D, 8], [1, 128]]))])

                def mseg(t, oc=oc):
                    for ni in range(4):
                        for kc in range(8):
                            r = t.matmul(psm[ni][:],
                                         wsb[:, kc * 128:(kc + 1) * 128],
                                         yb[kc][:, ni * 512:(ni + 1) * 512],
                                         start=(kc == 0), stop=(kc == 7))
                    return r
                ch.seg(blk, "tensor", mseg)

                def ev(v, oc=oc):
                    for ni in range(4):
                        r = v.tensor_scalar(dT[:, ni * 512:(ni + 1) * 512],
                                            psm[ni][:], obb[:, oc:oc + 1], None,
                                            op0=ALU.add)
                    return r
                ch.seg(blk, "vector", ev)
                items = [((lambda n=n: dT[:, n * 128:(n + 1) * 128]),
                          (lambda n=n, oc=oc:
                           dtok[:, n * D + oc * 128:n * D + oc * 128 + 128]))
                         for n in range(NTT)]
                _tr_wave(ch, blk, items, pst, idb)
            _ln_block(ch, blk, (xt, m, vv, sq, lnb), True, o_x1, o_ln2,
                      None, [lambda i: dtok[:, i * D:(i + 1) * D]])
            ch.final_wait(blk)
    return nc


def build_L6():
    nc = bass.Bass("TRN2", target_bir_lowering=False)
    ln2 = nc.dram_tensor("ln2", [TPC, D], BF16, kind="ExternalInput").ap()
    mk = nc.dram_tensor("mk", [4, 64, 16, 16], F32, kind="ExternalInput").ap()
    up = nc.dram_tensor("up", [4, 64, 64, 16], F32, kind="ExternalInput").ap()
    wc = nc.dram_tensor("wc", [D, 16 * D], F32, kind="ExternalInput").ap()
    msk = nc.dram_tensor("msk", [128, 128], BF16, kind="ExternalInput").ap()
    ident = nc.dram_tensor("ident", [128, 128], BF16, kind="ExternalInput").ap()
    x1 = nc.dram_tensor("x1", [TPC, D], F32, kind="ExternalInput").ap()
    o_x2 = nc.dram_tensor("o_x2", [TPC, D], F32, kind="ExternalOutput").ap()
    o_ln3 = nc.dram_tensor("o_ln3", [TPC, D], BF16, kind="ExternalOutput").ap()
    chbm = nc.dram_tensor("chbm", [4 * 8 * 8 * 16 * 64], BF16)
    with contextlib.ExitStack() as es:
        mskb = es.enter_context(nc.sbuf_tensor("mskb", [128, 128], BF16))
        idb = es.enter_context(nc.sbuf_tensor("idb", [128, 128], BF16))
        d1 = [es.enter_context(nc.sbuf_tensor(f"d1_{k}", [128, TPC], BF16))
              for k in range(8)]
        psm = [es.enter_context(nc.psum_tensor(f"psm{k}", [128, 512], F32))
               for k in range(4)]
        pstb = es.enter_context(nc.psum_tensor("pstb", [128, 512], BF16))
        pst = _psum_views(pstb, 4, 128)
        es_mid = es.enter_context(contextlib.ExitStack())
        lT = [es_mid.enter_context(nc.sbuf_tensor(f"lT{k}", [128, TPC], BF16))
              for k in range(8)]
        bd = es_mid.enter_context(nc.sbuf_tensor("bd", [128, 128 * 128], BF16))
        ch = Chain(nc, es)
        with nc.Block() as blk:
            ch.seg(blk, "gpsimd", lambda g: [
                g.dma_start(out=mskb[:], in_=msk),
                g.dma_start(out=idb[:], in_=ident)])
            for kc in range(8):
                ch.seg(blk, "sync", lambda sy, kc=kc: [
                    sy.dma_start_transpose(lT[kc][:],
                                           ln2[:, kc * 128:(kc + 1) * 128])])
            with contextlib.ExitStack() as es2:
                mkb = es2.enter_context(nc.sbuf_tensor("mkb", [128, 32 * 16], BF16))
                upb = es2.enter_context(nc.sbuf_tensor("upb", [128, 128 * 16], BF16))
                mkT = es2.enter_context(nc.sbuf_tensor("mkT", [16, 32 * 128], BF16))
                upT = es2.enter_context(nc.sbuf_tensor("upT", [16, 128 * 128], BF16))
                cst = es2.enter_context(nc.sbuf_tensor("cst", [16, 32 * 512], BF16))
                csb = es2.enter_context(nc.sbuf_tensor("csb", [128, 4 * 64], BF16))
                psc = es2.enter_context(nc.psum_tensor("psc", [16, 512], F32))
                ch.seg(blk, "gpsimd", lambda g: [
                    g.dma_start(out=mkb[:, i * 16:(i + 1) * 16],
                                in_=bass.AP(mk.tensor, i * 2048,
                                            [[16, 128], [1, 16]]))
                    for i in range(32)] + [
                    g.dma_start(out=upb[:, i * 16:(i + 1) * 16],
                                in_=bass.AP(up.tensor, i * 2048,
                                            [[16, 128], [1, 16]]))
                    for i in range(128)])
                items = []
                for i in range(32):
                    items.append((
                        (lambda i=i: mkb[:, i * 16:(i + 1) * 16]),
                        (lambda i=i: mkT[0:16, i * 128:(i + 1) * 128]),
                        True))
                for i in range(128):
                    items.append((
                        (lambda i=i: upb[:, i * 16:(i + 1) * 16]),
                        (lambda i=i: upT[0:16, i * 128:(i + 1) * 128]),
                        True))
                W = len(pst)
                for w0 in range(0, len(items), W):
                    wave = items[w0:w0 + W]

                    def tseg(t, wave=wave):
                        for j, (srcf, _, _) in enumerate(wave):
                            t.transpose(pst[j][0:16, 0:128], srcf(), idb[:])
                    ch.seg(blk, "tensor", tseg)

                    def vseg(v, wave=wave):
                        for j, (_, dstf, _) in enumerate(wave):
                            v.tensor_copy(dstf(), pst[j][0:16, 0:128])
                    ch.seg(blk, "vector", vseg)
                for hg in range(32):
                    def cm(t, hg=hg):
                        for rr in range(8):
                            r8 = hg * 8 + rr
                            t.matmul(psc[0:16, rr * 64:(rr + 1) * 64],
                                     mkT[0:16, r8 * 128 // 8 + 0:0] if False else
                                     mkT[0:16, hg * 128 + rr * 16:
                                         hg * 128 + rr * 16 + 16],
                                     upT[0:16, (hg * 8 + rr) * 128 + 0:
                                         (hg * 8 + rr) * 128 + 0] if False else
                                     upT[0:16, (hg * 8 + rr) * 128 // 2 + 0:0]
                                     if False else
                                     upT[0:16, hg * 512 + rr * 64:
                                         hg * 512 + rr * 64 + 64],
                                     start=True, stop=True)
                    ch.seg(blk, "tensor", cm)
                    ch.seg(blk, "vector",
                           lambda v, hg=hg: v.tensor_copy(
                               cst[:, hg * 512:(hg + 1) * 512], psc[0:16, :]))
                ch.seg(blk, "gpsimd", lambda g: [
                    g.dma_start(
                        out=bass.AP(chbm, hg * 8192,
                                    [[64, 16], [1024, 8], [1, 64]]),
                        in_=cst[:, hg * 512:(hg + 1) * 512]
                        .rearrange("p (a f) -> p a f", a=8))
                    for hg in range(32)])
                for hg0 in range(0, 32, 4):
                    ch.seg(blk, "gpsimd", lambda g, hg0=hg0: [
                        g.dma_start(out=csb[:, k * 64:(k + 1) * 64],
                                    in_=bass.AP(chbm, (hg0 + k) * 8192,
                                                [[64, 128], [1, 64]]))
                        for k in range(4)])

                    def bdm(v, hg0=hg0):
                        for k in range(4):
                            hg = hg0 + k
                            for cc in range(4):
                                sl = bd[:, (hg * 4 + cc) * 128:
                                        (hg * 4 + cc) * 128 + 128]
                                v.tensor_tensor(
                                    sl.rearrange("p (a f) -> p a f", a=8),
                                    bass.AP(csb, k * 64 + 16 * cc,
                                            [[256, 128], [0, 8], [1, 16]]),
                                    bass.AP(mskb, 0,
                                            [[128, 128], [16, 8], [1, 16]]),
                                    op=ALU.mult)
                    ch.seg(blk, "vector", bdm)
            with contextlib.ExitStack() as es3:
                hz = [es3.enter_context(nc.sbuf_tensor(f"hz{k}", [128, TPC], BF16))
                      for k in range(16)]
                wblk = [es3.enter_context(nc.sbuf_tensor(f"wblk{j}", [128, 2048],
                                                         BF16)) for j in range(2)]
                wT = es3.enter_context(nc.sbuf_tensor("wT", [128, 16 * 128], BF16))
                wcc = es3.enter_context(nc.sbuf_tensor("wcc", [128, 2 * 128], BF16))
                scr4 = es3.enter_context(nc.sbuf_tensor("scr4", [128, 4 * 512],
                                                        BF16))
                pc = [es3.enter_context(nc.psum_tensor(f"pc{j}", [128, 512], F32))
                      for j in range(2)]
                pt2 = es3.enter_context(nc.psum_tensor("pt2", [128, 128], BF16))
                pt = [pst[0], pt2]
                s3t = es3.enter_context(nc.semaphore("s3t"))
                s3v = es3.enter_context(nc.semaphore("s3v"))
                s3a = es3.enter_context(nc.semaphore("s3a"))
                s3d = es3.enter_context(nc.semaphore("s3d"))
                s3w = es3.enter_context(nc.semaphore("s3w"))
                entry = {k: v for k, v in ch.cnt.items() if v}

                def te_z(it2, q):
                    return it2 * 112 + q + 1

                def te_wt(it2, oc, w):
                    return it2 * 112 + 16 + oc * 12 + w + 1

                def te_cp(it2, oc, ni):
                    return it2 * 112 + 16 + oc * 12 + 8 + ni + 1

                def ve_wg(it2, oc, w):
                    return it2 * 160 + oc * 20 + 2 * w + 1

                def ve_we(it2, oc, w):
                    return it2 * 160 + oc * 20 + 2 * w + 2

                def ve_acc(it2, oc, ni):
                    return it2 * 160 + oc * 20 + 16 + ni + 1

                def a_g(it2, q):
                    return it2 * 16 + q + 1

                D0 = ch.cnt["dma"]

                def s3_gp(g):
                    for k, v in entry.items():
                        g.wait_ge(ch.sem[k], v)
                    for it2 in range(8):
                        h, gh = it2 // 2, it2 % 2
                        for oc in range(8):
                            if it2 * 8 + oc >= 2:
                                po = it2 * 8 + oc - 2
                                g.wait_ge(s3v, (po // 8) * 160
                                          + (po % 8) * 20 + 15)
                            g.dma_start(out=wblk[oc % 2][:],
                                        in_=bass.AP(wc.tensor,
                                                    oc * 128 * 16384
                                                    + h * 4096 + gh * 2048,
                                                    [[16384, 128], [1, 2048]])
                                        ).then_inc(s3d, 16)
                    g.wait_ge(s3d, 64 * 16)
                blk.gpsimd(s3_gp)

                def s3_te(t):
                    for k, v in entry.items():
                        t.wait_ge(ch.sem[k], v)
                    for it2 in range(8):
                        h, gh = it2 // 2, it2 % 2
                        for q in range(16):
                            gi, cc = q // 4, q % 4
                            g2 = gh * 4 + gi
                            if q > 0:
                                t.wait_ge(s3a, a_g(it2, q - 1))
                            elif it2 > 0:
                                t.wait_ge(s3a, a_g(it2 - 1, 15))
                            for ni in range(4):
                                r = t.matmul(
                                    psm[ni][:],
                                    bd[:, ((h * 8 + g2) * 4 + cc) * 128:
                                       ((h * 8 + g2) * 4 + cc) * 128 + 128],
                                    lT[g2][:, ni * 512:(ni + 1) * 512],
                                    start=True, stop=True)
                            r.then_inc(s3t, 1)
                        for oc in range(8):
                            for w in range(8):
                                t.wait_ge(s3v, ve_wg(it2, oc, w))
                                if w >= 1:
                                    t.wait_ge(s3v, ve_we(it2, oc, w - 1))
                                elif oc > 0:
                                    t.wait_ge(s3v, ve_we(it2, oc - 1, 7))
                                elif it2 > 0:
                                    t.wait_ge(s3v, ve_we(it2 - 1, 7, 7))
                                for j in range(2):
                                    r = t.transpose(pt[j][:],
                                                    wcc[:, j * 128:(j + 1) * 128],
                                                    idb[:])
                                r.then_inc(s3t, 1)
                            for ni in range(4):
                                if ni == 0:
                                    t.wait_ge(s3a, a_g(it2, 15))
                                    t.wait_ge(s3v, ve_we(it2, oc, 7))
                                b = ni % 2
                                if oc > 0:
                                    t.wait_ge(s3v, ve_acc(it2, oc - 1, b + 2))
                                elif it2 > 0:
                                    t.wait_ge(s3v, ve_acc(it2 - 1, 7, b + 2))
                                if ni >= 2:
                                    t.wait_ge(s3v, ve_acc(it2, oc, ni - 2))
                                for q in range(16):
                                    r = t.matmul(pc[b][:],
                                                 wT[:, q * 128:(q + 1) * 128],
                                                 hz[q][:, ni * 512:(ni + 1) * 512],
                                                 start=(q == 0), stop=(q == 15))
                                r.then_inc(s3t, 1)
                blk.tensor(s3_te)

                def s3_act(s):
                    for k, v in entry.items():
                        s.wait_ge(ch.sem[k], v)
                    for it2 in range(8):
                        for q in range(16):
                            s.wait_ge(s3t, te_z(it2, q))
                            if q == 0 and it2 > 0:
                                s.wait_ge(s3t, te_cp(it2 - 1, 7, 3))
                            for ni in range(4):
                                r = s.activation(
                                    hz[q][:, ni * 512:(ni + 1) * 512],
                                    psm[ni][:], AF.Gelu)
                            r.then_inc(s3a, 1)
                blk.scalar(s3_act)

                def s3_ve(v):
                    for k, v2 in entry.items():
                        v.wait_ge(ch.sem[k], v2)
                    nvv = 0
                    for it2 in range(8):
                        for oc in range(8):
                            for w in range(8):
                                v.wait_ge(s3d, 32 * ((it2 * 8 + oc) // 2 + 1))
                                if w > 0:
                                    v.wait_ge(s3t, te_wt(it2, oc, w - 1))
                                for j in range(2):
                                    q = w * 2 + j
                                    gi, cc = q // 4, q % 4
                                    r = v.tensor_copy(
                                        wcc[:, j * 128:(j + 1) * 128]
                                        .rearrange("p (a f) -> p a f", a=8),
                                        bass.AP(wblk[oc % 2],
                                                gi * 512 + 16 * cc,
                                                [[2048, 128], [64, 8], [1, 16]]))
                                r.then_inc(s3v, 1)
                                v.wait_ge(s3t, te_wt(it2, oc, w))
                                for j in range(2):
                                    q = w * 2 + j
                                    r = v.tensor_copy(
                                        wT[:, q * 128:(q + 1) * 128], pt[j][:])
                                r.then_inc(s3v, 1)
                            for ni in range(4):
                                v.wait_ge(s3t, te_cp(it2, oc, ni))
                                b = ni % 2
                                if it2 == 0:
                                    r = v.tensor_copy(
                                        d1[oc][:, ni * 512:(ni + 1) * 512],
                                        pc[b][:])
                                else:
                                    v.tensor_copy(
                                        scr4[:, ni * 512:(ni + 1) * 512],
                                        pc[b][:]).then_inc(s3w, 1)
                                    nvv += 1
                                    v.wait_ge(s3w, nvv)
                                    r = v.tensor_tensor(
                                        d1[oc][:, ni * 512:(ni + 1) * 512],
                                        d1[oc][:, ni * 512:(ni + 1) * 512],
                                        scr4[:, ni * 512:(ni + 1) * 512],
                                        op=ALU.add)
                                r.then_inc(s3v, 1)
                blk.vector(s3_ve)

                def bridge(v):
                    v.wait_ge(s3t, 8 * 112)
                    v.wait_ge(s3v, 8 * 160)
                    v.wait_ge(s3a, 8 * 16)
                    v.wait_ge(s3d, 64 * 16)
                    v.memset(scr4[:, 0:4], 0.0).then_inc(ch.sem["vector"], 1)
                blk.vector(bridge)
                ch.cnt["vector"] += 1
            es_mid.close()
            with contextlib.ExitStack() as es4:
                dtok = es4.enter_context(nc.sbuf_tensor("dtok",
                                                        [128, NTT * D], F32))
                xt = es4.enter_context(nc.sbuf_tensor("xt", [128, NTT * D], F32))
                m = es4.enter_context(nc.sbuf_tensor("m", [128, NTT], F32))
                vv = es4.enter_context(nc.sbuf_tensor("vv", [128, NTT], F32))
                sq = es4.enter_context(nc.sbuf_tensor("sq", [128, D], F32))
                lnb = es4.enter_context(nc.sbuf_tensor("lnb", [128, 4 * D], BF16))
                ch.seg(blk, "gpsimd", lambda g: [
                    g.dma_start(out=xt[:].rearrange("p (n c) -> p n c", c=D),
                                in_=x1.rearrange("(n p) c -> p n c", p=128))])
                for oc in range(8):
                    items = [((lambda n=n, oc=oc: d1[oc][:, n * 128:(n + 1) * 128]),
                              (lambda n=n, oc=oc:
                               dtok[:, n * D + oc * 128:n * D + oc * 128 + 128]))
                             for n in range(NTT)]
                    _tr_wave(ch, blk, items, pst, idb)
                _ln_block(ch, blk, (xt, m, vv, sq, lnb), True, o_x2, o_ln3,
                          None, [lambda i: dtok[:, i * D:(i + 1) * D]])
            ch.final_wait(blk)
    return nc


def build_L7():
    NT = T // 128
    nc = bass.Bass("TRN2", target_bir_lowering=False)
    ln3 = nc.dram_tensor("ln3", [T, D], BF16, kind="ExternalInput").ap()
    wq = nc.dram_tensor("wq", [512, D], F32, kind="ExternalInput").ap()
    wk = nc.dram_tensor("wk", [512, D], F32, kind="ExternalInput").ap()
    wv = nc.dram_tensor("wv", [512, D], F32, kind="ExternalInput").ap()
    rot = nc.dram_tensor("rot", [128, 128], BF16, kind="ExternalInput").ap()
    cosb = nc.dram_tensor("cosb", [128, T], F32, kind="ExternalInput").ap()
    sinb = nc.dram_tensor("sinb", [128, T], F32, kind="ExternalInput").ap()
    ident = nc.dram_tensor("ident", [128, 128], BF16, kind="ExternalInput").ap()
    o_qr = nc.dram_tensor("o_qr", [4, 128, T], BF16, kind="ExternalOutput").ap()
    o_kr = nc.dram_tensor("o_kr", [4, 128, T], BF16, kind="ExternalOutput").ap()
    o_v = nc.dram_tensor("o_v", [T, 512], BF16, kind="ExternalOutput").ap()
    with contextlib.ExitStack() as es:
        lT = [es.enter_context(nc.sbuf_tensor(f"lT{k}", [128, T], BF16))
              for k in range(8)]
        idb = es.enter_context(nc.sbuf_tensor("idb", [128, 128], BF16))
        rotb = es.enter_context(nc.sbuf_tensor("rotb", [128, 128], BF16))
        cosbs = es.enter_context(nc.sbuf_tensor("cosbs", [128, T], F32))
        sinbs = es.enter_context(nc.sbuf_tensor("sinbs", [128, T], F32))
        wqb = es.enter_context(nc.sbuf_tensor("wqb", [128, D], BF16))
        wqT = es.enter_context(nc.sbuf_tensor("wqT", [128, 8 * 128], BF16))
        qT = es.enter_context(nc.sbuf_tensor("qT", [128, T], BF16))
        rq = es.enter_context(nc.sbuf_tensor("rq", [128, T], BF16))
        t1 = es.enter_context(nc.sbuf_tensor("t1", [128, T], BF16))
        wvT = [es.enter_context(nc.sbuf_tensor(f"wvT{k}", [128, 512], BF16))
               for k in range(8)]
        vtk4 = es.enter_context(nc.sbuf_tensor("vtk4", [128, 4 * 512], BF16))
        psm = [es.enter_context(nc.psum_tensor(f"psm{k}", [128, 512], F32))
               for k in range(4)]
        pst = [es.enter_context(nc.psum_tensor(f"pst{k}", [128, 128], BF16))
               for k in range(2)]
        ch = Chain(nc, es)
        with nc.Block() as blk:
            for kc in range(8):
                ch.seg(blk, "sync", lambda sy, kc=kc: [
                    sy.dma_start_transpose(lT[kc][:],
                                           ln3[:, kc * 128:(kc + 1) * 128])])
            ch.seg(blk, "gpsimd", lambda g: [
                g.dma_start(out=idb[:], in_=ident),
                g.dma_start(out=rotb[:], in_=rot),
                g.dma_start(out=cosbs[:], in_=cosb),
                g.dma_start(out=sinbs[:], in_=sinb)])
            for wsrc, odst in ((wq, o_qr), (wk, o_kr)):
                for hp in range(4):
                    ch.seg(blk, "gpsimd", lambda g, wsrc=wsrc, hp=hp: [
                        g.dma_start(out=wqb[:],
                                    in_=wsrc[hp * 128:(hp + 1) * 128, :])])
                    items = [((lambda kc=kc: wqb[:, kc * 128:(kc + 1) * 128]),
                              (lambda kc=kc: wqT[:, kc * 128:(kc + 1) * 128]))
                             for kc in range(8)]
                    _tr_wave(ch, blk, items, pst, idb)
                    for n0 in range(0, 8, 4):
                        def mm(t, n0=n0):
                            for ni in range(4):
                                for kc in range(8):
                                    r = t.matmul(psm[ni][:],
                                                 wqT[:, kc * 128:(kc + 1) * 128],
                                                 lT[kc][:, (n0 + ni) * 512:
                                                         (n0 + ni + 1) * 512],
                                                 start=(kc == 0), stop=(kc == 7))
                            return r
                        ch.seg(blk, "tensor", mm)

                        def ev(v, n0=n0):
                            for ni in range(4):
                                r = v.tensor_copy(
                                    qT[:, (n0 + ni) * 512:(n0 + ni + 1) * 512],
                                    psm[ni][:])
                            return r
                        ch.seg(blk, "vector", ev)
                    for n0 in range(0, 8, 4):
                        def rm(t, n0=n0):
                            for ni in range(4):
                                r = t.matmul(psm[ni][:], rotb[:],
                                             qT[:, (n0 + ni) * 512:(n0 + ni + 1) * 512],
                                             start=True, stop=True)
                            return r
                        ch.seg(blk, "tensor", rm)

                        def re(v, n0=n0):
                            for ni in range(4):
                                r = v.tensor_copy(
                                    rq[:, (n0 + ni) * 512:(n0 + ni + 1) * 512],
                                    psm[ni][:])
                            return r
                        ch.seg(blk, "vector", re)

                    def rope(v):
                        v.tensor_tensor(t1[:], qT[:], cosbs[:], op=ALU.mult)
                        v.tensor_tensor(rq[:], rq[:], sinbs[:], op=ALU.mult)
                        return v.tensor_tensor(t1[:], t1[:], rq[:], op=ALU.add)
                    ch.seg(blk, "vector", rope)
                    ch.seg(blk, "gpsimd", lambda g, odst=odst, hp=hp: [
                        g.dma_start(out=odst[hp, :, :], in_=t1[:])])
            for ro in range(4):
                ch.seg(blk, "gpsimd", lambda g, ro=ro: [
                    g.dma_start(out=wqb[:], in_=wv[ro * 128:(ro + 1) * 128, :])])
                items = [((lambda kc=kc: wqb[:, kc * 128:(kc + 1) * 128]),
                          (lambda kc=kc, ro=ro: wvT[kc][:, ro * 128:(ro + 1) * 128]))
                         for kc in range(8)]
                _tr_wave(ch, blk, items, pst, idb)
            for tc0 in range(0, NT, 4):
                def vm(t, tc0=tc0):
                    for j in range(4):
                        for kc in range(8):
                            t.matmul(psm[j][:],
                                     lT[kc][:, (tc0 + j) * 128:(tc0 + j + 1) * 128],
                                     wvT[kc][:], start=(kc == 0), stop=(kc == 7))
                ch.seg(blk, "tensor", vm)

                def vev(v, tc0=tc0):
                    for j in range(4):
                        v.tensor_copy(vtk4[:, j * 512:(j + 1) * 512], psm[j][:])
                ch.seg(blk, "vector", vev)
                ch.seg(blk, "gpsimd", lambda g, tc0=tc0: [
                    g.dma_start(out=o_v[tc0 * 128:(tc0 + 4) * 128, :]
                                .rearrange("(a p) f -> p a f", p=128),
                                in_=vtk4[:].rearrange("p (a f) -> p a f", f=512))])
            ch.final_wait(blk)
    return nc


def build_L8():
    NT = T // 128
    nc = bass.Bass("TRN2", target_bir_lowering=False)
    qr = nc.dram_tensor("qr", [4, 128, T], BF16, kind="ExternalInput").ap()
    kr = nc.dram_tensor("kr", [4, 128, T], BF16, kind="ExternalInput").ap()
    vt = nc.dram_tensor("vt", [T, 512], BF16, kind="ExternalInput").ap()
    ident = nc.dram_tensor("ident", [128, 128], BF16, kind="ExternalInput").ap()
    o_d = nc.dram_tensor("o_d", [T, 512], BF16, kind="ExternalOutput").ap()
    HPD = 288
    QTE = NT + 8
    HXE = 8 * NT + 1
    with contextlib.ExitStack() as es:
        qs = es.enter_context(nc.sbuf_tensor("qs", [128, T], BF16))
        ks = es.enter_context(nc.sbuf_tensor("ks", [128, T], BF16))
        vp = [es.enter_context(nc.sbuf_tensor(f"vp{j}", [128, NT * 65], BF16))
              for j in range(2)]
        idb = es.enter_context(nc.sbuf_tensor("idb", [128, 128], BF16))
        pa = [[es.enter_context(nc.sbuf_tensor(f"pa{j}_{p}", [128, 512], BF16))
               for p in range(2)] for j in range(2)]
        ou = [es.enter_context(nc.sbuf_tensor(f"ou{j}", [128, 512], BF16))
              for j in range(2)]
        rz = es.enter_context(nc.sbuf_tensor("rz", [128, 1], F32))
        da = es.enter_context(nc.sbuf_tensor("da", [128, 4 * 128], BF16))
        psc = [es.enter_context(nc.psum_tensor(f"psc{j}", [128, 512], F32))
               for j in range(4)]
        psv = [es.enter_context(nc.psum_tensor(f"psv{j}", [128, 512], F32))
               for j in range(2)]
        pst = [es.enter_context(nc.psum_tensor(f"pstA", [128, 128], BF16)),
               es.enter_context(nc.psum_tensor(f"pstB", [128, 128], BF16))]
        s_ld = es.enter_context(nc.semaphore("s_ld"))
        s_st = es.enter_context(nc.semaphore("s_st"))
        s_qk = es.enter_context(nc.semaphore("s_qk"))
        s_ex = es.enter_context(nc.semaphore("s_ex"))
        s_pv = es.enter_context(nc.semaphore("s_pv"))
        s_ep = es.enter_context(nc.semaphore("s_ep"))
        s_vv = es.enter_context(nc.semaphore("s_vv"))
        with nc.Block() as blk:
            def gp(g):
                g.dma_start(out=idb[:], in_=ident).then_inc(s_ld, 16)
                nld = 16
                for hp in range(4):
                    if hp > 0:
                        g.wait_ge(s_pv, hp * 8 * QTE)
                    g.dma_start(out=qs[:], in_=qr[hp, :, :]).then_inc(s_ld, 16)
                    g.dma_start(out=ks[:], in_=kr[hp, :, :]).then_inc(s_ld, 16)
                    for j in range(2):
                        for c0 in range(4):
                            g.dma_start(
                                out=bass.AP(vp[j], c0 * 8 * 65,
                                            [[NT * 65, 128], [65, 8], [1, 64]]),
                                in_=bass.AP(vt.tensor,
                                            c0 * 8 * 128 * 512 + hp * 128 + j * 64,
                                            [[512, 128], [128 * 512, 8], [1, 64]])
                            ).then_inc(s_ld, 16)
                    nld += 160
                    for qt in range(8):
                        it = hp * 8 + qt
                        g.wait_ge(s_ep, it * 9 + 9)
                        g.dma_start(
                            out=bass.AP(o_d.tensor, qt * 512 * 512 + hp * 128,
                                        [[512, 128], [128 * 512, 4], [1, 128]]),
                            in_=da[:].rearrange("p (a f) -> p a f", a=4)
                        ).then_inc(s_st, 16)
                g.wait_ge(s_ld, nld)
                g.wait_ge(s_st, 32 * 16)
            blk.gpsimd(gp)

            def ve(v):
                nvv = 0
                for hp in range(4):
                    v.wait_ge(s_ld, 16 + (hp + 1) * 160)
                    v.memset(bass.AP(vp[0], 64, [[NT * 65, 128], [65, NT]]), 1.0)
                    v.memset(bass.AP(vp[1], 64,
                                     [[NT * 65, 128], [65, NT]]),
                             1.0).then_inc(s_ex, 1)
                    for qt in range(8):
                        it = hp * 8 + qt
                        v.wait_ge(s_pv, it * QTE + NT)
                        v.tensor_copy(ou[0][0:65, :], psv[0][0:65, :])
                        v.tensor_copy(ou[1][0:65, :],
                                      psv[1][0:65, :]).then_inc(s_ep, 1)
                        if it > 0:
                            v.wait_ge(s_st, it * 16)
                        for k in range(8):
                            qsi, j = k // 2, k % 2
                            v.wait_ge(s_pv, it * QTE + NT + k + 1)
                            if k > 0:
                                v.wait_ge(s_ep, it * 9 + 1 + k)
                            v.reciprocal(rz[:],
                                         pst[k % 2][:, 64:65]).then_inc(s_vv, 1)
                            nvv += 1
                            v.wait_ge(s_vv, nvv)
                            v.tensor_scalar(
                                da[:, qsi * 128 + j * 64:
                                   qsi * 128 + j * 64 + 64],
                                pst[k % 2][:, 0:64], rz[:],
                                None, op0=ALU.mult).then_inc(s_ep, 1)
            blk.vector(ve)

            def sc(s):
                for hp in range(4):
                    H = hp * HXE + 1
                    for qt in range(8):
                        it = hp * 8 + qt
                        for c in range(NT):
                            s.wait_ge(s_qk, it * NT + c + 1)
                            if c >= 2:
                                s.wait_ge(s_pv, it * QTE + c - 1)
                            elif it > 0:
                                s.wait_ge(s_pv, (it - 1) * QTE + NT - 1 + c)
                            s.activation(pa[0][c % 2][:], psc[2 * (c % 2)][:],
                                         AF.Exp, scale=0.125)
                            s.activation(pa[1][c % 2][:],
                                         psc[2 * (c % 2) + 1][:],
                                         AF.Exp, scale=0.125).then_inc(s_ex, 1)
            blk.scalar(sc)

            def te(t):
                for hp in range(4):
                    t.wait_ge(s_ld, 16 + (hp + 1) * 160)
                    t.wait_ge(s_ex, hp * HXE + 1)
                    H = hp * HXE + 1
                    for qt in range(8):
                        it = hp * 8 + qt
                        q0 = qt * 512

                        def qk_mm(c):
                            if c >= 2 or qt > 0:
                                t.wait_ge(s_ex, H + qt * NT + c - 1)
                            t.matmul(psc[2 * (c % 2)][:],
                                     ks[0:64, c * 128:(c + 1) * 128],
                                     qs[0:64, q0:q0 + 512],
                                     start=True, stop=True, tile_position=(0, 0))
                            t.matmul(psc[2 * (c % 2) + 1][:],
                                     ks[64:128, c * 128:(c + 1) * 128],
                                     qs[64:128, q0:q0 + 512],
                                     start=True, stop=True,
                                     tile_position=(64, 0)).then_inc(s_qk, 1)

                        def pv_mm(c):
                            t.wait_ge(s_ex, H + qt * NT + c + 1)
                            if c == 0 and it > 0:
                                t.wait_ge(s_ep, (it - 1) * 9 + 1)
                            t.matmul(psv[0][0:65, :],
                                     vp[0][:, c * 65:(c + 1) * 65],
                                     pa[0][c % 2][:],
                                     start=(c == 0), stop=(c == NT - 1))
                            t.matmul(psv[1][0:65, :],
                                     vp[1][:, c * 65:(c + 1) * 65],
                                     pa[1][c % 2][:],
                                     start=(c == 0),
                                     stop=(c == NT - 1)).then_inc(s_pv, 1)
                        qk_mm(0)
                        for c in range(1, NT):
                            qk_mm(c)
                            pv_mm(c - 1)
                        pv_mm(NT - 1)
                        t.wait_ge(s_ep, it * 9 + 1)
                        for k in range(8):
                            qsi, j = k // 2, k % 2
                            if k >= 2:
                                t.wait_ge(s_ep, it * 9 + k)
                            elif it > 0:
                                t.wait_ge(s_ep, (it - 1) * 9 + k + 8)
                            t.transpose(pst[k % 2][:, 0:65],
                                        ou[j][0:65, qsi * 128:(qsi + 1) * 128],
                                        idb[0:65, 0:65]).then_inc(s_pv, 1)
            blk.tensor(te)
    return nc


def build_L9():
    nc = bass.Bass("TRN2", target_bir_lowering=False)
    x2 = nc.dram_tensor("x2", [TPC, D], F32, kind="ExternalInput").ap()
    dat = nc.dram_tensor("dat", [TPC, D], BF16, kind="ExternalInput").ap()
    o_x3 = nc.dram_tensor("o_x3", [TPC, D], F32, kind="ExternalOutput").ap()
    o_ln4 = nc.dram_tensor("o_ln4", [TPC, D], BF16, kind="ExternalOutput").ap()
    with contextlib.ExitStack() as es:
        xt = es.enter_context(nc.sbuf_tensor("xt", [128, NTT * D], F32))
        db = es.enter_context(nc.sbuf_tensor("db", [128, NTT * D], F32))
        m = es.enter_context(nc.sbuf_tensor("m", [128, NTT], F32))
        vv = es.enter_context(nc.sbuf_tensor("vv", [128, NTT], F32))
        sq = es.enter_context(nc.sbuf_tensor("sq", [128, D], F32))
        lnb = es.enter_context(nc.sbuf_tensor("lnb", [128, 4 * D], BF16))
        ch = Chain(nc, es)
        with nc.Block() as blk:
            ch.seg(blk, "gpsimd", lambda g: [
                g.dma_start(out=xt[:].rearrange("p (n c) -> p n c", c=D), in_=x2.rearrange("(n p) c -> p n c", p=128)),
                g.dma_start(out=db[:].rearrange("p (n c) -> p n c", c=D), in_=dat.rearrange("(n p) c -> p n c", p=128))])
            _ln_block(ch, blk, (xt, m, vv, sq, lnb), True, o_x3, o_ln4,
                      None, [lambda i: db[:, i * D:(i + 1) * D]])
            ch.final_wait(blk)
    return nc


def build_L10():
    NA = 2048
    NAT = NA // 128
    nc = bass.Bass("TRN2", target_bir_lowering=False)
    ln4 = nc.dram_tensor("ln4", [NA, D], BF16, kind="ExternalInput").ap()
    w1 = nc.dram_tensor("w1", [D, 2048], F32, kind="ExternalInput").ap()
    b1 = nc.dram_tensor("b1", [128, 2048], F32, kind="ExternalInput").ap()
    w2 = nc.dram_tensor("w2", [D, D], F32, kind="ExternalInput").ap()
    b2 = nc.dram_tensor("b2", [128, 8], F32, kind="ExternalInput").ap()
    ri = nc.dram_tensor("ri", [NA, 2], I32, kind="ExternalInput").ap()
    rw = nc.dram_tensor("rw", [128, 16], F32, kind="ExternalInput").ap()
    ev_ = nc.dram_tensor("ev", [128, 1], F32, kind="ExternalInput").ap()
    ident = nc.dram_tensor("ident", [128, 128], BF16, kind="ExternalInput").ap()
    o_p = nc.dram_tensor("o_p", [NA, D], BF16, kind="ExternalOutput").ap()
    with contextlib.ExitStack() as es:
        lT = [es.enter_context(nc.sbuf_tensor(f"lT{k}", [128, NA], BF16))
              for k in range(8)]
        idb = es.enter_context(nc.sbuf_tensor("idb", [128, 128], BF16))
        w1b = [es.enter_context(nc.sbuf_tensor(f"w1b{k}", [128, 2048], BF16))
               for k in range(8)]
        b1b = es.enter_context(nc.sbuf_tensor("b1b", [128, 2048], F32))
        w2b = [es.enter_context(nc.sbuf_tensor(f"w2b{k}", [128, D], BF16))
               for k in range(8)]
        b2b = es.enter_context(nc.sbuf_tensor("b2b", [128, 8], F32))
        idxb = es.enter_context(nc.sbuf_tensor("idxb", [128, 32], I32))
        idxf = es.enter_context(nc.sbuf_tensor("idxf", [128, 32], F32))
        evb = es.enter_context(nc.sbuf_tensor("evb", [128, 1], F32))
        rwb = es.enter_context(nc.sbuf_tensor("rwb", [128, 16], F32))
        wcol = es.enter_context(nc.sbuf_tensor("wcol", [128, 16], F32))
        eq = es.enter_context(nc.sbuf_tensor("eq", [128, 32], F32))
        gc = es.enter_context(nc.sbuf_tensor("gc", [128, 1024], F32))
        sg = es.enter_context(nc.sbuf_tensor("sg", [128, 1024], F32))
        uc = es.enter_context(nc.sbuf_tensor("uc", [128, 1024], F32))
        inter = es.enter_context(nc.sbuf_tensor("inter", [128, D], BF16))
        iT = [es.enter_context(nc.sbuf_tensor(f"iT{k}", [128, NA], BF16))
              for k in range(8)]
        dTo = [es.enter_context(nc.sbuf_tensor(f"dTo{k}", [128, NA], BF16))
               for k in range(8)]
        dtok = es.enter_context(nc.sbuf_tensor("dtok", [128, D], BF16))
        psm = [es.enter_context(nc.psum_tensor(f"psm{k}", [128, 512], F32))
               for k in range(4)]
        pst = [es.enter_context(nc.psum_tensor(f"pst{k}", [128, 128], BF16))
               for k in range(2)]
        ch = Chain(nc, es)
        with nc.Block() as blk:
            for kc in range(8):
                ch.seg(blk, "sync", lambda sy, kc=kc: [
                    sy.dma_start_transpose(lT[kc][:],
                                           ln4[:, kc * 128:(kc + 1) * 128])])
            ch.seg(blk, "gpsimd", lambda g: [
                g.dma_start(out=idb[:], in_=ident),
                g.dma_start(out=b1b[:], in_=b1),
                g.dma_start(out=b2b[:], in_=b2),
                g.dma_start(out=evb[:], in_=ev_),
                g.dma_start(out=idxb[:].rearrange("p (c k) -> p c k", k=2),
                            in_=ri.rearrange("(c p) k -> p c k", p=128)),
                g.dma_start(out=rwb[:], in_=rw)] + [
                g.dma_start(out=w1b[k][:], in_=w1[k * 128:(k + 1) * 128, :])
                for k in range(8)] + [
                g.dma_start(out=w2b[k][:], in_=w2[k * 128:(k + 1) * 128, :])
                for k in range(8)])

            def wcol_f(v):
                v.tensor_copy(idxf[:], idxb[:])
                v.tensor_scalar(eq[:, 0:16],
                                bass.AP(idxf, 0, [[32, 128], [2, 16]]),
                                evb[:], None, op0=ALU.is_equal)
                v.tensor_scalar(eq[:, 16:32],
                                bass.AP(idxf, 1, [[32, 128], [2, 16]]),
                                evb[:], None, op0=ALU.is_equal)
                v.tensor_tensor(wcol[:], eq[:, 0:16], eq[:, 16:32], op=ALU.add)
                return v.tensor_tensor(wcol[:], wcol[:], rwb[:], op=ALU.mult)
            ch.seg(blk, "vector", wcol_f)
            for tc in range(NAT):
                def mm(t, tc=tc):
                    for o4 in range(4):
                        for kc in range(8):
                            t.matmul(psm[o4][:],
                                     lT[kc][:, tc * 128:(tc + 1) * 128],
                                     w1b[kc][:, o4 * 512:(o4 + 1) * 512],
                                     start=(kc == 0), stop=(kc == 7))
                ch.seg(blk, "tensor", mm)

                def swi(v):
                    for o4 in range(4):
                        v.tensor_tensor(psm[o4][:], psm[o4][:],
                                        b1b[:, o4 * 512:(o4 + 1) * 512], op=ALU.add)
                        v.tensor_scalar(gc[:, o4 * 256:(o4 + 1) * 256],
                                        bass.AP(psm[o4], 0, [[512, 128], [2, 256]]),
                                        7.0, None, op0=ALU.min)
                        v.tensor_scalar(uc[:, o4 * 256:(o4 + 1) * 256],
                                        bass.AP(psm[o4], 1, [[512, 128], [2, 256]]),
                                        7.0, -7.0, op0=ALU.min, op1=ALU.max)
                ch.seg(blk, "vector", swi)
                ch.seg(blk, "scalar",
                       lambda s: s.activation(sg[:], gc[:], AF.Sigmoid,
                                              scale=1.702))

                def glu(v):
                    v.tensor_tensor(gc[:], gc[:], sg[:], op=ALU.mult)
                    v.scalar_tensor_tensor(inter[:], uc[:], 1.0, gc[:],
                                           op0=ALU.add, op1=ALU.mult)
                ch.seg(blk, "vector", glu)
                items = [((lambda ic=ic: inter[:, ic * 128:(ic + 1) * 128]),
                          (lambda ic=ic, tc=tc: iT[ic][:, tc * 128:(tc + 1) * 128]))
                         for ic in range(8)]
                _tr_wave(ch, blk, items, pst, idb)
            for oc in range(8):
                def dm(t, oc=oc):
                    for ni in range(4):
                        for kc in range(8):
                            r = t.matmul(psm[ni][:],
                                         w2b[kc][:, oc * 128:(oc + 1) * 128],
                                         iT[kc][:, ni * 512:(ni + 1) * 512],
                                         start=(kc == 0), stop=(kc == 7))
                    return r
                ch.seg(blk, "tensor", dm)

                def ev2(v, oc=oc):
                    for ni in range(4):
                        r = v.tensor_scalar(dTo[oc][:, ni * 512:(ni + 1) * 512],
                                            psm[ni][:], b2b[:, oc:oc + 1], None,
                                            op0=ALU.add)
                    return r
                ch.seg(blk, "vector", ev2)
            for tc in range(NAT):
                items = [((lambda oc=oc, tc=tc: dTo[oc][:, tc * 128:(tc + 1) * 128]),
                          (lambda oc=oc: dtok[:, oc * 128:(oc + 1) * 128]))
                         for oc in range(8)]
                _tr_wave(ch, blk, items, pst, idb)

                def wmul(v, tc=tc):
                    return v.tensor_scalar(dtok[:], dtok[:], wcol[:, tc:tc + 1],
                                           None, op0=ALU.mult)
                ch.seg(blk, "vector", wmul)
                ch.seg(blk, "gpsimd", lambda g, tc=tc: [
                    g.dma_start(out=o_p[tc * 128:(tc + 1) * 128, :], in_=dtok[:])])
            ch.final_wait(blk)
    return nc


def build_L11():
    R = 2048 // NC
    nc = bass.Bass("TRN2", target_bir_lowering=False)
    x3 = nc.dram_tensor("x3", [R, D], F32, kind="ExternalInput").ap()
    ps = [nc.dram_tensor(f"p{e}", [R, D], BF16, kind="ExternalInput").ap()
          for e in range(8)]
    o = nc.dram_tensor("o", [R, D], F32, kind="ExternalOutput").ap()
    with contextlib.ExitStack() as es:
        xb = es.enter_context(nc.sbuf_tensor("xb", [128, (R // 128) * D], F32))
        pb = es.enter_context(nc.sbuf_tensor("pb", [128, (R // 128) * D], F32))
        ch = Chain(nc, es)
        with nc.Block() as blk:
            ch.seg(blk, "gpsimd", lambda g: [
                g.dma_start(out=xb[:].rearrange("p (n c) -> p n c", c=D), in_=x3.rearrange("(n p) c -> p n c", p=128))])
            for e in range(8):
                ch.seg(blk, "gpsimd", lambda g, e=e: [
                    g.dma_start(out=pb[:].rearrange("p (n c) -> p n c", c=D),
                                in_=ps[e].rearrange("(n p) c -> p n c", p=128))])
                ch.seg(blk, "vector",
                       lambda v: v.tensor_tensor(xb[:], xb[:], pb[:], op=ALU.add))
            ch.seg(blk, "gpsimd", lambda g: [
                g.dma_start(out=o.rearrange("(n p) c -> p n c", p=128), in_=xb[:].rearrange("p (n c) -> p n c", c=D))])
            ch.final_wait(blk)
    return nc


def _cat(outs, name, axis):
    return np.concatenate([o[name] for o in outs], axis=axis)


_BUILD_CACHE = {}


def _run(key, build_fn, in_maps):
    if key not in _BUILD_CACHE:
        _BUILD_CACHE[key] = build_fn()
    return run_bass_kernel_spmd(_BUILD_CACHE[key], in_maps,
                                core_ids=list(range(NC))).results


def _consts():
    ident = np.eye(128, dtype=np.float32).astype(NBF)
    mask = np.zeros((128, 128), np.float32)
    for b0 in range(8):
        mask[b0 * 16:(b0 + 1) * 16, b0 * 16:(b0 + 1) * 16] = 1.0
    mask = mask.astype(NBF)
    P = np.zeros((128, 128), np.float32)
    for i in range(64):
        P[2 * i, 2 * i + 1] = -1.0
        P[2 * i + 1, 2 * i] = 1.0
    rotT = P.T.copy().astype(NBF)
    pos = np.arange(T, dtype=np.float64)
    inv = np.power(10000.0, -2.0 * np.arange(32, dtype=np.float64) / 64)
    ang = pos[:, None] * inv[None, :]
    cos = np.repeat(np.cos(ang), 2, axis=1).T
    sin = np.repeat(np.sin(ang), 2, axis=1).T
    cosb = np.concatenate([cos, cos], 0).astype(np.float32)
    sinb = np.concatenate([sin, sin], 0).astype(np.float32)
    return ident, mask, rotT, cosb, sinb


def kernel(**inp):
    ident, mask, rotT, cosb, sinb = _consts()
    x = np.ascontiguousarray(inp["L0_attn_x"].reshape(B * T, D).astype(np.float32))

    nwrep = np.ascontiguousarray(
        np.broadcast_to(inp["L0_attn_norm_w"][None, :], (128, D)).astype(np.float32))
    o1 = _run("L1", build_L1, [
        {"x": x[c * TPC:(c + 1) * TPC], "nwrep": nwrep} for c in range(NC)])
    ln = _cat(o1, "o_ln", 0)
    xn = _cat(o1, "o_xn", 0)

    in_w = np.ascontiguousarray(inp["L0_attn_in_w"].astype(np.float32))
    in_b = np.ascontiguousarray(inp["L0_attn_in_b"].reshape(16, 128).T)
    conv_w = np.ascontiguousarray(
        inp["L0_attn_conv_w"].reshape(4, 8, 128).transpose(2, 1, 0).reshape(128, 32))
    conv_b = np.ascontiguousarray(inp["L0_attn_conv_b"].reshape(8, 128).T)
    zero_lb = np.zeros((128, D), NBF)
    maps = []
    for c in range(NC):
        s0 = c * TPC
        lb = zero_lb if (c % 2 == 0) else np.ascontiguousarray(xn[s0 - 128:s0])
        zfv = np.zeros((128, 1), np.float32) if c % 2 == 0 else \
            np.ones((128, 1), np.float32)
        maps.append({"xn": np.concatenate([lb, xn[s0:s0 + TPC]], 0),
                     "w": in_w, "bi": in_b, "cw": conv_w, "cb": conv_b,
                     "zf": zfv, "ident": ident})
    o2 = _run("L2", build_L2, maps)
    uT = _cat(o2, "o_u", 1)
    gT = _cat(o2, "o_g", 1)

    pw = np.ascontiguousarray(inp["L0_attn_param_w"].astype(np.float32))
    pb = np.ascontiguousarray(inp["L0_attn_param_b"].reshape(24, 128).T)
    o3 = _run("L3", build_L3, [
        {"u": np.ascontiguousarray(uT[:, c * TPC:(c + 1) * TPC]),
         "pw": pw, "pb": pb} for c in range(NC)])
    aT = _cat(o3, "o_a", 1)
    wT = _cat(o3, "o_w", 1)
    cT = _cat(o3, "o_c", 1)

    skip_r = np.ascontiguousarray(inp["L0_attn_skip"].reshape(8, 128))
    maps = []
    for c in range(NC):
        b, hh = c // 2, c % 2
        rs = slice(hh * 512, (hh + 1) * 512)
        cs = slice(b * T, (b + 1) * T)
        maps.append({"a": np.ascontiguousarray(aT[rs, cs]),
                     "w": np.ascontiguousarray(wT[rs, cs]),
                     "c": np.ascontiguousarray(cT[rs, cs]),
                     "u": np.ascontiguousarray(uT[rs, cs]),
                     "g": np.ascontiguousarray(gT[rs, cs]),
                     "skp": np.ascontiguousarray(skip_r[hh * 4:(hh + 1) * 4].T)})
    o4 = _run("L4", build_L4, maps)
    yT = np.empty((D, B * T), NBF)
    for c in range(NC):
        b, hh = c // 2, c % 2
        yT[hh * 512:(hh + 1) * 512, b * T:(b + 1) * T] = o4[c]["o_y"]

    ow = np.ascontiguousarray(inp["L0_attn_out_w"].astype(np.float32))
    ob = np.ascontiguousarray(inp["L0_attn_out_b"].reshape(8, 128).T)
    o5 = _run("L5", build_L5, [
        {"y": np.ascontiguousarray(yT[:, c * TPC:(c + 1) * TPC]),
         "ow": ow, "ob": ob, "x": x[c * TPC:(c + 1) * TPC],
         "ln": ln[c * TPC:(c + 1) * TPC], "ident": ident} for c in range(NC)])
    x1 = _cat(o5, "o_x1", 0)
    ln2 = _cat(o5, "o_ln2", 0)

    o6 = _run("L6", build_L6, [
        {"ln2": ln2[c * TPC:(c + 1) * TPC], "mk": inp["L0_ffn_rail_maker"],
         "up": inp["L0_ffn_rail_up_proj"], "wc": inp["L0_ffn_w_cproj"],
         "msk": mask, "ident": ident, "x1": x1[c * TPC:(c + 1) * TPC]}
        for c in range(NC)])
    x2 = _cat(o6, "o_x2", 0)
    ln3 = _cat(o6, "o_ln3", 0)

    maps = []
    for c in range(NC):
        b, hh = c // 2, c % 2
        maps.append({"ln3": ln3[b * T:(b + 1) * T],
                     "wq": np.ascontiguousarray(
                         inp["L1_attn_Wq"][hh * 512:(hh + 1) * 512].astype(np.float32)),
                     "wk": np.ascontiguousarray(
                         inp["L1_attn_Wk"][hh * 512:(hh + 1) * 512].astype(np.float32)),
                     "wv": np.ascontiguousarray(
                         inp["L1_attn_Wv"][hh * 512:(hh + 1) * 512].astype(np.float32)),
                     "rot": rotT, "cosb": cosb, "sinb": sinb, "ident": ident})
    o7 = _run("L7", build_L7, maps)

    o8 = _run("L8", build_L8, [
        {"qr": o7[c]["o_qr"], "kr": o7[c]["o_kr"], "vt": o7[c]["o_v"],
         "ident": ident} for c in range(NC)])
    datt = np.empty((B * T, D), NBF)
    for c in range(NC):
        b, hh = c // 2, c % 2
        datt[b * T:(b + 1) * T, hh * 512:(hh + 1) * 512] = o8[c]["o_d"]

    o9 = _run("L9", build_L9, [
        {"x2": x2[c * TPC:(c + 1) * TPC],
         "dat": np.ascontiguousarray(datt[c * TPC:(c + 1) * TPC])}
        for c in range(NC)])
    x3 = _cat(o9, "o_x3", 0)
    ln4a = o9[0]["o_ln4"]

    NA = 2048
    b1r = np.ascontiguousarray(
        np.broadcast_to(inp["L1_ffn_gate_up_proj_bias"][:, None, :],
                        (8, 128, 2048)).astype(np.float32))
    maps = []
    for e in range(NC):
        maps.append({"ln4": ln4a,
                     "w1": np.ascontiguousarray(
                         inp["L1_ffn_gate_up_proj"][e].astype(np.float32)),
                     "b1": b1r[e],
                     "w2": np.ascontiguousarray(
                         inp["L1_ffn_down_proj"][e].astype(np.float32)),
                     "b2": np.ascontiguousarray(
                         inp["L1_ffn_down_proj_bias"][e].reshape(8, 128).T),
                     "ri": inp["L1_ffn_router_indices"],
                     "rw": np.ascontiguousarray(
                         inp["L1_ffn_routing_weights"][:, e].reshape(16, 128).T),
                     "ev": np.full((128, 1), float(e), np.float32),
                     "ident": ident})
    o10 = _run("L10", build_L10, maps)

    R = NA // NC
    o11 = _run("L11", build_L11, [
        dict({"x3": x3[c * R:(c + 1) * R]},
             **{f"p{e}": np.ascontiguousarray(o10[e]["o_p"][c * R:(c + 1) * R])
                for e in range(8)}) for c in range(NC)])
    fin_active = _cat(o11, "o", 0)

    out = x3.copy()
    out[0:NA] = fin_active
    return out.reshape(B, T, D)


# version 12
# speedup vs baseline: 1.0191x; 1.0033x over previous
import contextlib
import numpy as np
import ml_dtypes
import concourse.bass as bass
import concourse.mybir as mybir
from concourse.bass_utils import run_bass_kernel_spmd

F32 = mybir.dt.float32
BF16 = mybir.dt.bfloat16
I32 = mybir.dt.int32
AF = mybir.ActivationFunctionType
ALU = mybir.AluOpType
AX = mybir.AxisListType
NBF = ml_dtypes.bfloat16

B, T, D = 4, 4096, 1024
NC = 8
TPC = 2048
EXT = TPC + 128
NTT = TPC // 128


_DMA_OPS = {"dma_start", "dma_start_transpose"}


class _ChainedEngine:

    def __init__(self, eng, chain, kind):
        self._eng = eng
        self._ch = chain
        self._kind = kind

    def __getattr__(self, name):
        f = getattr(self._eng, name)
        if not callable(f):
            return f

        def wrapped(*a, **k):
            ch = self._ch
            kind = self._kind
            if kind in ("vector", "scalar"):
                sem, cnt = ch.sem[kind], ch.cnt[kind]
                if cnt > ch.seg_base[kind]:
                    self._eng.wait_ge(sem, cnt)
            r = f(*a, **k)
            if name in _DMA_OPS:
                dk = kind if kind == "hdma" else "dma"
                r.then_inc(ch.sem[dk], 16)
                ch.cnt[dk] += 16
            else:
                r.then_inc(ch.sem[kind], 1)
                ch.cnt[kind] += 1
            return r
        return wrapped


class Chain:

    def __init__(self, nc, es):
        self.sem = {
            "dma": es.enter_context(nc.semaphore("chd")),
            "hdma": es.enter_context(nc.semaphore("chh")),
            "tensor": es.enter_context(nc.semaphore("cht")),
            "vector": es.enter_context(nc.semaphore("chv")),
            "scalar": es.enter_context(nc.semaphore("chs")),
        }
        self.cnt = {"dma": 0, "hdma": 0, "tensor": 0, "vector": 0, "scalar": 0}
        self.seg_base = dict(self.cnt)

    def seg(self, block, engname, fn):
        ch = self
        kind = ("dma" if engname == "gpsimd" else
                "hdma" if engname == "sync" else engname)
        waits = {k: v for k, v in ch.cnt.items() if v}

        def section(eng):
            for k, v in waits.items():
                eng.wait_ge(ch.sem[k], v)
            ch.seg_base = dict(ch.cnt)
            p = _ChainedEngine(eng, ch, kind)
            if kind in ("dma", "hdma"):
                for _ in fn(p):
                    pass
            else:
                fn(p)
        getattr(block, engname)(section)

    def final_wait(self, block):
        ch = self

        def section(eng):
            for k, v in ch.cnt.items():
                if v:
                    eng.wait_ge(ch.sem[k], v)
        block.gpsimd(section)


def _psum_views(handle, n, w):
    class _V:
        def __init__(self, off):
            self.off = off

        def __getitem__(self, idx):
            if not isinstance(idx, tuple):
                return handle[:, self.off:self.off + w]
            rows, cols = idx
            cs = cols.start or 0
            ce = cols.stop if cols.stop is not None else w
            return handle[rows, self.off + cs:self.off + ce]
    return [_V(i * w) for i in range(n)]


def _tr_wave(ch, blk, items, pst, idb):
    W = len(pst)
    for w0 in range(0, len(items), W):
        wave = items[w0:w0 + W]

        def tseg(t, wave=wave):
            for j, (srcf, _) in enumerate(wave):
                r = t.transpose(pst[j][:], srcf(), idb[:])
            return r
        ch.seg(blk, "tensor", tseg)

        def vseg(v, wave=wave):
            for j, (_, dstf) in enumerate(wave):
                r = v.tensor_copy(dstf(), pst[j][:])
            return r
        ch.seg(blk, "vector", vseg)


def _ln_block(ch, blk, nc_objs, with_out_x, o_x, o_ln, x_in_tiles, add_tiles):
    xt, m, vv, sq, lnb = nc_objs

    def resid(v):
        for i in range(NTT):
            xs = xt[:, i * D:(i + 1) * D]
            for af in add_tiles:
                v.tensor_tensor(xs, xs, af(i), op=ALU.add)
            v.reduce_sum(m[:, i:i + 1], xs, axis=AX.X)
            v.tensor_scalar(m[:, i:i + 1], m[:, i:i + 1], 1.0 / D, None,
                            op0=ALU.mult)
        r = v.tensor_copy(m[:, 0:1], m[:, 0:1])
        return r
    ch.seg(blk, "vector", resid)
    if with_out_x:
        ch.seg(blk, "gpsimd", lambda g: [
            g.dma_start(out=o_x.rearrange("(n p) c -> p n c", p=128), in_=xt[:].rearrange("p (n c) -> p n c", c=D))])

    def center(v):
        for i in range(NTT):
            xs = xt[:, i * D:(i + 1) * D]
            v.tensor_scalar(xs, xs, m[:, i:i + 1], None, op0=ALU.subtract)
            v.tensor_tensor(sq[:], xs, xs, op=ALU.mult)
            v.reduce_sum(vv[:, i:i + 1], sq[:], axis=AX.X)
            r = v.tensor_scalar(vv[:, i:i + 1], vv[:, i:i + 1], 1.0 / D, None,
                                op0=ALU.mult)
        return r
    ch.seg(blk, "vector", center)

    def epsadd(v):
        return v.tensor_scalar(vv[:], vv[:], 1e-5, None, op0=ALU.add)
    ch.seg(blk, "vector", epsadd)

    def rstds(s):
        s.activation(vv[:], vv[:], AF.Ln)
        return s.activation(vv[:], vv[:], AF.Exp, scale=-0.5)
    ch.seg(blk, "scalar", rstds)
    for i0 in range(0, NTT, 4):
        def lnw(v, i0=i0):
            for j in range(4):
                i = i0 + j
                v.tensor_scalar(lnb[:, j * D:(j + 1) * D],
                                xt[:, i * D:(i + 1) * D],
                                vv[:, i:i + 1], None, op0=ALU.mult)
        ch.seg(blk, "vector", lnw)
        ch.seg(blk, "gpsimd", lambda g, i0=i0: [
            g.dma_start(out=o_ln[i0 * 128:(i0 + 4) * 128, :]
                        .rearrange("(n p) c -> p n c", p=128),
                        in_=lnb[:].rearrange("p (n c) -> p n c", c=D))])


def build_L1():
    nc = bass.Bass("TRN2", target_bir_lowering=False)
    x = nc.dram_tensor("x", [TPC, D], F32, kind="ExternalInput").ap()
    nwrep = nc.dram_tensor("nwrep", [128, D], F32, kind="ExternalInput").ap()
    o_ln = nc.dram_tensor("o_ln", [TPC, D], F32, kind="ExternalOutput").ap()
    o_xn = nc.dram_tensor("o_xn", [TPC, D], BF16, kind="ExternalOutput").ap()
    with contextlib.ExitStack() as es:
        xt = es.enter_context(nc.sbuf_tensor("xt", [128, NTT * D], F32))
        nw = es.enter_context(nc.sbuf_tensor("nw", [128, D], F32))
        m = es.enter_context(nc.sbuf_tensor("m", [128, NTT], F32))
        vv = es.enter_context(nc.sbuf_tensor("vv", [128, NTT], F32))
        rstd = es.enter_context(nc.sbuf_tensor("rstd", [128, NTT], F32))
        tmp = es.enter_context(nc.sbuf_tensor("tmp", [128, NTT], F32))
        lnt = es.enter_context(nc.sbuf_tensor("lnt", [128, D], F32))
        xnt = es.enter_context(nc.sbuf_tensor("xnt", [128, D], BF16))
        sq = es.enter_context(nc.sbuf_tensor("sq", [128, D], F32))
        ch = Chain(nc, es)
        with nc.Block() as blk:
            ch.seg(blk, "gpsimd", lambda g: [
                g.dma_start(out=xt[:].rearrange("p (n c) -> p n c", c=D), in_=x.rearrange("(n p) c -> p n c", p=128)),
                g.dma_start(out=nw[:], in_=nwrep)])

            def stats(v):
                for i in range(NTT):
                    sl = xt[:, i * D:(i + 1) * D]
                    v.reduce_sum(m[:, i:i + 1], sl, axis=AX.X)
                    v.tensor_scalar(m[:, i:i + 1], m[:, i:i + 1], 1.0 / D, None,
                                    op0=ALU.mult)
                    v.tensor_scalar(sl, sl, m[:, i:i + 1], None, op0=ALU.subtract)
                    v.tensor_tensor(sq[:], sl, sl, op=ALU.mult)
                    v.reduce_sum(vv[:, i:i + 1], sq[:], axis=AX.X)
                    r = v.tensor_scalar(vv[:, i:i + 1], vv[:, i:i + 1], 1.0 / D,
                                        None, op0=ALU.mult)
                return r
            ch.seg(blk, "vector", stats)

            def epsadd1(v):
                return v.tensor_scalar(rstd[:], vv[:], 1e-5, None, op0=ALU.add)
            ch.seg(blk, "vector", epsadd1)

            def rstds(s):
                s.activation(rstd[:], rstd[:], AF.Ln)
                return s.activation(rstd[:], rstd[:], AF.Exp, scale=-0.5)
            ch.seg(blk, "scalar", rstds)

            def q(v):
                v.tensor_tensor(tmp[:], rstd[:], rstd[:], op=ALU.mult)
                return v.tensor_tensor(tmp[:], tmp[:], vv[:], op=ALU.mult)
            ch.seg(blk, "vector", q)

            def epsadd2(v):
                return v.tensor_scalar(tmp[:], tmp[:], 1e-6, None, op0=ALU.add)
            ch.seg(blk, "vector", epsadd2)

            def rs2(s):
                s.activation(tmp[:], tmp[:], AF.Ln)
                return s.activation(tmp[:], tmp[:], AF.Exp, scale=-0.5)
            ch.seg(blk, "scalar", rs2)

            def fin(v):
                return v.tensor_tensor(tmp[:], rstd[:], tmp[:], op=ALU.mult)
            ch.seg(blk, "vector", fin)
            for i in range(NTT):
                def wr(v, i=i):
                    xc = xt[:, i * D:(i + 1) * D]
                    v.tensor_scalar(lnt[:], xc, rstd[:, i:i + 1], None, op0=ALU.mult)
                    v.tensor_scalar(xc, xc, tmp[:, i:i + 1], None, op0=ALU.mult)
                    return v.tensor_tensor(xnt[:], xc, nw[:], op=ALU.mult)
                ch.seg(blk, "vector", wr)
                ch.seg(blk, "gpsimd", lambda g, i=i: [
                    g.dma_start(out=o_ln[i * 128:(i + 1) * 128, :], in_=lnt[:]),
                    g.dma_start(out=o_xn[i * 128:(i + 1) * 128, :], in_=xnt[:])])
            ch.final_wait(blk)
    return nc


def build_L2():
    NE = EXT // 128
    nc = bass.Bass("TRN2", target_bir_lowering=False)
    xn = nc.dram_tensor("xn", [EXT, D], BF16, kind="ExternalInput").ap()
    w = nc.dram_tensor("w", [D, 2 * D], F32, kind="ExternalInput").ap()
    bi = nc.dram_tensor("bi", [128, 16], F32, kind="ExternalInput").ap()
    cw = nc.dram_tensor("cw", [128, 32], F32, kind="ExternalInput").ap()
    cb = nc.dram_tensor("cb", [128, 8], F32, kind="ExternalInput").ap()
    zf = nc.dram_tensor("zf", [128, 1], F32, kind="ExternalInput").ap()
    ident = nc.dram_tensor("ident", [128, 128], BF16, kind="ExternalInput").ap()
    o_u = nc.dram_tensor("o_u", [D, TPC], F32, kind="ExternalOutput").ap()
    o_g = nc.dram_tensor("o_g", [D, TPC], BF16, kind="ExternalOutput").ap()
    NT5 = [(0, 512), (512, 512), (1024, 512), (1536, 512), (2048, 128)]
    with contextlib.ExitStack() as es:
        xT = [es.enter_context(nc.sbuf_tensor(f"xT{k}", [128, EXT], BF16))
              for k in range(8)]
        wsb = [es.enter_context(nc.sbuf_tensor(f"wsb{k}", [128, 2 * D], BF16))
               for k in range(8)]
        idb = es.enter_context(nc.sbuf_tensor("idb", [128, 128], BF16))
        bib = es.enter_context(nc.sbuf_tensor("bib", [128, 16], F32))
        cwb = es.enter_context(nc.sbuf_tensor("cwb", [128, 32], F32))
        cbb = es.enter_context(nc.sbuf_tensor("cbb", [128, 8], F32))
        zfb = es.enter_context(nc.sbuf_tensor("zfb", [128, 1], F32))
        uraw = es.enter_context(nc.sbuf_tensor("uraw", [128, EXT], F32))
        ucv = es.enter_context(nc.sbuf_tensor("ucv", [128, TPC], F32))
        usil = es.enter_context(nc.sbuf_tensor("usil", [128, TPC], F32))
        gsil = es.enter_context(nc.sbuf_tensor("gsil", [128, TPC], BF16))
        pst = [es.enter_context(nc.psum_tensor(f"pst{k}", [128, 128], BF16))
               for k in range(4)]
        psm = [es.enter_context(nc.psum_tensor(f"psm{k}", [128, 512], F32))
               for k in range(3)]
        ch = Chain(nc, es)
        with nc.Block() as blk:
            for kc in range(8):
                ch.seg(blk, "sync", lambda sy, kc=kc: [
                    sy.dma_start_transpose(xT[kc][:],
                                           xn[:, kc * 128:(kc + 1) * 128])])
            ch.seg(blk, "gpsimd", lambda g: [
                g.dma_start(out=idb[:], in_=ident),
                g.dma_start(out=bib[:], in_=bi),
                g.dma_start(out=cwb[:], in_=cw),
                g.dma_start(out=cbb[:], in_=cb),
                g.dma_start(out=zfb[:], in_=zf)] + [
                g.dma_start(out=wsb[k][:], in_=w[k * 128:(k + 1) * 128, :])
                for k in range(8)])

            for oc in range(16):
                for w0 in range(0, 5, 3):
                    sub = list(enumerate(NT5))[w0:w0 + 3]

                    def mseg(t, oc=oc, sub=sub):
                        for ni, (n0, nn) in sub:
                            for kc in range(8):
                                r = t.matmul(psm[ni - sub[0][0]][0:128, 0:nn],
                                             wsb[kc][:, oc * 128:(oc + 1) * 128],
                                             xT[kc][:, n0:n0 + nn],
                                             start=(kc == 0), stop=(kc == 7))
                        return r
                    ch.seg(blk, "tensor", mseg)

                    def eseg(v, oc=oc, sub=sub):
                        for ni, (n0, nn) in sub:
                            r = v.tensor_scalar(uraw[:, n0:n0 + nn],
                                                psm[ni - sub[0][0]][0:128, 0:nn],
                                                bib[:, oc:oc + 1], None, op0=ALU.add)
                        return r
                    ch.seg(blk, "vector", eseg)
                if oc < 8:
                    def conv(v, oc=oc):
                        v.tensor_scalar(uraw[:, 0:128], uraw[:, 0:128], zfb[:, 0:1],
                                        None, op0=ALU.mult)
                        v.tensor_scalar(ucv[:], uraw[:, 125:125 + TPC],
                                        cwb[:, oc * 4:oc * 4 + 1],
                                        cbb[:, oc:oc + 1], op0=ALU.mult, op1=ALU.add)
                        for k in range(1, 4):
                            r = v.scalar_tensor_tensor(
                                ucv[:], uraw[:, 125 + k:125 + k + TPC],
                                cwb[:, oc * 4 + k:oc * 4 + k + 1], ucv[:],
                                op0=ALU.mult, op1=ALU.add)
                        return r
                    ch.seg(blk, "vector", conv)
                    ch.seg(blk, "scalar",
                           lambda s: s.activation(usil[:], ucv[:], AF.Silu))
                    ch.seg(blk, "gpsimd", lambda g, oc=oc: [
                        g.dma_start(out=o_u[oc * 128:(oc + 1) * 128, :], in_=usil[:])])
                else:
                    ch.seg(blk, "scalar",
                           lambda s: s.activation(gsil[:], uraw[:, 128:EXT], AF.Silu))
                    ch.seg(blk, "gpsimd", lambda g, oc=oc: [
                        g.dma_start(out=o_g[(oc - 8) * 128:(oc - 7) * 128, :],
                                    in_=gsil[:])])
            ch.final_wait(blk)
    return nc


def build_L3():
    nc = bass.Bass("TRN2", target_bir_lowering=False)
    u = nc.dram_tensor("u", [D, TPC], F32, kind="ExternalInput").ap()
    pw = nc.dram_tensor("pw", [D, 3 * D], F32, kind="ExternalInput").ap()
    pb = nc.dram_tensor("pb", [128, 24], F32, kind="ExternalInput").ap()
    o_a = nc.dram_tensor("o_a", [D, TPC], F32, kind="ExternalOutput").ap()
    o_w = nc.dram_tensor("o_w", [D, TPC], F32, kind="ExternalOutput").ap()
    o_c = nc.dram_tensor("o_c", [D, TPC], BF16, kind="ExternalOutput").ap()
    with contextlib.ExitStack() as es:
        uf = [es.enter_context(nc.sbuf_tensor(f"uf{k}", [128, TPC], F32))
              for k in range(8)]
        ub = [es.enter_context(nc.sbuf_tensor(f"ub{k}", [128, TPC], BF16))
              for k in range(8)]
        wsb = es.enter_context(nc.sbuf_tensor("wsb", [128, 8 * 128], BF16))
        pbb = es.enter_context(nc.sbuf_tensor("pbb", [128, 24], F32))
        acc = es.enter_context(nc.sbuf_tensor("acc", [128, TPC], F32))
        accb = es.enter_context(nc.sbuf_tensor("accb", [128, TPC], BF16))
        psm = [es.enter_context(nc.psum_tensor(f"psm{k}", [128, 512], F32))
               for k in range(4)]
        ch = Chain(nc, es)
        with nc.Block() as blk:
            ch.seg(blk, "gpsimd", lambda g: [
                g.dma_start(out=uf[k][:], in_=u[k * 128:(k + 1) * 128, :])
                for k in range(8)] + [
                g.dma_start(out=pbb[:], in_=pb)])

            def cast(v):
                for k in range(8):
                    r = v.tensor_copy(ub[k][:], uf[k][:])
                return r
            ch.seg(blk, "vector", cast)
            for oc in range(24):
                ch.seg(blk, "gpsimd", lambda g, oc=oc: [
                    g.dma_start(out=wsb[:],
                                in_=bass.AP(pw.tensor, oc * 128,
                                            [[3 * D, 128], [128 * 3 * D, 8],
                                             [1, 128]]))])

                def mseg(t, oc=oc):
                    for ni in range(4):
                        for kc in range(8):
                            r = t.matmul(psm[ni][:],
                                         wsb[:, kc * 128:(kc + 1) * 128],
                                         ub[kc][:, ni * 512:(ni + 1) * 512],
                                         start=(kc == 0), stop=(kc == 7))
                    return r
                ch.seg(blk, "tensor", mseg)
                if oc < 8:
                    def ev(s, oc=oc):
                        for ni in range(4):
                            r = s.activation(acc[:, ni * 512:(ni + 1) * 512],
                                             psm[ni][:], AF.Sigmoid,
                                             bias=pbb[:, oc:oc + 1])
                        return r
                    ch.seg(blk, "scalar", ev)
                    ch.seg(blk, "gpsimd", lambda g, oc=oc: [
                        g.dma_start(out=o_a[oc * 128:(oc + 1) * 128, :], in_=acc[:])])
                elif oc < 16:
                    def ev(s, oc=oc):
                        for ni in range(4):
                            r = s.activation(acc[:, ni * 512:(ni + 1) * 512],
                                             psm[ni][:], AF.Tanh,
                                             bias=pbb[:, oc:oc + 1])
                        return r
                    ch.seg(blk, "scalar", ev)
                    ch.seg(blk, "vector",
                           lambda v, oc=oc: v.tensor_tensor(acc[:], acc[:],
                                                            uf[oc - 8][:],
                                                            op=ALU.mult))
                    ch.seg(blk, "gpsimd", lambda g, oc=oc: [
                        g.dma_start(out=o_w[(oc - 8) * 128:(oc - 7) * 128, :],
                                    in_=acc[:])])
                else:
                    def ev(s, oc=oc):
                        for ni in range(4):
                            r = s.activation(accb[:, ni * 512:(ni + 1) * 512],
                                             psm[ni][:], AF.Sigmoid,
                                             bias=pbb[:, oc:oc + 1])
                        return r
                    ch.seg(blk, "scalar", ev)
                    ch.seg(blk, "gpsimd", lambda g, oc=oc: [
                        g.dma_start(out=o_c[(oc - 16) * 128:(oc - 15) * 128, :],
                                    in_=accb[:])])
            ch.final_wait(blk)
    return nc


def build_L23():
    NE = EXT // 128
    nc = bass.Bass("TRN2", target_bir_lowering=False)
    xn = nc.dram_tensor("xn", [EXT, D], BF16, kind="ExternalInput").ap()
    w = nc.dram_tensor("w", [D, 2 * D], F32, kind="ExternalInput").ap()
    bi = nc.dram_tensor("bi", [128, 16], F32, kind="ExternalInput").ap()
    cw = nc.dram_tensor("cw", [128, 32], F32, kind="ExternalInput").ap()
    cb = nc.dram_tensor("cb", [128, 8], F32, kind="ExternalInput").ap()
    zf = nc.dram_tensor("zf", [128, 1], F32, kind="ExternalInput").ap()
    pw = nc.dram_tensor("pw", [D, 3 * D], F32, kind="ExternalInput").ap()
    pb = nc.dram_tensor("pb", [128, 24], F32, kind="ExternalInput").ap()
    o_u = nc.dram_tensor("o_u", [D, TPC], F32, kind="ExternalOutput").ap()
    o_g = nc.dram_tensor("o_g", [D, TPC], BF16, kind="ExternalOutput").ap()
    o_a = nc.dram_tensor("o_a", [D, TPC], F32, kind="ExternalOutput").ap()
    o_w = nc.dram_tensor("o_w", [D, TPC], F32, kind="ExternalOutput").ap()
    o_c = nc.dram_tensor("o_c", [D, TPC], BF16, kind="ExternalOutput").ap()
    NT5 = [(0, 512), (512, 512), (1024, 512), (1536, 512), (2048, 128)]
    with contextlib.ExitStack() as es:
        uf = [es.enter_context(nc.sbuf_tensor(f"uf{k}", [128, TPC], F32))
              for k in range(8)]
        psm = [es.enter_context(nc.psum_tensor(f"psm{k}", [128, 512], F32))
               for k in range(4)]
        ch = Chain(nc, es)
        with nc.Block() as blk:
            with contextlib.ExitStack() as e2:
                xT = [e2.enter_context(nc.sbuf_tensor(f"xT{k}", [128, EXT], BF16))
                      for k in range(8)]
                wsb = [e2.enter_context(nc.sbuf_tensor(f"wsb{k}", [128, 2 * D],
                                                       BF16)) for k in range(8)]
                bib = e2.enter_context(nc.sbuf_tensor("bib", [128, 16], F32))
                cwb = e2.enter_context(nc.sbuf_tensor("cwb", [128, 32], F32))
                cbb = e2.enter_context(nc.sbuf_tensor("cbb", [128, 8], F32))
                zfb = e2.enter_context(nc.sbuf_tensor("zfb", [128, 1], F32))
                uraw = e2.enter_context(nc.sbuf_tensor("uraw", [128, EXT], F32))
                ucv = e2.enter_context(nc.sbuf_tensor("ucv", [128, TPC], F32))
                gsil = e2.enter_context(nc.sbuf_tensor("gsil", [128, TPC], BF16))
                for kc in range(8):
                    ch.seg(blk, "sync", lambda sy, kc=kc: [
                        sy.dma_start_transpose(xT[kc][:],
                                               xn[:, kc * 128:(kc + 1) * 128])])
                ch.seg(blk, "gpsimd", lambda g: [
                    g.dma_start(out=bib[:], in_=bi),
                    g.dma_start(out=cwb[:], in_=cw),
                    g.dma_start(out=cbb[:], in_=cb),
                    g.dma_start(out=zfb[:], in_=zf)] + [
                    g.dma_start(out=wsb[k][:], in_=w[k * 128:(k + 1) * 128, :])
                    for k in range(8)])
                for oc in range(16):
                    for w0 in range(0, 5, 3):
                        sub = list(enumerate(NT5))[w0:w0 + 3]

                        def mseg(t, oc=oc, sub=sub):
                            for ni, (n0, nn) in sub:
                                for kc in range(8):
                                    t.matmul(psm[ni - sub[0][0]][0:128, 0:nn],
                                             wsb[kc][:, oc * 128:(oc + 1) * 128],
                                             xT[kc][:, n0:n0 + nn],
                                             start=(kc == 0), stop=(kc == 7))
                        ch.seg(blk, "tensor", mseg)

                        def eseg(v, oc=oc, sub=sub):
                            for ni, (n0, nn) in sub:
                                v.tensor_scalar(uraw[:, n0:n0 + nn],
                                                psm[ni - sub[0][0]][0:128, 0:nn],
                                                bib[:, oc:oc + 1], None,
                                                op0=ALU.add)
                        ch.seg(blk, "vector", eseg)
                    if oc < 8:
                        def conv(v, oc=oc):
                            v.tensor_scalar(uraw[:, 0:128], uraw[:, 0:128],
                                            zfb[:, 0:1], None, op0=ALU.mult)
                            v.tensor_scalar(ucv[:], uraw[:, 125:125 + TPC],
                                            cwb[:, oc * 4:oc * 4 + 1],
                                            cbb[:, oc:oc + 1], op0=ALU.mult,
                                            op1=ALU.add)
                            for k in range(1, 4):
                                v.scalar_tensor_tensor(
                                    ucv[:], uraw[:, 125 + k:125 + k + TPC],
                                    cwb[:, oc * 4 + k:oc * 4 + k + 1], ucv[:],
                                    op0=ALU.mult, op1=ALU.add)
                        ch.seg(blk, "vector", conv)
                        ch.seg(blk, "scalar",
                               lambda s, oc=oc: s.activation(uf[oc][:], ucv[:],
                                                             AF.Silu))
                        ch.seg(blk, "gpsimd", lambda g, oc=oc: [
                            g.dma_start(out=o_u[oc * 128:(oc + 1) * 128, :],
                                        in_=uf[oc][:])])
                    else:
                        ch.seg(blk, "scalar",
                               lambda s: s.activation(gsil[:], uraw[:, 128:EXT],
                                                      AF.Silu))
                        ch.seg(blk, "gpsimd", lambda g, oc=oc: [
                            g.dma_start(out=o_g[(oc - 8) * 128:(oc - 7) * 128, :],
                                        in_=gsil[:])])
            with contextlib.ExitStack() as e3:
                ub = [e3.enter_context(nc.sbuf_tensor(f"ub{k}", [128, TPC], BF16))
                      for k in range(8)]
                pwsb = e3.enter_context(nc.sbuf_tensor("pwsb", [128, 8 * 128],
                                                       BF16))
                pbb = e3.enter_context(nc.sbuf_tensor("pbb", [128, 24], F32))
                acc = e3.enter_context(nc.sbuf_tensor("acc", [128, TPC], F32))
                accb = e3.enter_context(nc.sbuf_tensor("accb", [128, TPC], BF16))
                ch.seg(blk, "gpsimd", lambda g: [g.dma_start(out=pbb[:], in_=pb)])

                def cast(v):
                    for k in range(8):
                        v.tensor_copy(ub[k][:], uf[k][:])
                ch.seg(blk, "vector", cast)
                for oc in range(24):
                    ch.seg(blk, "gpsimd", lambda g, oc=oc: [
                        g.dma_start(out=pwsb[:],
                                    in_=bass.AP(pw.tensor, oc * 128,
                                                [[3 * D, 128], [128 * 3 * D, 8],
                                                 [1, 128]]))])

                    def mseg(t, oc=oc):
                        for ni in range(4):
                            for kc in range(8):
                                t.matmul(psm[ni][:],
                                         pwsb[:, kc * 128:(kc + 1) * 128],
                                         ub[kc][:, ni * 512:(ni + 1) * 512],
                                         start=(kc == 0), stop=(kc == 7))
                    ch.seg(blk, "tensor", mseg)
                    if oc < 8:
                        def ev(s, oc=oc):
                            for ni in range(4):
                                s.activation(acc[:, ni * 512:(ni + 1) * 512],
                                             psm[ni][:], AF.Sigmoid,
                                             bias=pbb[:, oc:oc + 1])
                        ch.seg(blk, "scalar", ev)
                        ch.seg(blk, "gpsimd", lambda g, oc=oc: [
                            g.dma_start(out=o_a[oc * 128:(oc + 1) * 128, :],
                                        in_=acc[:])])
                    elif oc < 16:
                        def ev(s, oc=oc):
                            for ni in range(4):
                                s.activation(acc[:, ni * 512:(ni + 1) * 512],
                                             psm[ni][:], AF.Tanh,
                                             bias=pbb[:, oc:oc + 1])
                        ch.seg(blk, "scalar", ev)
                        ch.seg(blk, "vector",
                               lambda v, oc=oc: v.tensor_tensor(
                                   acc[:], acc[:], uf[oc - 8][:], op=ALU.mult))
                        ch.seg(blk, "gpsimd", lambda g, oc=oc: [
                            g.dma_start(out=o_w[(oc - 8) * 128:(oc - 7) * 128, :],
                                        in_=acc[:])])
                    else:
                        def ev(s, oc=oc):
                            for ni in range(4):
                                s.activation(accb[:, ni * 512:(ni + 1) * 512],
                                             psm[ni][:], AF.Sigmoid,
                                             bias=pbb[:, oc:oc + 1])
                        ch.seg(blk, "scalar", ev)
                        ch.seg(blk, "gpsimd", lambda g, oc=oc: [
                            g.dma_start(out=o_c[(oc - 16) * 128:(oc - 15) * 128, :],
                                        in_=accb[:])])
            ch.final_wait(blk)
    return nc


def build_L4():
    CH = 512
    nc = bass.Bass("TRN2", target_bir_lowering=False)
    a = nc.dram_tensor("a", [CH, T], F32, kind="ExternalInput").ap()
    w = nc.dram_tensor("w", [CH, T], F32, kind="ExternalInput").ap()
    c = nc.dram_tensor("c", [CH, T], BF16, kind="ExternalInput").ap()
    u = nc.dram_tensor("u", [CH, T], F32, kind="ExternalInput").ap()
    g_ = nc.dram_tensor("g", [CH, T], BF16, kind="ExternalInput").ap()
    skp = nc.dram_tensor("skp", [128, 4], F32, kind="ExternalInput").ap()
    o_y = nc.dram_tensor("o_y", [CH, T], BF16, kind="ExternalOutput").ap()
    with contextlib.ExitStack() as es:
        ab = es.enter_context(nc.sbuf_tensor("ab", [128, T], F32))
        wb = es.enter_context(nc.sbuf_tensor("wb", [128, T], F32))
        cb = es.enter_context(nc.sbuf_tensor("cb", [128, T], F32))
        ub = es.enter_context(nc.sbuf_tensor("ub", [128, T], F32))
        gb = es.enter_context(nc.sbuf_tensor("gb", [128, T], F32))
        sk = es.enter_context(nc.sbuf_tensor("sk", [128, 4], F32))
        st = es.enter_context(nc.sbuf_tensor("st", [128, T], F32))
        yb = es.enter_context(nc.sbuf_tensor("yb", [128, T], BF16))
        ch = Chain(nc, es)
        with nc.Block() as blk:
            ch.seg(blk, "gpsimd", lambda g: [
                g.dma_start(out=sk[:], in_=skp)])
            for grp in range(4):
                sl = slice(grp * 128, (grp + 1) * 128)
                ch.seg(blk, "gpsimd", lambda g, sl=sl: [
                    g.dma_start(out=ab[:], in_=a[sl, :]),
                    g.dma_start(out=wb[:], in_=w[sl, :]),
                    g.dma_start(out=cb[:], in_=c[sl, :]),
                    g.dma_start(out=ub[:], in_=u[sl, :]),
                    g.dma_start(out=gb[:], in_=g_[sl, :])])

                def scan(v, grp=grp):
                    v.tensor_tensor_scan(st[:], ab[:], wb[:], 0.0,
                                         op0=ALU.mult, op1=ALU.add)
                    v.tensor_tensor(st[:], st[:], cb[:], op=ALU.mult)
                    v.tensor_scalar(ub[:], ub[:], sk[:, grp:grp + 1], None,
                                    op0=ALU.mult)
                    v.tensor_tensor(st[:], st[:], ub[:], op=ALU.add)
                    return v.tensor_tensor(yb[:], st[:], gb[:], op=ALU.mult)
                ch.seg(blk, "vector", scan)
                ch.seg(blk, "gpsimd", lambda g, sl=sl: [
                    g.dma_start(out=o_y[sl, :], in_=yb[:])])
            ch.final_wait(blk)
    return nc


def build_L5():
    nc = bass.Bass("TRN2", target_bir_lowering=False)
    y = nc.dram_tensor("y", [D, TPC], BF16, kind="ExternalInput").ap()
    ow = nc.dram_tensor("ow", [D, D], F32, kind="ExternalInput").ap()
    ob = nc.dram_tensor("ob", [128, 8], F32, kind="ExternalInput").ap()
    x = nc.dram_tensor("x", [TPC, D], F32, kind="ExternalInput").ap()
    ln = nc.dram_tensor("ln", [TPC, D], F32, kind="ExternalInput").ap()
    ident = nc.dram_tensor("ident", [128, 128], BF16, kind="ExternalInput").ap()
    o_x1 = nc.dram_tensor("o_x1", [TPC, D], F32, kind="ExternalOutput").ap()
    o_ln2 = nc.dram_tensor("o_ln2", [TPC, D], BF16, kind="ExternalOutput").ap()
    with contextlib.ExitStack() as es:
        yb = [es.enter_context(nc.sbuf_tensor(f"yb{k}", [128, TPC], BF16))
              for k in range(8)]
        wsb = es.enter_context(nc.sbuf_tensor("wsb", [128, 8 * 128], BF16))
        obb = es.enter_context(nc.sbuf_tensor("obb", [128, 8], F32))
        idb = es.enter_context(nc.sbuf_tensor("idb", [128, 128], BF16))
        dT = es.enter_context(nc.sbuf_tensor("dT", [128, TPC], BF16))
        dtok = es.enter_context(nc.sbuf_tensor("dtok", [128, NTT * D], F32))
        xt = es.enter_context(nc.sbuf_tensor("xt", [128, NTT * D], F32))
        lns = es.enter_context(nc.sbuf_tensor("lns", [128, 4 * D], F32))
        m = es.enter_context(nc.sbuf_tensor("m", [128, NTT], F32))
        vv = es.enter_context(nc.sbuf_tensor("vv", [128, NTT], F32))
        sq = es.enter_context(nc.sbuf_tensor("sq", [128, D], F32))
        lnb = es.enter_context(nc.sbuf_tensor("lnb", [128, 4 * D], BF16))
        psm = [es.enter_context(nc.psum_tensor(f"psm{k}", [128, 512], F32))
               for k in range(4)]
        pst = [es.enter_context(nc.psum_tensor(f"pst{k}", [128, 128], BF16))
               for k in range(4)]
        ch = Chain(nc, es)
        with nc.Block() as blk:
            ch.seg(blk, "gpsimd", lambda g: [
                g.dma_start(out=yb[k][:], in_=y[k * 128:(k + 1) * 128, :])
                for k in range(8)] + [
                g.dma_start(out=obb[:], in_=ob),
                g.dma_start(out=idb[:], in_=ident),
                g.dma_start(out=xt[:].rearrange("p (n c) -> p n c", c=D), in_=x.rearrange("(n p) c -> p n c", p=128))])
            for r0 in range(0, NTT, 4):
                ch.seg(blk, "gpsimd", lambda g, r0=r0: [
                    g.dma_start(out=lns[:].rearrange("p (n c) -> p n c", c=D),
                                in_=ln[r0 * 128:(r0 + 4) * 128, :]
                                .rearrange("(n p) c -> p n c", p=128))])

                def addln(v, r0=r0):
                    for j in range(4):
                        v.tensor_tensor(xt[:, (r0 + j) * D:(r0 + j + 1) * D],
                                        xt[:, (r0 + j) * D:(r0 + j + 1) * D],
                                        lns[:, j * D:(j + 1) * D], op=ALU.add)
                ch.seg(blk, "vector", addln)
            for oc in range(8):
                ch.seg(blk, "gpsimd", lambda g, oc=oc: [
                    g.dma_start(out=wsb[:],
                                in_=bass.AP(ow.tensor, oc * 128,
                                            [[D, 128], [128 * D, 8], [1, 128]]))])

                def mseg(t, oc=oc):
                    for ni in range(4):
                        for kc in range(8):
                            r = t.matmul(psm[ni][:],
                                         wsb[:, kc * 128:(kc + 1) * 128],
                                         yb[kc][:, ni * 512:(ni + 1) * 512],
                                         start=(kc == 0), stop=(kc == 7))
                    return r
                ch.seg(blk, "tensor", mseg)

                def ev(v, oc=oc):
                    for ni in range(4):
                        r = v.tensor_scalar(dT[:, ni * 512:(ni + 1) * 512],
                                            psm[ni][:], obb[:, oc:oc + 1], None,
                                            op0=ALU.add)
                    return r
                ch.seg(blk, "vector", ev)
                items = [((lambda n=n: dT[:, n * 128:(n + 1) * 128]),
                          (lambda n=n, oc=oc:
                           dtok[:, n * D + oc * 128:n * D + oc * 128 + 128]))
                         for n in range(NTT)]
                _tr_wave(ch, blk, items, pst, idb)
            _ln_block(ch, blk, (xt, m, vv, sq, lnb), True, o_x1, o_ln2,
                      None, [lambda i: dtok[:, i * D:(i + 1) * D]])
            ch.final_wait(blk)
    return nc


def build_L6():
    nc = bass.Bass("TRN2", target_bir_lowering=False)
    ln2 = nc.dram_tensor("ln2", [TPC, D], BF16, kind="ExternalInput").ap()
    mk = nc.dram_tensor("mk", [4, 64, 16, 16], F32, kind="ExternalInput").ap()
    up = nc.dram_tensor("up", [4, 64, 64, 16], F32, kind="ExternalInput").ap()
    wc = nc.dram_tensor("wc", [D, 16 * D], F32, kind="ExternalInput").ap()
    msk = nc.dram_tensor("msk", [128, 128], BF16, kind="ExternalInput").ap()
    ident = nc.dram_tensor("ident", [128, 128], BF16, kind="ExternalInput").ap()
    x1 = nc.dram_tensor("x1", [TPC, D], F32, kind="ExternalInput").ap()
    o_x2 = nc.dram_tensor("o_x2", [TPC, D], F32, kind="ExternalOutput").ap()
    o_ln3 = nc.dram_tensor("o_ln3", [TPC, D], BF16, kind="ExternalOutput").ap()
    chbm = nc.dram_tensor("chbm", [4 * 8 * 8 * 16 * 64], BF16)
    with contextlib.ExitStack() as es:
        mskb = es.enter_context(nc.sbuf_tensor("mskb", [128, 128], BF16))
        idb = es.enter_context(nc.sbuf_tensor("idb", [128, 128], BF16))
        d1 = [es.enter_context(nc.sbuf_tensor(f"d1_{k}", [128, TPC], BF16))
              for k in range(8)]
        psm = [es.enter_context(nc.psum_tensor(f"psm{k}", [128, 512], F32))
               for k in range(4)]
        pstb = es.enter_context(nc.psum_tensor("pstb", [128, 512], BF16))
        pst = _psum_views(pstb, 4, 128)
        es_mid = es.enter_context(contextlib.ExitStack())
        lT = [es_mid.enter_context(nc.sbuf_tensor(f"lT{k}", [128, TPC], BF16))
              for k in range(8)]
        bd = es_mid.enter_context(nc.sbuf_tensor("bd", [128, 128 * 128], BF16))
        ch = Chain(nc, es)
        with nc.Block() as blk:
            ch.seg(blk, "gpsimd", lambda g: [
                g.dma_start(out=mskb[:], in_=msk),
                g.dma_start(out=idb[:], in_=ident)])
            for kc in range(8):
                ch.seg(blk, "sync", lambda sy, kc=kc: [
                    sy.dma_start_transpose(lT[kc][:],
                                           ln2[:, kc * 128:(kc + 1) * 128])])
            with contextlib.ExitStack() as es2:
                mkb = es2.enter_context(nc.sbuf_tensor("mkb", [128, 32 * 16], BF16))
                upb = es2.enter_context(nc.sbuf_tensor("upb", [128, 128 * 16], BF16))
                mkT = es2.enter_context(nc.sbuf_tensor("mkT", [16, 32 * 128], BF16))
                upT = es2.enter_context(nc.sbuf_tensor("upT", [16, 128 * 128], BF16))
                cst = es2.enter_context(nc.sbuf_tensor("cst", [16, 32 * 512], BF16))
                csb = es2.enter_context(nc.sbuf_tensor("csb", [128, 4 * 64], BF16))
                psc = es2.enter_context(nc.psum_tensor("psc", [16, 512], F32))
                ch.seg(blk, "gpsimd", lambda g: [
                    g.dma_start(out=mkb[:, i * 16:(i + 1) * 16],
                                in_=bass.AP(mk.tensor, i * 2048,
                                            [[16, 128], [1, 16]]))
                    for i in range(32)] + [
                    g.dma_start(out=upb[:, i * 16:(i + 1) * 16],
                                in_=bass.AP(up.tensor, i * 2048,
                                            [[16, 128], [1, 16]]))
                    for i in range(128)])
                items = []
                for i in range(32):
                    items.append((
                        (lambda i=i: mkb[:, i * 16:(i + 1) * 16]),
                        (lambda i=i: mkT[0:16, i * 128:(i + 1) * 128]),
                        True))
                for i in range(128):
                    items.append((
                        (lambda i=i: upb[:, i * 16:(i + 1) * 16]),
                        (lambda i=i: upT[0:16, i * 128:(i + 1) * 128]),
                        True))
                W = len(pst)
                for w0 in range(0, len(items), W):
                    wave = items[w0:w0 + W]

                    def tseg(t, wave=wave):
                        for j, (srcf, _, _) in enumerate(wave):
                            t.transpose(pst[j][0:16, 0:128], srcf(), idb[:])
                    ch.seg(blk, "tensor", tseg)

                    def vseg(v, wave=wave):
                        for j, (_, dstf, _) in enumerate(wave):
                            v.tensor_copy(dstf(), pst[j][0:16, 0:128])
                    ch.seg(blk, "vector", vseg)
                for hg in range(32):
                    def cm(t, hg=hg):
                        for rr in range(8):
                            r8 = hg * 8 + rr
                            t.matmul(psc[0:16, rr * 64:(rr + 1) * 64],
                                     mkT[0:16, r8 * 128 // 8 + 0:0] if False else
                                     mkT[0:16, hg * 128 + rr * 16:
                                         hg * 128 + rr * 16 + 16],
                                     upT[0:16, (hg * 8 + rr) * 128 + 0:
                                         (hg * 8 + rr) * 128 + 0] if False else
                                     upT[0:16, (hg * 8 + rr) * 128 // 2 + 0:0]
                                     if False else
                                     upT[0:16, hg * 512 + rr * 64:
                                         hg * 512 + rr * 64 + 64],
                                     start=True, stop=True)
                    ch.seg(blk, "tensor", cm)
                    ch.seg(blk, "vector",
                           lambda v, hg=hg: v.tensor_copy(
                               cst[:, hg * 512:(hg + 1) * 512], psc[0:16, :]))
                ch.seg(blk, "gpsimd", lambda g: [
                    g.dma_start(
                        out=bass.AP(chbm, hg * 8192,
                                    [[64, 16], [1024, 8], [1, 64]]),
                        in_=cst[:, hg * 512:(hg + 1) * 512]
                        .rearrange("p (a f) -> p a f", a=8))
                    for hg in range(32)])
                for hg0 in range(0, 32, 4):
                    ch.seg(blk, "gpsimd", lambda g, hg0=hg0: [
                        g.dma_start(out=csb[:, k * 64:(k + 1) * 64],
                                    in_=bass.AP(chbm, (hg0 + k) * 8192,
                                                [[64, 128], [1, 64]]))
                        for k in range(4)])

                    def bdm(v, hg0=hg0):
                        for k in range(4):
                            hg = hg0 + k
                            for cc in range(4):
                                sl = bd[:, (hg * 4 + cc) * 128:
                                        (hg * 4 + cc) * 128 + 128]
                                v.tensor_tensor(
                                    sl.rearrange("p (a f) -> p a f", a=8),
                                    bass.AP(csb, k * 64 + 16 * cc,
                                            [[256, 128], [0, 8], [1, 16]]),
                                    bass.AP(mskb, 0,
                                            [[128, 128], [16, 8], [1, 16]]),
                                    op=ALU.mult)
                    ch.seg(blk, "vector", bdm)
            with contextlib.ExitStack() as es3:
                hz = [es3.enter_context(nc.sbuf_tensor(f"hz{k}", [128, TPC], BF16))
                      for k in range(16)]
                wblk = [es3.enter_context(nc.sbuf_tensor(f"wblk{j}", [128, 2048],
                                                         BF16)) for j in range(2)]
                wT = es3.enter_context(nc.sbuf_tensor("wT", [128, 16 * 128], BF16))
                wcc = es3.enter_context(nc.sbuf_tensor("wcc", [128, 2 * 128], BF16))
                scr4 = es3.enter_context(nc.sbuf_tensor("scr4", [128, 4 * 512],
                                                        BF16))
                pc = [es3.enter_context(nc.psum_tensor(f"pc{j}", [128, 512], F32))
                      for j in range(2)]
                pt2 = es3.enter_context(nc.psum_tensor("pt2", [128, 128], BF16))
                pt = [pst[0], pt2]
                s3t = es3.enter_context(nc.semaphore("s3t"))
                s3v = es3.enter_context(nc.semaphore("s3v"))
                s3a = es3.enter_context(nc.semaphore("s3a"))
                s3d = es3.enter_context(nc.semaphore("s3d"))
                s3w = es3.enter_context(nc.semaphore("s3w"))
                entry = {k: v for k, v in ch.cnt.items() if v}

                def te_z(it2, q):
                    return it2 * 112 + q + 1

                def te_wt(it2, oc, w):
                    return it2 * 112 + 16 + oc * 12 + w + 1

                def te_cp(it2, oc, ni):
                    return it2 * 112 + 16 + oc * 12 + 8 + ni + 1

                def ve_wg(it2, oc, w):
                    return it2 * 160 + oc * 20 + 2 * w + 1

                def ve_we(it2, oc, w):
                    return it2 * 160 + oc * 20 + 2 * w + 2

                def ve_acc(it2, oc, ni):
                    return it2 * 160 + oc * 20 + 16 + ni + 1

                def a_g(it2, q):
                    return it2 * 16 + q + 1

                D0 = ch.cnt["dma"]

                def s3_gp(g):
                    for k, v in entry.items():
                        g.wait_ge(ch.sem[k], v)
                    for it2 in range(8):
                        h, gh = it2 // 2, it2 % 2
                        for oc in range(8):
                            if it2 * 8 + oc >= 2:
                                po = it2 * 8 + oc - 2
                                g.wait_ge(s3v, (po // 8) * 160
                                          + (po % 8) * 20 + 15)
                            g.dma_start(out=wblk[oc % 2][:],
                                        in_=bass.AP(wc.tensor,
                                                    oc * 128 * 16384
                                                    + h * 4096 + gh * 2048,
                                                    [[16384, 128], [1, 2048]])
                                        ).then_inc(s3d, 16)
                    g.wait_ge(s3d, 64 * 16)
                blk.gpsimd(s3_gp)

                def s3_te(t):
                    for k, v in entry.items():
                        t.wait_ge(ch.sem[k], v)
                    for it2 in range(8):
                        h, gh = it2 // 2, it2 % 2
                        for q in range(16):
                            gi, cc = q // 4, q % 4
                            g2 = gh * 4 + gi
                            if q > 0:
                                t.wait_ge(s3a, a_g(it2, q - 1))
                            elif it2 > 0:
                                t.wait_ge(s3a, a_g(it2 - 1, 15))
                            for ni in range(4):
                                r = t.matmul(
                                    psm[ni][:],
                                    bd[:, ((h * 8 + g2) * 4 + cc) * 128:
                                       ((h * 8 + g2) * 4 + cc) * 128 + 128],
                                    lT[g2][:, ni * 512:(ni + 1) * 512],
                                    start=True, stop=True)
                            r.then_inc(s3t, 1)
                        for oc in range(8):
                            for w in range(8):
                                t.wait_ge(s3v, ve_wg(it2, oc, w))
                                if w >= 1:
                                    t.wait_ge(s3v, ve_we(it2, oc, w - 1))
                                elif oc > 0:
                                    t.wait_ge(s3v, ve_we(it2, oc - 1, 7))
                                elif it2 > 0:
                                    t.wait_ge(s3v, ve_we(it2 - 1, 7, 7))
                                for j in range(2):
                                    r = t.transpose(pt[j][:],
                                                    wcc[:, j * 128:(j + 1) * 128],
                                                    idb[:])
                                r.then_inc(s3t, 1)
                            for ni in range(4):
                                if ni == 0:
                                    t.wait_ge(s3a, a_g(it2, 15))
                                    t.wait_ge(s3v, ve_we(it2, oc, 7))
                                b = ni % 2
                                if oc > 0:
                                    t.wait_ge(s3v, ve_acc(it2, oc - 1, b + 2))
                                elif it2 > 0:
                                    t.wait_ge(s3v, ve_acc(it2 - 1, 7, b + 2))
                                if ni >= 2:
                                    t.wait_ge(s3v, ve_acc(it2, oc, ni - 2))
                                for q in range(16):
                                    r = t.matmul(pc[b][:],
                                                 wT[:, q * 128:(q + 1) * 128],
                                                 hz[q][:, ni * 512:(ni + 1) * 512],
                                                 start=(q == 0), stop=(q == 15))
                                r.then_inc(s3t, 1)
                blk.tensor(s3_te)

                def s3_act(s):
                    for k, v in entry.items():
                        s.wait_ge(ch.sem[k], v)
                    for it2 in range(8):
                        for q in range(16):
                            s.wait_ge(s3t, te_z(it2, q))
                            if q == 0 and it2 > 0:
                                s.wait_ge(s3t, te_cp(it2 - 1, 7, 3))
                            for ni in range(4):
                                r = s.activation(
                                    hz[q][:, ni * 512:(ni + 1) * 512],
                                    psm[ni][:], AF.Gelu)
                            r.then_inc(s3a, 1)
                blk.scalar(s3_act)

                def s3_ve(v):
                    for k, v2 in entry.items():
                        v.wait_ge(ch.sem[k], v2)
                    nvv = 0
                    for it2 in range(8):
                        for oc in range(8):
                            for w in range(8):
                                v.wait_ge(s3d, 32 * ((it2 * 8 + oc) // 2 + 1))
                                if w > 0:
                                    v.wait_ge(s3t, te_wt(it2, oc, w - 1))
                                for j in range(2):
                                    q = w * 2 + j
                                    gi, cc = q // 4, q % 4
                                    r = v.tensor_copy(
                                        wcc[:, j * 128:(j + 1) * 128]
                                        .rearrange("p (a f) -> p a f", a=8),
                                        bass.AP(wblk[oc % 2],
                                                gi * 512 + 16 * cc,
                                                [[2048, 128], [64, 8], [1, 16]]))
                                r.then_inc(s3v, 1)
                                v.wait_ge(s3t, te_wt(it2, oc, w))
                                for j in range(2):
                                    q = w * 2 + j
                                    r = v.tensor_copy(
                                        wT[:, q * 128:(q + 1) * 128], pt[j][:])
                                r.then_inc(s3v, 1)
                            for ni in range(4):
                                v.wait_ge(s3t, te_cp(it2, oc, ni))
                                b = ni % 2
                                if it2 == 0:
                                    r = v.tensor_copy(
                                        d1[oc][:, ni * 512:(ni + 1) * 512],
                                        pc[b][:])
                                else:
                                    v.tensor_copy(
                                        scr4[:, ni * 512:(ni + 1) * 512],
                                        pc[b][:]).then_inc(s3w, 1)
                                    nvv += 1
                                    v.wait_ge(s3w, nvv)
                                    r = v.tensor_tensor(
                                        d1[oc][:, ni * 512:(ni + 1) * 512],
                                        d1[oc][:, ni * 512:(ni + 1) * 512],
                                        scr4[:, ni * 512:(ni + 1) * 512],
                                        op=ALU.add)
                                r.then_inc(s3v, 1)
                blk.vector(s3_ve)

                def bridge(v):
                    v.wait_ge(s3t, 8 * 112)
                    v.wait_ge(s3v, 8 * 160)
                    v.wait_ge(s3a, 8 * 16)
                    v.wait_ge(s3d, 64 * 16)
                    v.memset(scr4[:, 0:4], 0.0).then_inc(ch.sem["vector"], 1)
                blk.vector(bridge)
                ch.cnt["vector"] += 1
            es_mid.close()
            with contextlib.ExitStack() as es4:
                dtok = es4.enter_context(nc.sbuf_tensor("dtok",
                                                        [128, NTT * D], F32))
                xt = es4.enter_context(nc.sbuf_tensor("xt", [128, NTT * D], F32))
                m = es4.enter_context(nc.sbuf_tensor("m", [128, NTT], F32))
                vv = es4.enter_context(nc.sbuf_tensor("vv", [128, NTT], F32))
                sq = es4.enter_context(nc.sbuf_tensor("sq", [128, D], F32))
                lnb = es4.enter_context(nc.sbuf_tensor("lnb", [128, 4 * D], BF16))
                ch.seg(blk, "gpsimd", lambda g: [
                    g.dma_start(out=xt[:].rearrange("p (n c) -> p n c", c=D),
                                in_=x1.rearrange("(n p) c -> p n c", p=128))])
                for oc in range(8):
                    items = [((lambda n=n, oc=oc: d1[oc][:, n * 128:(n + 1) * 128]),
                              (lambda n=n, oc=oc:
                               dtok[:, n * D + oc * 128:n * D + oc * 128 + 128]))
                             for n in range(NTT)]
                    _tr_wave(ch, blk, items, pst, idb)
                _ln_block(ch, blk, (xt, m, vv, sq, lnb), True, o_x2, o_ln3,
                          None, [lambda i: dtok[:, i * D:(i + 1) * D]])
            ch.final_wait(blk)
    return nc


def build_L7():
    NT = T // 128
    nc = bass.Bass("TRN2", target_bir_lowering=False)
    ln3 = nc.dram_tensor("ln3", [T, D], BF16, kind="ExternalInput").ap()
    wq = nc.dram_tensor("wq", [512, D], F32, kind="ExternalInput").ap()
    wk = nc.dram_tensor("wk", [512, D], F32, kind="ExternalInput").ap()
    wv = nc.dram_tensor("wv", [512, D], F32, kind="ExternalInput").ap()
    rot = nc.dram_tensor("rot", [128, 128], BF16, kind="ExternalInput").ap()
    cosb = nc.dram_tensor("cosb", [128, T], F32, kind="ExternalInput").ap()
    sinb = nc.dram_tensor("sinb", [128, T], F32, kind="ExternalInput").ap()
    ident = nc.dram_tensor("ident", [128, 128], BF16, kind="ExternalInput").ap()
    o_qr = nc.dram_tensor("o_qr", [4, 128, T], BF16, kind="ExternalOutput").ap()
    o_kr = nc.dram_tensor("o_kr", [4, 128, T], BF16, kind="ExternalOutput").ap()
    o_v = nc.dram_tensor("o_v", [T, 512], BF16, kind="ExternalOutput").ap()
    with contextlib.ExitStack() as es:
        lT = [es.enter_context(nc.sbuf_tensor(f"lT{k}", [128, T], BF16))
              for k in range(8)]
        idb = es.enter_context(nc.sbuf_tensor("idb", [128, 128], BF16))
        rotb = es.enter_context(nc.sbuf_tensor("rotb", [128, 128], BF16))
        cosbs = es.enter_context(nc.sbuf_tensor("cosbs", [128, T], F32))
        sinbs = es.enter_context(nc.sbuf_tensor("sinbs", [128, T], F32))
        wqb = es.enter_context(nc.sbuf_tensor("wqb", [128, D], BF16))
        wqT = es.enter_context(nc.sbuf_tensor("wqT", [128, 8 * 128], BF16))
        qT = es.enter_context(nc.sbuf_tensor("qT", [128, T], BF16))
        rq = es.enter_context(nc.sbuf_tensor("rq", [128, T], BF16))
        t1 = es.enter_context(nc.sbuf_tensor("t1", [128, T], BF16))
        wvT = [es.enter_context(nc.sbuf_tensor(f"wvT{k}", [128, 512], BF16))
               for k in range(8)]
        vtk4 = es.enter_context(nc.sbuf_tensor("vtk4", [128, 4 * 512], BF16))
        psm = [es.enter_context(nc.psum_tensor(f"psm{k}", [128, 512], F32))
               for k in range(4)]
        pst = [es.enter_context(nc.psum_tensor(f"pst{k}", [128, 128], BF16))
               for k in range(2)]
        ch = Chain(nc, es)
        with nc.Block() as blk:
            for kc in range(8):
                ch.seg(blk, "sync", lambda sy, kc=kc: [
                    sy.dma_start_transpose(lT[kc][:],
                                           ln3[:, kc * 128:(kc + 1) * 128])])
            ch.seg(blk, "gpsimd", lambda g: [
                g.dma_start(out=idb[:], in_=ident),
                g.dma_start(out=rotb[:], in_=rot),
                g.dma_start(out=cosbs[:], in_=cosb),
                g.dma_start(out=sinbs[:], in_=sinb)])
            for wsrc, odst in ((wq, o_qr), (wk, o_kr)):
                for hp in range(4):
                    ch.seg(blk, "gpsimd", lambda g, wsrc=wsrc, hp=hp: [
                        g.dma_start(out=wqb[:],
                                    in_=wsrc[hp * 128:(hp + 1) * 128, :])])
                    items = [((lambda kc=kc: wqb[:, kc * 128:(kc + 1) * 128]),
                              (lambda kc=kc: wqT[:, kc * 128:(kc + 1) * 128]))
                             for kc in range(8)]
                    _tr_wave(ch, blk, items, pst, idb)
                    for n0 in range(0, 8, 4):
                        def mm(t, n0=n0):
                            for ni in range(4):
                                for kc in range(8):
                                    r = t.matmul(psm[ni][:],
                                                 wqT[:, kc * 128:(kc + 1) * 128],
                                                 lT[kc][:, (n0 + ni) * 512:
                                                         (n0 + ni + 1) * 512],
                                                 start=(kc == 0), stop=(kc == 7))
                            return r
                        ch.seg(blk, "tensor", mm)

                        def ev(v, n0=n0):
                            for ni in range(4):
                                r = v.tensor_copy(
                                    qT[:, (n0 + ni) * 512:(n0 + ni + 1) * 512],
                                    psm[ni][:])
                            return r
                        ch.seg(blk, "vector", ev)
                    for n0 in range(0, 8, 4):
                        def rm(t, n0=n0):
                            for ni in range(4):
                                r = t.matmul(psm[ni][:], rotb[:],
                                             qT[:, (n0 + ni) * 512:(n0 + ni + 1) * 512],
                                             start=True, stop=True)
                            return r
                        ch.seg(blk, "tensor", rm)

                        def re(v, n0=n0):
                            for ni in range(4):
                                r = v.tensor_copy(
                                    rq[:, (n0 + ni) * 512:(n0 + ni + 1) * 512],
                                    psm[ni][:])
                            return r
                        ch.seg(blk, "vector", re)

                    def rope(v):
                        v.tensor_tensor(t1[:], qT[:], cosbs[:], op=ALU.mult)
                        v.tensor_tensor(rq[:], rq[:], sinbs[:], op=ALU.mult)
                        return v.tensor_tensor(t1[:], t1[:], rq[:], op=ALU.add)
                    ch.seg(blk, "vector", rope)
                    ch.seg(blk, "gpsimd", lambda g, odst=odst, hp=hp: [
                        g.dma_start(out=odst[hp, :, :], in_=t1[:])])
            for ro in range(4):
                ch.seg(blk, "gpsimd", lambda g, ro=ro: [
                    g.dma_start(out=wqb[:], in_=wv[ro * 128:(ro + 1) * 128, :])])
                items = [((lambda kc=kc: wqb[:, kc * 128:(kc + 1) * 128]),
                          (lambda kc=kc, ro=ro: wvT[kc][:, ro * 128:(ro + 1) * 128]))
                         for kc in range(8)]
                _tr_wave(ch, blk, items, pst, idb)
            for tc0 in range(0, NT, 4):
                def vm(t, tc0=tc0):
                    for j in range(4):
                        for kc in range(8):
                            t.matmul(psm[j][:],
                                     lT[kc][:, (tc0 + j) * 128:(tc0 + j + 1) * 128],
                                     wvT[kc][:], start=(kc == 0), stop=(kc == 7))
                ch.seg(blk, "tensor", vm)

                def vev(v, tc0=tc0):
                    for j in range(4):
                        v.tensor_copy(vtk4[:, j * 512:(j + 1) * 512], psm[j][:])
                ch.seg(blk, "vector", vev)
                ch.seg(blk, "gpsimd", lambda g, tc0=tc0: [
                    g.dma_start(out=o_v[tc0 * 128:(tc0 + 4) * 128, :]
                                .rearrange("(a p) f -> p a f", p=128),
                                in_=vtk4[:].rearrange("p (a f) -> p a f", f=512))])
            ch.final_wait(blk)
    return nc


def build_L8():
    NT = T // 128
    nc = bass.Bass("TRN2", target_bir_lowering=False)
    qr = nc.dram_tensor("qr", [4, 128, T], BF16, kind="ExternalInput").ap()
    kr = nc.dram_tensor("kr", [4, 128, T], BF16, kind="ExternalInput").ap()
    vt = nc.dram_tensor("vt", [T, 512], BF16, kind="ExternalInput").ap()
    ident = nc.dram_tensor("ident", [128, 128], BF16, kind="ExternalInput").ap()
    o_d = nc.dram_tensor("o_d", [T, 512], BF16, kind="ExternalOutput").ap()
    HPD = 288
    QTE = NT + 8
    HXE = 8 * NT + 1
    with contextlib.ExitStack() as es:
        qs = es.enter_context(nc.sbuf_tensor("qs", [128, T], BF16))
        ks = es.enter_context(nc.sbuf_tensor("ks", [128, T], BF16))
        vp = [es.enter_context(nc.sbuf_tensor(f"vp{j}", [128, NT * 65], BF16))
              for j in range(2)]
        idb = es.enter_context(nc.sbuf_tensor("idb", [128, 128], BF16))
        pa = [[es.enter_context(nc.sbuf_tensor(f"pa{j}_{p}", [128, 512], BF16))
               for p in range(2)] for j in range(2)]
        ou = [es.enter_context(nc.sbuf_tensor(f"ou{j}", [128, 512], BF16))
              for j in range(2)]
        rz = es.enter_context(nc.sbuf_tensor("rz", [128, 1], F32))
        da = es.enter_context(nc.sbuf_tensor("da", [128, 4 * 128], BF16))
        psc = [es.enter_context(nc.psum_tensor(f"psc{j}", [128, 512], F32))
               for j in range(4)]
        psv = [es.enter_context(nc.psum_tensor(f"psv{j}", [128, 512], F32))
               for j in range(2)]
        pst = [es.enter_context(nc.psum_tensor(f"pstA", [128, 128], BF16)),
               es.enter_context(nc.psum_tensor(f"pstB", [128, 128], BF16))]
        s_ld = es.enter_context(nc.semaphore("s_ld"))
        s_st = es.enter_context(nc.semaphore("s_st"))
        s_qk = es.enter_context(nc.semaphore("s_qk"))
        s_ex = es.enter_context(nc.semaphore("s_ex"))
        s_pv = es.enter_context(nc.semaphore("s_pv"))
        s_ep = es.enter_context(nc.semaphore("s_ep"))
        s_vv = es.enter_context(nc.semaphore("s_vv"))
        with nc.Block() as blk:
            def gp(g):
                g.dma_start(out=idb[:], in_=ident).then_inc(s_ld, 16)
                nld = 16
                for hp in range(4):
                    if hp > 0:
                        g.wait_ge(s_pv, hp * 8 * QTE)
                    g.dma_start(out=qs[:], in_=qr[hp, :, :]).then_inc(s_ld, 16)
                    g.dma_start(out=ks[:], in_=kr[hp, :, :]).then_inc(s_ld, 16)
                    for j in range(2):
                        for c0 in range(4):
                            g.dma_start(
                                out=bass.AP(vp[j], c0 * 8 * 65,
                                            [[NT * 65, 128], [65, 8], [1, 64]]),
                                in_=bass.AP(vt.tensor,
                                            c0 * 8 * 128 * 512 + hp * 128 + j * 64,
                                            [[512, 128], [128 * 512, 8], [1, 64]])
                            ).then_inc(s_ld, 16)
                    nld += 160
                    for qt in range(8):
                        it = hp * 8 + qt
                        g.wait_ge(s_ep, it * 9 + 9)
                        g.dma_start(
                            out=bass.AP(o_d.tensor, qt * 512 * 512 + hp * 128,
                                        [[512, 128], [128 * 512, 4], [1, 128]]),
                            in_=da[:].rearrange("p (a f) -> p a f", a=4)
                        ).then_inc(s_st, 16)
                g.wait_ge(s_ld, nld)
                g.wait_ge(s_st, 32 * 16)
            blk.gpsimd(gp)

            def ve(v):
                nvv = 0
                for hp in range(4):
                    v.wait_ge(s_ld, 16 + (hp + 1) * 160)
                    v.memset(bass.AP(vp[0], 64, [[NT * 65, 128], [65, NT]]), 1.0)
                    v.memset(bass.AP(vp[1], 64,
                                     [[NT * 65, 128], [65, NT]]),
                             1.0).then_inc(s_ex, 1)
                    for qt in range(8):
                        it = hp * 8 + qt
                        v.wait_ge(s_pv, it * QTE + NT)
                        v.tensor_copy(ou[0][0:65, :], psv[0][0:65, :])
                        v.tensor_copy(ou[1][0:65, :],
                                      psv[1][0:65, :]).then_inc(s_ep, 1)
                        if it > 0:
                            v.wait_ge(s_st, it * 16)
                        for k in range(8):
                            qsi, j = k // 2, k % 2
                            v.wait_ge(s_pv, it * QTE + NT + k + 1)
                            if k > 0:
                                v.wait_ge(s_ep, it * 9 + 1 + k)
                            v.reciprocal(rz[:],
                                         pst[k % 2][:, 64:65]).then_inc(s_vv, 1)
                            nvv += 1
                            v.wait_ge(s_vv, nvv)
                            v.tensor_scalar(
                                da[:, qsi * 128 + j * 64:
                                   qsi * 128 + j * 64 + 64],
                                pst[k % 2][:, 0:64], rz[:],
                                None, op0=ALU.mult).then_inc(s_ep, 1)
            blk.vector(ve)

            def sc(s):
                for hp in range(4):
                    H = hp * HXE + 1
                    for qt in range(8):
                        it = hp * 8 + qt
                        for c in range(NT):
                            s.wait_ge(s_qk, it * NT + c + 1)
                            if c >= 2:
                                s.wait_ge(s_pv, it * QTE + c - 1)
                            elif it > 0:
                                s.wait_ge(s_pv, (it - 1) * QTE + NT - 1 + c)
                            s.activation(pa[0][c % 2][:], psc[2 * (c % 2)][:],
                                         AF.Exp, scale=0.125)
                            s.activation(pa[1][c % 2][:],
                                         psc[2 * (c % 2) + 1][:],
                                         AF.Exp, scale=0.125).then_inc(s_ex, 1)
            blk.scalar(sc)

            def te(t):
                for hp in range(4):
                    t.wait_ge(s_ld, 16 + (hp + 1) * 160)
                    t.wait_ge(s_ex, hp * HXE + 1)
                    H = hp * HXE + 1
                    for qt in range(8):
                        it = hp * 8 + qt
                        q0 = qt * 512

                        def qk_mm(c):
                            if c >= 2 or qt > 0:
                                t.wait_ge(s_ex, H + qt * NT + c - 1)
                            t.matmul(psc[2 * (c % 2)][:],
                                     ks[0:64, c * 128:(c + 1) * 128],
                                     qs[0:64, q0:q0 + 512],
                                     start=True, stop=True, tile_position=(0, 0))
                            t.matmul(psc[2 * (c % 2) + 1][:],
                                     ks[64:128, c * 128:(c + 1) * 128],
                                     qs[64:128, q0:q0 + 512],
                                     start=True, stop=True,
                                     tile_position=(64, 0)).then_inc(s_qk, 1)

                        def pv_mm(c):
                            t.wait_ge(s_ex, H + qt * NT + c + 1)
                            if c == 0 and it > 0:
                                t.wait_ge(s_ep, (it - 1) * 9 + 1)
                            t.matmul(psv[0][0:65, :],
                                     vp[0][:, c * 65:(c + 1) * 65],
                                     pa[0][c % 2][:],
                                     start=(c == 0), stop=(c == NT - 1))
                            t.matmul(psv[1][0:65, :],
                                     vp[1][:, c * 65:(c + 1) * 65],
                                     pa[1][c % 2][:],
                                     start=(c == 0),
                                     stop=(c == NT - 1)).then_inc(s_pv, 1)
                        qk_mm(0)
                        for c in range(1, NT):
                            qk_mm(c)
                            pv_mm(c - 1)
                        pv_mm(NT - 1)
                        t.wait_ge(s_ep, it * 9 + 1)
                        for k in range(8):
                            qsi, j = k // 2, k % 2
                            if k >= 2:
                                t.wait_ge(s_ep, it * 9 + k)
                            elif it > 0:
                                t.wait_ge(s_ep, (it - 1) * 9 + k + 8)
                            t.transpose(pst[k % 2][:, 0:65],
                                        ou[j][0:65, qsi * 128:(qsi + 1) * 128],
                                        idb[0:65, 0:65]).then_inc(s_pv, 1)
            blk.tensor(te)
    return nc


def build_L9():
    nc = bass.Bass("TRN2", target_bir_lowering=False)
    x2 = nc.dram_tensor("x2", [TPC, D], F32, kind="ExternalInput").ap()
    dat = nc.dram_tensor("dat", [TPC, D], BF16, kind="ExternalInput").ap()
    o_x3 = nc.dram_tensor("o_x3", [TPC, D], F32, kind="ExternalOutput").ap()
    o_ln4 = nc.dram_tensor("o_ln4", [TPC, D], BF16, kind="ExternalOutput").ap()
    with contextlib.ExitStack() as es:
        xt = es.enter_context(nc.sbuf_tensor("xt", [128, NTT * D], F32))
        db = es.enter_context(nc.sbuf_tensor("db", [128, NTT * D], F32))
        m = es.enter_context(nc.sbuf_tensor("m", [128, NTT], F32))
        vv = es.enter_context(nc.sbuf_tensor("vv", [128, NTT], F32))
        sq = es.enter_context(nc.sbuf_tensor("sq", [128, D], F32))
        lnb = es.enter_context(nc.sbuf_tensor("lnb", [128, 4 * D], BF16))
        ch = Chain(nc, es)
        with nc.Block() as blk:
            ch.seg(blk, "gpsimd", lambda g: [
                g.dma_start(out=xt[:].rearrange("p (n c) -> p n c", c=D), in_=x2.rearrange("(n p) c -> p n c", p=128)),
                g.dma_start(out=db[:].rearrange("p (n c) -> p n c", c=D), in_=dat.rearrange("(n p) c -> p n c", p=128))])
            _ln_block(ch, blk, (xt, m, vv, sq, lnb), True, o_x3, o_ln4,
                      None, [lambda i: db[:, i * D:(i + 1) * D]])
            ch.final_wait(blk)
    return nc


def build_L10():
    NA = 2048
    NAT = NA // 128
    nc = bass.Bass("TRN2", target_bir_lowering=False)
    ln4 = nc.dram_tensor("ln4", [NA, D], BF16, kind="ExternalInput").ap()
    w1 = nc.dram_tensor("w1", [D, 2048], F32, kind="ExternalInput").ap()
    b1 = nc.dram_tensor("b1", [128, 2048], F32, kind="ExternalInput").ap()
    w2 = nc.dram_tensor("w2", [D, D], F32, kind="ExternalInput").ap()
    b2 = nc.dram_tensor("b2", [128, 8], F32, kind="ExternalInput").ap()
    ri = nc.dram_tensor("ri", [NA, 2], I32, kind="ExternalInput").ap()
    rw = nc.dram_tensor("rw", [128, 16], F32, kind="ExternalInput").ap()
    ev_ = nc.dram_tensor("ev", [128, 1], F32, kind="ExternalInput").ap()
    ident = nc.dram_tensor("ident", [128, 128], BF16, kind="ExternalInput").ap()
    o_p = nc.dram_tensor("o_p", [NA, D], BF16, kind="ExternalOutput").ap()
    with contextlib.ExitStack() as es:
        lT = [es.enter_context(nc.sbuf_tensor(f"lT{k}", [128, NA], BF16))
              for k in range(8)]
        idb = es.enter_context(nc.sbuf_tensor("idb", [128, 128], BF16))
        w1b = [es.enter_context(nc.sbuf_tensor(f"w1b{k}", [128, 2048], BF16))
               for k in range(8)]
        b1b = es.enter_context(nc.sbuf_tensor("b1b", [128, 2048], F32))
        w2b = [es.enter_context(nc.sbuf_tensor(f"w2b{k}", [128, D], BF16))
               for k in range(8)]
        b2b = es.enter_context(nc.sbuf_tensor("b2b", [128, 8], F32))
        idxb = es.enter_context(nc.sbuf_tensor("idxb", [128, 32], I32))
        idxf = es.enter_context(nc.sbuf_tensor("idxf", [128, 32], F32))
        evb = es.enter_context(nc.sbuf_tensor("evb", [128, 1], F32))
        rwb = es.enter_context(nc.sbuf_tensor("rwb", [128, 16], F32))
        wcol = es.enter_context(nc.sbuf_tensor("wcol", [128, 16], F32))
        eq = es.enter_context(nc.sbuf_tensor("eq", [128, 32], F32))
        gc = es.enter_context(nc.sbuf_tensor("gc", [128, 1024], F32))
        sg = es.enter_context(nc.sbuf_tensor("sg", [128, 1024], F32))
        uc = es.enter_context(nc.sbuf_tensor("uc", [128, 1024], F32))
        inter = es.enter_context(nc.sbuf_tensor("inter", [128, D], BF16))
        iT = [es.enter_context(nc.sbuf_tensor(f"iT{k}", [128, NA], BF16))
              for k in range(8)]
        dTo = [es.enter_context(nc.sbuf_tensor(f"dTo{k}", [128, NA], BF16))
               for k in range(8)]
        dtok = es.enter_context(nc.sbuf_tensor("dtok", [128, D], BF16))
        psm = [es.enter_context(nc.psum_tensor(f"psm{k}", [128, 512], F32))
               for k in range(4)]
        pst = [es.enter_context(nc.psum_tensor(f"pst{k}", [128, 128], BF16))
               for k in range(2)]
        ch = Chain(nc, es)
        with nc.Block() as blk:
            for kc in range(8):
                ch.seg(blk, "sync", lambda sy, kc=kc: [
                    sy.dma_start_transpose(lT[kc][:],
                                           ln4[:, kc * 128:(kc + 1) * 128])])
            ch.seg(blk, "gpsimd", lambda g: [
                g.dma_start(out=idb[:], in_=ident),
                g.dma_start(out=b1b[:], in_=b1),
                g.dma_start(out=b2b[:], in_=b2),
                g.dma_start(out=evb[:], in_=ev_),
                g.dma_start(out=idxb[:].rearrange("p (c k) -> p c k", k=2),
                            in_=ri.rearrange("(c p) k -> p c k", p=128)),
                g.dma_start(out=rwb[:], in_=rw)] + [
                g.dma_start(out=w1b[k][:], in_=w1[k * 128:(k + 1) * 128, :])
                for k in range(8)] + [
                g.dma_start(out=w2b[k][:], in_=w2[k * 128:(k + 1) * 128, :])
                for k in range(8)])

            def wcol_f(v):
                v.tensor_copy(idxf[:], idxb[:])
                v.tensor_scalar(eq[:, 0:16],
                                bass.AP(idxf, 0, [[32, 128], [2, 16]]),
                                evb[:], None, op0=ALU.is_equal)
                v.tensor_scalar(eq[:, 16:32],
                                bass.AP(idxf, 1, [[32, 128], [2, 16]]),
                                evb[:], None, op0=ALU.is_equal)
                v.tensor_tensor(wcol[:], eq[:, 0:16], eq[:, 16:32], op=ALU.add)
                return v.tensor_tensor(wcol[:], wcol[:], rwb[:], op=ALU.mult)
            ch.seg(blk, "vector", wcol_f)
            for tc in range(NAT):
                def mm(t, tc=tc):
                    for o4 in range(4):
                        for kc in range(8):
                            t.matmul(psm[o4][:],
                                     lT[kc][:, tc * 128:(tc + 1) * 128],
                                     w1b[kc][:, o4 * 512:(o4 + 1) * 512],
                                     start=(kc == 0), stop=(kc == 7))
                ch.seg(blk, "tensor", mm)

                def swi(v):
                    for o4 in range(4):
                        v.tensor_tensor(psm[o4][:], psm[o4][:],
                                        b1b[:, o4 * 512:(o4 + 1) * 512], op=ALU.add)
                        v.tensor_scalar(gc[:, o4 * 256:(o4 + 1) * 256],
                                        bass.AP(psm[o4], 0, [[512, 128], [2, 256]]),
                                        7.0, None, op0=ALU.min)
                        v.tensor_scalar(uc[:, o4 * 256:(o4 + 1) * 256],
                                        bass.AP(psm[o4], 1, [[512, 128], [2, 256]]),
                                        7.0, -7.0, op0=ALU.min, op1=ALU.max)
                ch.seg(blk, "vector", swi)
                ch.seg(blk, "scalar",
                       lambda s: s.activation(sg[:], gc[:], AF.Sigmoid,
                                              scale=1.702))

                def glu(v):
                    v.tensor_tensor(gc[:], gc[:], sg[:], op=ALU.mult)
                    v.scalar_tensor_tensor(inter[:], uc[:], 1.0, gc[:],
                                           op0=ALU.add, op1=ALU.mult)
                ch.seg(blk, "vector", glu)
                items = [((lambda ic=ic: inter[:, ic * 128:(ic + 1) * 128]),
                          (lambda ic=ic, tc=tc: iT[ic][:, tc * 128:(tc + 1) * 128]))
                         for ic in range(8)]
                _tr_wave(ch, blk, items, pst, idb)
            for oc in range(8):
                def dm(t, oc=oc):
                    for ni in range(4):
                        for kc in range(8):
                            r = t.matmul(psm[ni][:],
                                         w2b[kc][:, oc * 128:(oc + 1) * 128],
                                         iT[kc][:, ni * 512:(ni + 1) * 512],
                                         start=(kc == 0), stop=(kc == 7))
                    return r
                ch.seg(blk, "tensor", dm)

                def ev2(v, oc=oc):
                    for ni in range(4):
                        r = v.tensor_scalar(dTo[oc][:, ni * 512:(ni + 1) * 512],
                                            psm[ni][:], b2b[:, oc:oc + 1], None,
                                            op0=ALU.add)
                    return r
                ch.seg(blk, "vector", ev2)
            for tc in range(NAT):
                items = [((lambda oc=oc, tc=tc: dTo[oc][:, tc * 128:(tc + 1) * 128]),
                          (lambda oc=oc: dtok[:, oc * 128:(oc + 1) * 128]))
                         for oc in range(8)]
                _tr_wave(ch, blk, items, pst, idb)

                def wmul(v, tc=tc):
                    return v.tensor_scalar(dtok[:], dtok[:], wcol[:, tc:tc + 1],
                                           None, op0=ALU.mult)
                ch.seg(blk, "vector", wmul)
                ch.seg(blk, "gpsimd", lambda g, tc=tc: [
                    g.dma_start(out=o_p[tc * 128:(tc + 1) * 128, :], in_=dtok[:])])
            ch.final_wait(blk)
    return nc


def build_L11():
    R = 2048 // NC
    nc = bass.Bass("TRN2", target_bir_lowering=False)
    x3 = nc.dram_tensor("x3", [R, D], F32, kind="ExternalInput").ap()
    ps = [nc.dram_tensor(f"p{e}", [R, D], BF16, kind="ExternalInput").ap()
          for e in range(8)]
    o = nc.dram_tensor("o", [R, D], F32, kind="ExternalOutput").ap()
    with contextlib.ExitStack() as es:
        xb = es.enter_context(nc.sbuf_tensor("xb", [128, (R // 128) * D], F32))
        pb = es.enter_context(nc.sbuf_tensor("pb", [128, (R // 128) * D], F32))
        ch = Chain(nc, es)
        with nc.Block() as blk:
            ch.seg(blk, "gpsimd", lambda g: [
                g.dma_start(out=xb[:].rearrange("p (n c) -> p n c", c=D), in_=x3.rearrange("(n p) c -> p n c", p=128))])
            for e in range(8):
                ch.seg(blk, "gpsimd", lambda g, e=e: [
                    g.dma_start(out=pb[:].rearrange("p (n c) -> p n c", c=D),
                                in_=ps[e].rearrange("(n p) c -> p n c", p=128))])
                ch.seg(blk, "vector",
                       lambda v: v.tensor_tensor(xb[:], xb[:], pb[:], op=ALU.add))
            ch.seg(blk, "gpsimd", lambda g: [
                g.dma_start(out=o.rearrange("(n p) c -> p n c", p=128), in_=xb[:].rearrange("p (n c) -> p n c", c=D))])
            ch.final_wait(blk)
    return nc


def _cat(outs, name, axis):
    return np.concatenate([o[name] for o in outs], axis=axis)


_BUILD_CACHE = {}


def _run(key, build_fn, in_maps):
    if key not in _BUILD_CACHE:
        _BUILD_CACHE[key] = build_fn()
    return run_bass_kernel_spmd(_BUILD_CACHE[key], in_maps,
                                core_ids=list(range(NC))).results


def _consts():
    ident = np.eye(128, dtype=np.float32).astype(NBF)
    mask = np.zeros((128, 128), np.float32)
    for b0 in range(8):
        mask[b0 * 16:(b0 + 1) * 16, b0 * 16:(b0 + 1) * 16] = 1.0
    mask = mask.astype(NBF)
    P = np.zeros((128, 128), np.float32)
    for i in range(64):
        P[2 * i, 2 * i + 1] = -1.0
        P[2 * i + 1, 2 * i] = 1.0
    rotT = P.T.copy().astype(NBF)
    pos = np.arange(T, dtype=np.float64)
    inv = np.power(10000.0, -2.0 * np.arange(32, dtype=np.float64) / 64)
    ang = pos[:, None] * inv[None, :]
    cos = np.repeat(np.cos(ang), 2, axis=1).T
    sin = np.repeat(np.sin(ang), 2, axis=1).T
    cosb = np.concatenate([cos, cos], 0).astype(np.float32)
    sinb = np.concatenate([sin, sin], 0).astype(np.float32)
    return ident, mask, rotT, cosb, sinb


def kernel(**inp):
    ident, mask, rotT, cosb, sinb = _consts()
    x = np.ascontiguousarray(inp["L0_attn_x"].reshape(B * T, D).astype(np.float32))

    nwrep = np.ascontiguousarray(
        np.broadcast_to(inp["L0_attn_norm_w"][None, :], (128, D)).astype(np.float32))
    o1 = _run("L1", build_L1, [
        {"x": x[c * TPC:(c + 1) * TPC], "nwrep": nwrep} for c in range(NC)])
    ln = _cat(o1, "o_ln", 0)
    xn = _cat(o1, "o_xn", 0)

    in_w = np.ascontiguousarray(inp["L0_attn_in_w"].astype(np.float32))
    in_b = np.ascontiguousarray(inp["L0_attn_in_b"].reshape(16, 128).T)
    conv_w = np.ascontiguousarray(
        inp["L0_attn_conv_w"].reshape(4, 8, 128).transpose(2, 1, 0).reshape(128, 32))
    conv_b = np.ascontiguousarray(inp["L0_attn_conv_b"].reshape(8, 128).T)
    zero_lb = np.zeros((128, D), NBF)
    maps = []
    for c in range(NC):
        s0 = c * TPC
        lb = zero_lb if (c % 2 == 0) else np.ascontiguousarray(xn[s0 - 128:s0])
        zfv = np.zeros((128, 1), np.float32) if c % 2 == 0 else \
            np.ones((128, 1), np.float32)
        maps.append({"xn": np.concatenate([lb, xn[s0:s0 + TPC]], 0),
                     "w": in_w, "bi": in_b, "cw": conv_w, "cb": conv_b,
                     "zf": zfv, "ident": ident})
    pw = np.ascontiguousarray(inp["L0_attn_param_w"].astype(np.float32))
    pb = np.ascontiguousarray(inp["L0_attn_param_b"].reshape(24, 128).T)
    for mp in maps:
        mp["pw"] = pw
        mp["pb"] = pb
        del mp["ident"]
    o2 = _run("L23", build_L23, maps)
    uT = _cat(o2, "o_u", 1)
    gT = _cat(o2, "o_g", 1)
    aT = _cat(o2, "o_a", 1)
    wT = _cat(o2, "o_w", 1)
    cT = _cat(o2, "o_c", 1)

    skip_r = np.ascontiguousarray(inp["L0_attn_skip"].reshape(8, 128))
    maps = []
    for c in range(NC):
        b, hh = c // 2, c % 2
        rs = slice(hh * 512, (hh + 1) * 512)
        cs = slice(b * T, (b + 1) * T)
        maps.append({"a": np.ascontiguousarray(aT[rs, cs]),
                     "w": np.ascontiguousarray(wT[rs, cs]),
                     "c": np.ascontiguousarray(cT[rs, cs]),
                     "u": np.ascontiguousarray(uT[rs, cs]),
                     "g": np.ascontiguousarray(gT[rs, cs]),
                     "skp": np.ascontiguousarray(skip_r[hh * 4:(hh + 1) * 4].T)})
    o4 = _run("L4", build_L4, maps)
    yT = np.empty((D, B * T), NBF)
    for c in range(NC):
        b, hh = c // 2, c % 2
        yT[hh * 512:(hh + 1) * 512, b * T:(b + 1) * T] = o4[c]["o_y"]

    ow = np.ascontiguousarray(inp["L0_attn_out_w"].astype(np.float32))
    ob = np.ascontiguousarray(inp["L0_attn_out_b"].reshape(8, 128).T)
    o5 = _run("L5", build_L5, [
        {"y": np.ascontiguousarray(yT[:, c * TPC:(c + 1) * TPC]),
         "ow": ow, "ob": ob, "x": x[c * TPC:(c + 1) * TPC],
         "ln": ln[c * TPC:(c + 1) * TPC], "ident": ident} for c in range(NC)])
    x1 = _cat(o5, "o_x1", 0)
    ln2 = _cat(o5, "o_ln2", 0)

    o6 = _run("L6", build_L6, [
        {"ln2": ln2[c * TPC:(c + 1) * TPC], "mk": inp["L0_ffn_rail_maker"],
         "up": inp["L0_ffn_rail_up_proj"], "wc": inp["L0_ffn_w_cproj"],
         "msk": mask, "ident": ident, "x1": x1[c * TPC:(c + 1) * TPC]}
        for c in range(NC)])
    x2 = _cat(o6, "o_x2", 0)
    ln3 = _cat(o6, "o_ln3", 0)

    maps = []
    for c in range(NC):
        b, hh = c // 2, c % 2
        maps.append({"ln3": ln3[b * T:(b + 1) * T],
                     "wq": np.ascontiguousarray(
                         inp["L1_attn_Wq"][hh * 512:(hh + 1) * 512].astype(np.float32)),
                     "wk": np.ascontiguousarray(
                         inp["L1_attn_Wk"][hh * 512:(hh + 1) * 512].astype(np.float32)),
                     "wv": np.ascontiguousarray(
                         inp["L1_attn_Wv"][hh * 512:(hh + 1) * 512].astype(np.float32)),
                     "rot": rotT, "cosb": cosb, "sinb": sinb, "ident": ident})
    o7 = _run("L7", build_L7, maps)

    o8 = _run("L8", build_L8, [
        {"qr": o7[c]["o_qr"], "kr": o7[c]["o_kr"], "vt": o7[c]["o_v"],
         "ident": ident} for c in range(NC)])
    datt = np.empty((B * T, D), NBF)
    for c in range(NC):
        b, hh = c // 2, c % 2
        datt[b * T:(b + 1) * T, hh * 512:(hh + 1) * 512] = o8[c]["o_d"]

    o9 = _run("L9", build_L9, [
        {"x2": x2[c * TPC:(c + 1) * TPC],
         "dat": np.ascontiguousarray(datt[c * TPC:(c + 1) * TPC])}
        for c in range(NC)])
    x3 = _cat(o9, "o_x3", 0)
    ln4a = o9[0]["o_ln4"]

    NA = 2048
    b1r = np.ascontiguousarray(
        np.broadcast_to(inp["L1_ffn_gate_up_proj_bias"][:, None, :],
                        (8, 128, 2048)).astype(np.float32))
    maps = []
    for e in range(NC):
        maps.append({"ln4": ln4a,
                     "w1": np.ascontiguousarray(
                         inp["L1_ffn_gate_up_proj"][e].astype(np.float32)),
                     "b1": b1r[e],
                     "w2": np.ascontiguousarray(
                         inp["L1_ffn_down_proj"][e].astype(np.float32)),
                     "b2": np.ascontiguousarray(
                         inp["L1_ffn_down_proj_bias"][e].reshape(8, 128).T),
                     "ri": inp["L1_ffn_router_indices"],
                     "rw": np.ascontiguousarray(
                         inp["L1_ffn_routing_weights"][:, e].reshape(16, 128).T),
                     "ev": np.full((128, 1), float(e), np.float32),
                     "ident": ident})
    o10 = _run("L10", build_L10, maps)

    R = NA // NC
    o11 = _run("L11", build_L11, [
        dict({"x3": x3[c * R:(c + 1) * R]},
             **{f"p{e}": np.ascontiguousarray(o10[e]["o_p"][c * R:(c + 1) * R])
                for e in range(8)}) for c in range(NC)])
    fin_active = _cat(o11, "o", 0)

    out = x3.copy()
    out[0:NA] = fin_active
    return out.reshape(B, T, D)
